# version 07
# speedup vs baseline: 1.0667x; 1.0105x over previous
import contextlib
import numpy as np
import ml_dtypes
import concourse.bass as bass
import concourse.mybir as mybir
from concourse.bass_utils import run_bass_kernel_spmd

F32 = mybir.dt.float32
BF16 = mybir.dt.bfloat16
AF = mybir.ActivationFunctionType
ALU = mybir.AluOpType
AX = mybir.AxisListType

B, T, D = 4, 4096, 1024
NCORES = 8
TL = T // 2
DC = D // 128
TT = 512
NTT = TL // TT

ENGINES = ("PE", "ACT", "DVE", "POOL", "SP")
_ENG_ATTR = {"PE": "tensor", "ACT": "scalar", "DVE": "vector", "POOL": "gpsimd", "SP": "sync"}


class _Op:
    __slots__ = ("engine", "idx", "kind", "fn", "deps", "marked", "semval", "self_wait")

    def __init__(self, engine, idx, kind, fn):
        self.engine = engine
        self.idx = idx
        self.kind = kind
        self.self_wait = None
        self.fn = fn
        self.deps = []
        self.marked = False
        self.semval = None


class Prog:
    def __init__(self, name="mk", paranoid=False, paranoid_range=None):
        self.nc = bass.Bass("TRN2", target_bir_lowering=False, debug=False)
        self.name = name
        self.paranoid = paranoid
        self.paranoid_range = paranoid_range
        self._gidx = 0
        self._prev_op = None
        self.stack = contextlib.ExitStack()
        self._scopes = [self.stack]
        self.streams = {e: [] for e in ENGINES}
        self.last_w = {}
        self.readers = {}
        self.n_sbuf_bytes = 0
        self._finalized = False
        self._cc_count = 0
        self.dram_prefix = ""
        self.dram_map = {}
        self._sec_bar = None
        self._last_per_engine = {}
        self._secbar_tile = self.stack.enter_context(
            self.nc.sbuf_tensor("_secbar", [1, 1], F32))
        self._bar_tile = self.stack.enter_context(
            self.nc.sbuf_tensor("_bar_tile", [1, 1], F32))

    def push_scope(self):
        self._scopes.append(contextlib.ExitStack())

    def pop_scope(self):
        self._scopes.pop().close()

    def section_barrier(self):
        t = self._secbar_tile
        lasts = dict(self._last_per_engine)
        def fn(eng):
            return eng.memset(t[:], 0.0)
        o = self.op("DVE", fn, reads=(), writes=("_secbar",))
        for d in lasts.values():
            if d is not o:
                o.deps.append(d)
        self._sec_bar = o

    def dram(self, name, shape, dtype, kind=None):
        if name in self.dram_map:
            return self.dram_map[name]
        name = self.dram_prefix + name
        if kind is None:
            t = self.nc.dram_tensor(name, list(shape), dtype)
        else:
            t = self.nc.dram_tensor(name, list(shape), dtype, kind=kind)
        return t

    def sbuf(self, name, shape, dtype):
        t = self._scopes[-1].enter_context(
            self.nc.sbuf_tensor(self.dram_prefix + name, list(shape), dtype))
        self.n_sbuf_bytes += shape[1] * (4 if dtype == F32 else 2)
        return t

    def psum(self, name, shape, dtype=F32):
        return self._scopes[-1].enter_context(
            self.nc.psum_tensor(self.dram_prefix + name, list(shape), dtype))

    def op(self, engine, fn, reads=(), writes=(), kind="compute"):
        o = _Op(engine, len(self.streams[engine]), kind, fn)
        deps = {}
        gi = self._gidx
        self._gidx += 1
        par = self.paranoid or (self.paranoid_range is not None and
                                self.paranoid_range[0] <= gi < self.paranoid_range[1])
        if par and self._prev_op is not None:
            d = self._prev_op
            deps[(d.engine, d.idx)] = d
        if self._sec_bar is not None:
            deps[(self._sec_bar.engine, self._sec_bar.idx)] = self._sec_bar
        for r in reads:
            w = self.last_w.get(r)
            if w is not None:
                deps[(w.engine, w.idx)] = w
        for w_ in writes:
            w = self.last_w.get(w_)
            if w is not None:
                deps[(w.engine, w.idx)] = w
            for rd in self.readers.get(w_, ()):
                deps[(rd.engine, rd.idx)] = rd
        for d in deps.values():
            if d.engine == engine:
                if engine == "PE":
                    continue
                if d.idx == o.idx - 1 and d.kind == "compute" and kind == "dma":
                    pass
            o.deps.append(d)
        self.streams[engine].append(o)
        for r in reads:
            self.readers.setdefault(r, []).append(o)
        for w_ in writes:
            self.last_w[w_] = o
            self.readers[w_] = []
        self._prev_op = o
        self._last_per_engine[engine] = o
        return o

    def dma(self, queue, out, in_, reads=(), writes=(), **kw):
        def fn(eng):
            return eng.dma_start(out=out, in_=in_, **kw)
        return self.op(queue, fn, reads=reads, writes=writes, kind="dma")

    def begin_section(self, prefix):
        self.section_barrier()
        self.push_scope()
        self.dram_prefix = prefix
        for attr in ("_moe", "_lnxb", "_lnsq", "_lnrows", "_mk0", "_nb_scr", "_cols_sec"):
            if hasattr(self, attr):
                delattr(self, attr)

    def end_section(self):
        self.pop_scope()
        self.dram_prefix = ""
        self.dram_map = {}

    def barrier(self, keys):
        t = self._bar_tile
        def fn(eng):
            return eng.memset(t[:], 0.0)
        return self.op("DVE", fn, reads=(), writes=tuple(keys) + ("_bar_tile",))

    def collective(self, kind_s, alu, groups, ins, outs, reads=(), writes=()):
        def fn(eng):
            return eng.collective_compute(kind_s, alu, replica_groups=groups, ins=ins, outs=outs)
        return self.op("POOL", fn, reads=reads, writes=writes, kind="cc")

    def finalize(self):
        assert not self._finalized
        self._finalized = True
        nc = self.nc
        for e in ENGINES:
            for o in self.streams[e]:
                for d in o.deps:
                    d.marked = True
        NS = 6
        for e in ENGINES:
            cnt = 0
            dmacnt = 0
            uses = {}
            for o in self.streams[e]:
                if o.kind == "dma":
                    sn = "D%s%d" % (e, dmacnt % NS)
                    dmacnt += 1
                    u = uses.get(sn, 0) + 1
                    uses[sn] = u
                    o.semval = (sn, u * 16)
                    o.self_wait = (sn, (u - 1) * 16) if u > 1 else None
                    o.marked = True
                elif o.kind == "cc":
                    sn = "CC%d" % (self._cc_count % 2)
                    self._cc_count += 1
                    u = uses.get(sn, 0) + 1
                    uses[sn] = u
                    o.semval = (sn, u)
                    o.self_wait = (sn, u - 1) if u > 1 else None
                    o.marked = True
                elif o.marked:
                    cnt += 1
                    o.semval = ("E" + e, cnt)
        sems = {}
        semnames = set()
        for e in ENGINES:
            for o in self.streams[e]:
                if o.semval:
                    semnames.add(o.semval[0])
        for sn in sorted(semnames):
            sems[sn] = self.stack.enter_context(nc.semaphore(sn))
        self.sems = sems

        stream_ops = {e: list(self.streams[e]) for e in ENGINES}
        waits_for = {e: [] for e in ENGINES}
        for e in ENGINES:
            clock = {}
            for o in stream_ops[e]:
                ws = {}
                if getattr(o, "self_wait", None) is not None:
                    sn, val = o.self_wait
                    ws[sn] = val
                for d in o.deps:
                    s = self.streams[d.engine]
                    j = d.idx
                    while j < len(s) and s[j].semval is None:
                        j += 1
                    assert j < len(s), f"dep of {e}#{o.idx} on unmarked tail {d.engine}#{d.idx}"
                    sn, val = s[j].semval
                    if ws.get(sn, -1) < val:
                        ws[sn] = val
                waits = []
                for sn, val in ws.items():
                    if clock.get(sn, -1) < val:
                        waits.append((sn, val))
                        clock[sn] = val
                waits_for[e].append(waits)

        block = self.stack.enter_context(nc.Block())
        prog = self

        def make_body(ename):
            def body(eng):
                for o, waits in zip(stream_ops[ename], waits_for[ename]):
                    for sn, val in waits:
                        eng.wait_ge(prog.sems[sn], val)
                    inst = o.fn(eng)
                    if o.semval is not None:
                        sn, val = o.semval
                        if o.kind == "dma":
                            inst.then_inc(prog.sems[sn], 16)
                        elif o.kind == "cc":
                            inst.then_inc(prog.sems[sn])
                        else:
                            inst.then_inc(prog.sems[sn], 1)
            return body

        for ename in ENGINES:
            if not stream_ops[ename]:
                continue
            getattr(block, _ENG_ATTR[ename])(make_body(ename))

    def close(self):
        self.stack.close()

    def final_wait(self, regions):
        def fn(eng):
            return eng.nop()
        self.op("SP", fn, reads=tuple(regions), kind="compute")


def run_prog(prog, in_maps, **kw):
    prog.finalize()
    res = run_bass_kernel_spmd(prog.nc, in_maps, core_ids=list(range(NCORES)), **kw)
    prog.close()
    return res



def load_wchunks(p, queue, dst, dkey, wt_ap, row_stride, d0, nchunk, f0, fw):
    in_ap = bass.AP(tensor=wt_ap.tensor, offset=wt_ap.offset + d0 * row_stride + f0,
                    ap=[[row_stride, 128], [128 * row_stride, nchunk], [1, fw]])
    p.dma(queue, dst, in_ap, writes=(dkey,))


MAGIC_RNE = 12582912.0


class Consts:
    def __init__(self, p: Prog):
        self.ones_col_f = p.sbuf("c_ones_col_f", [128, 1], F32)
        self.ones_row_f = p.sbuf("c_ones_row_f", [1, 128], F32)
        p.op("DVE", lambda e: e.memset(self.ones_col_f[:], 1.0), writes=("c_ones_col_f",))
        p.op("DVE", lambda e: e.memset(self.ones_row_f[:], 1.0), writes=("c_ones_row_f",))
        self._p = p
        self._cols = {}
        self.ps_s = p.psum("ps_s", [1, TT], F32)
        self.ps_s2 = p.psum("ps_s2", [1, TT], F32)
        self.ps_b = p.psum("ps_b", [128, TT], F32)
        self.ps_b2 = p.psum("ps_b2", [128, TT], F32)

    def col(self, value, parts=128):
        key = (float(value), parts)
        if key not in self._cols:
            p = self._p
            name = f"c_col{len(self._cols)}"
            t = p.sbuf(name, [parts, 1], F32)
            p.op("DVE", (lambda e, t=t, value=value: e.memset(t[:], float(value))), writes=(name,))
            self._cols[key] = (t, name)
        return self._cols[key]


def emit_ln_tile(p, c, x_src, ln_sb, tt, pref, kind="ln", eps=1e-5, gain=None, beta=None,
                 out_off=0, n_tok_tile=TT):
    u = f"{pref}_t{tt}"
    mode, x = x_src
    tsl = slice(tt * n_tok_tile, (tt + 1) * n_tok_tile)
    if mode == "hbm":
        xb = [p.sbuf(f"{u}_xb{k}", [128, n_tok_tile], F32) for k in range(3)] \
            if not hasattr(p, "_lnxb") else p._lnxb
        p._lnxb = xb
        xkeys = [f"lnxb{k}" for k in range(3)]
    sq4 = getattr(p, "_lnsq", None)
    if sq4 is None:
        sq4 = [p.sbuf(f"lnsq{j}", [128, n_tok_tile], F32) for j in range(3)]
        p._lnsq = sq4
        p._lnrows = [p.sbuf(f"lnrow{j}", [1, n_tok_tile], F32) for j in range(3)]
    mrow, qrow, rrow = p._lnrows
    ps_s, ps_b, ps_b2 = c.ps_s, c.ps_b, c.ps_b2
    ps_s2 = c.ps_s2

    def get_x(i, k):
        if mode == "hbm":
            p.dma("SP", xb[k][:], x[i * 128:(i + 1) * 128, tsl], writes=(xkeys[k],))
            return xb[k][:], xkeys[k]
        t, key = x[i]
        return t[:, tsl], key

    for i in range(DC):
        xv, xkey = get_x(i, i % 3)
        sq = sq4[i % 3]
        if kind == "ln":
            p.op("PE", (lambda e, xv=xv, i=i: e.matmul(ps_s[:], c.ones_col_f[:], xv,
                                                       start=(i == 0), stop=(i == DC - 1))),
                 reads=("c_ones_col_f", xkey), writes=("ps_s",))
        p.op("ACT", (lambda e, xv=xv, sq=sq: e.activation(sq[:], xv, AF.Square)),
             reads=(xkey,), writes=(f"lnsq{i % 3}",))
        p.op("PE", (lambda e, i=i, sq=sq: e.matmul(ps_s2[:], c.ones_col_f[:], sq[:],
                                            start=(i == 0), stop=(i == DC - 1))),
             reads=("c_ones_col_f", f"lnsq{i % 3}"), writes=("ps_s2",))
    if kind == "ln":
        p.op("DVE", lambda e: e.tensor_scalar_mul(mrow[:], ps_s[:], 1.0 / D),
             reads=("ps_s",), writes=("lnmrow",))
        p.op("DVE", lambda e: e.tensor_scalar_mul(qrow[:], ps_s2[:], 1.0 / D),
             reads=("ps_s2",), writes=("lnqrow",))
        p.op("DVE", lambda e: e.tensor_mul(rrow[:], mrow[:], mrow[:]),
             reads=("lnmrow",), writes=("lnrrow",))
        p.op("DVE", lambda e: e.tensor_sub(qrow[:], qrow[:], rrow[:]),
             reads=("lnqrow", "lnrrow"), writes=("lnqrow",))
    else:
        p.op("DVE", lambda e: e.tensor_scalar_mul(qrow[:], ps_s2[:], 1.0 / D),
             reads=("ps_s2",), writes=("lnqrow",))
    epst, epsk = c.col(eps, 1)
    p.op("ACT", lambda e: e.activation(rrow[:], qrow[:], AF.Sqrt, bias=epst[:]),
         reads=("lnqrow", epsk), writes=("lnrrow",))
    p.op("DVE", lambda e: e.reciprocal(rrow[:], rrow[:]), reads=("lnrrow",), writes=("lnrrow",))
    if kind == "ln":
        p.op("PE", lambda e: e.matmul(ps_b[:], c.ones_row_f[:], mrow[:], start=True, stop=True),
             reads=("c_ones_row_f", "lnmrow"), writes=("ps_b",))
    p.op("PE", lambda e: e.matmul(ps_b2[:], c.ones_row_f[:], rrow[:], start=True, stop=True),
         reads=("c_ones_row_f", "lnrrow"), writes=("ps_b2",))
    osl = slice(out_off, out_off + n_tok_tile)
    for i in range(DC):
        xv, xkey = get_x(i, i % 3)
        sq = sq4[i % 3]
        sqk = f"lnsq{i % 3}"
        lt, lkey = ln_sb[i]
        if kind == "ln":
            p.op("DVE", (lambda e, xv=xv, sq=sq: e.tensor_sub(sq[:], xv, ps_b[:])),
                 reads=(xkey, "ps_b"), writes=(sqk,))
            if gain is not None:
                gt, gkey, gc0 = gain
                bt, bkey, bc0 = beta
                p.op("DVE", (lambda e, sq=sq: e.tensor_mul(sq[:], sq[:], ps_b2[:])),
                     reads=(sqk, "ps_b2"), writes=(sqk,))
                p.op("DVE", (lambda e, lt=lt, osl=osl, i=i, sq=sq: e.tensor_scalar(
                        lt[:, osl], sq[:], gt[:, gc0 + i:gc0 + i + 1], bt[:, bc0 + i:bc0 + i + 1],
                        ALU.mult, ALU.add)),
                     reads=(sqk, gkey, bkey), writes=(lkey,))
            else:
                p.op("DVE", (lambda e, lt=lt, osl=osl, sq=sq: e.tensor_mul(lt[:, osl], sq[:], ps_b2[:])),
                     reads=(sqk, "ps_b2"), writes=(lkey,))
        else:
            if gain is not None:
                gt, gkey, gc0 = gain
                p.op("DVE", (lambda e, xv=xv, sq=sq: e.tensor_mul(sq[:], xv, ps_b2[:])),
                     reads=(xkey, "ps_b2"), writes=(sqk,))
                p.op("DVE", (lambda e, lt=lt, osl=osl, i=i, gt=gt, gc0=gc0, sq=sq:
                             e.tensor_scalar_mul(lt[:, osl], sq[:], gt[:, gc0 + i:gc0 + i + 1])),
                     reads=(sqk, gkey), writes=(lkey,))
            else:
                p.op("DVE", (lambda e, xv=xv, lt=lt, osl=osl: e.tensor_mul(lt[:, osl], xv, ps_b2[:])),
                     reads=(xkey, "ps_b2"), writes=(lkey,))


def emit_l1ffn_quant(p, c, wdram, wq_sb, nf, pref):
    acc = p.sbuf(f"{pref}_acc", [128, 1], F32)
    red = p.sbuf(f"{pref}_red", [128, 1], F32)
    wt = [p.sbuf(f"{pref}_wt{k}", [128, TT], F32) for k in range(2)]
    p.op("DVE", lambda e: e.memset(acc[:], 0.0), writes=(f"{pref}_acc",))
    nfb = nf // TT
    for i in range(DC):
        for fb in range(nfb):
            k = (i * nfb + fb) % 2
            p.dma("SP", wt[k][:], wdram[i * 128:(i + 1) * 128, fb * TT:(fb + 1) * TT],
                  writes=(f"{pref}_wt{k}",))
            p.op("DVE", (lambda e, k=k: e.tensor_reduce(red[:], wt[k][:], AX.X, ALU.add,
                                                        apply_absolute_value=True)),
                 reads=(f"{pref}_wt{k}",), writes=(f"{pref}_red",))
            p.op("DVE", lambda e: e.tensor_add(acc[:], acc[:], red[:]),
                 reads=(f"{pref}_red", f"{pref}_acc"), writes=(f"{pref}_acc",))
    return acc, wt


def emit_l1ffn_quant_pass2(p, c, wdram, wq_sb, nf, pref, recip_b, wt):
    tmp = p.sbuf(f"{pref}_qtmp", [128, TT], F32)
    onet = p.sbuf(f"{pref}_ones", [128, TT], F32)
    p._q1_qtmp, p._q1_ones = tmp, onet
    p.op("DVE", lambda e: e.memset(onet[:], 1.0), writes=(f"{pref}_ones",))
    nfb = nf // TT
    for i in range(DC):
        for fb in range(nfb):
            k = (i * nfb + fb) % 2
            fsl = slice(fb * TT, (fb + 1) * TT)
            p.dma("SP", wt[k][:], wdram[i * 128:(i + 1) * 128, fsl], writes=(f"{pref}_wt{k}",))
            p.op("DVE", (lambda e, k=k: e.tensor_scalar(tmp[:], wt[k][:], recip_b[0][:],
                                                        MAGIC_RNE, ALU.mult, ALU.add)),
                 reads=(f"{pref}_wt{k}", recip_b[1]), writes=("q1_qtmp",))
            p.op("DVE", (lambda e: e.scalar_tensor_tensor(tmp[:], tmp[:], MAGIC_RNE, onet[:],
                                                          ALU.subtract, ALU.min)),
                 reads=("q1_qtmp", "q1_ones"), writes=("q1_qtmp",))
            p.op("DVE", (lambda e, i=i, fsl=fsl: e.tensor_scalar_max(wq_sb[i][:, fsl], tmp[:], -1.0)),
                 reads=("q1_qtmp",), writes=(f"{pref}_wq{i}",))


def build_l1ffn_launch(p=None):
    p = p or Prog("l1ffn")
    c = Consts(p)
    NF = 4096
    x_in = p.dram("x_in", [D, TL], F32, kind="ExternalInput").ap()
    w1t = p.dram("w1t", [D, NF], F32, kind="ExternalInput").ap()
    w2t = p.dram("w2t", [NF, D], F32, kind="ExternalInput").ap()
    b1d = p.dram("b1d", [128, NF // 128], F32, kind="ExternalInput").ap()
    b2d = p.dram("b2d", [128, DC], F32, kind="ExternalInput").ap()
    x_out = p.dram("x_out", [D, TL], F32, kind="ExternalOutput").ap()

    wq1 = [p.sbuf(f"q1_wq{i}", [128, NF], BF16) for i in range(DC)]
    wq2 = [p.sbuf(f"q2_wq{i}", [128, D], BF16) for i in range(NF // 128)]

    acc1, wt1 = emit_l1ffn_quant(p, c, w1t, wq1, NF, "q1")
    acc2 = p.sbuf("q2_acc", [128, 1], F32)
    red2 = p.sbuf("q2_red", [128, 1], F32)
    wt2 = [p.sbuf(f"q2_wt{k}", [128, TT], F32) for k in range(2)]
    p.op("DVE", lambda e: e.memset(acc2[:], 0.0), writes=("q2_acc",))
    for i in range(NF // 128):
        for fb in range(D // TT):
            k = (i * (D // TT) + fb) % 2
            p.dma("SP", wt2[k][:], w2t[i * 128:(i + 1) * 128, fb * TT:(fb + 1) * TT],
                  writes=(f"q2_wt{k}",))
            p.op("DVE", (lambda e, k=k: e.tensor_reduce(red2[:], wt2[k][:], AX.X, ALU.add,
                                                        apply_absolute_value=True)),
                 reads=(f"q2_wt{k}",), writes=("q2_red",))
            p.op("DVE", lambda e: e.tensor_add(acc2[:], acc2[:], red2[:]),
                 reads=("q2_red", "q2_acc"), writes=("q2_acc",))

    ps_g = c.ps_s
    grow = p.sbuf("grow", [1, 2], F32)
    gcol = p.sbuf("gcol", [128, 2], F32)
    ps_gb = c.ps_b
    accb = p.sbuf("accb", [128, 2], F32)
    p.op("DVE", lambda e: e.tensor_copy(accb[:, 0:1], acc1[:]), reads=("q1_acc",), writes=("accb",))
    p.op("DVE", lambda e: e.tensor_copy(accb[:, 1:2], acc2[:]), reads=("q2_acc",), writes=("accb",))
    p.op("PE", lambda e: e.matmul(ps_g[0:1, 0:2], c.ones_col_f[:], accb[:], start=True, stop=True),
         reads=("c_ones_col_f", "accb"), writes=("ps_s",))
    p.op("DVE", lambda e: e.tensor_scalar(grow[:], ps_g[0:1, 0:2], 1.0 / (NF * D), 1e-5, ALU.mult, ALU.add),
         reads=("ps_s",), writes=("grow",))
    p.op("DVE", lambda e: e.reciprocal(grow[:], grow[:]), reads=("grow",), writes=("grow",))
    p.op("PE", lambda e: e.matmul(ps_gb[0:128, 0:2], c.ones_row_f[:], grow[:], start=True, stop=True),
         reads=("c_ones_row_f", "grow"), writes=("ps_b",))
    p.op("DVE", lambda e: e.tensor_copy(gcol[:], ps_gb[0:128, 0:2]), reads=("ps_b",), writes=("gcol",))

    emit_l1ffn_quant_pass2(p, c, w1t, wq1, NF, "q1", (gcol[:, 0:1], "gcol"), wt1)
    tmp2 = p._q1_qtmp
    ones2 = p._q1_ones
    for i in range(NF // 128):
        for fb in range(D // TT):
            k = (i * (D // TT) + fb) % 2
            fsl = slice(fb * TT, (fb + 1) * TT)
            p.dma("SP", wt2[k][:], w2t[i * 128:(i + 1) * 128, fsl], writes=(f"q2_wt{k}",))
            p.op("DVE", (lambda e, k=k: e.tensor_scalar(tmp2[:], wt2[k][:], gcol[:, 1:2],
                                                        MAGIC_RNE, ALU.mult, ALU.add)),
                 reads=(f"q2_wt{k}", "gcol"), writes=("q1_qtmp",))
            p.op("DVE", (lambda e: e.scalar_tensor_tensor(tmp2[:], tmp2[:], MAGIC_RNE, ones2[:],
                                                          ALU.subtract, ALU.min)),
                 reads=("q1_qtmp", "q1_ones"), writes=("q1_qtmp",))
            p.op("DVE", (lambda e, i=i, fsl=fsl: e.tensor_scalar_max(wq2[i][:, fsl], tmp2[:], -1.0)),
                 reads=("q1_qtmp",), writes=(f"q2_wq{i}",))

    b1_sb = p.sbuf("b1_sb", [128, NF // 128], F32)
    b2_sb = p.sbuf("b2_sb", [128, DC], F32)
    p.dma("SP", b1_sb[:], b1d, writes=("b1_sb",))
    p.dma("SP", b2_sb[:], b2d, writes=("b2_sb",))

    ln = [p.sbuf(f"ln_ln{i}", [128, TT], BF16) for i in range(DC)]

    hbuf = [p.sbuf(f"h_sb{i}", [128, TT], BF16) for i in range(NF // 128)]
    ps_h = [p.psum(f"ps_h{k}", [128, TT], F32) for k in range(2)]
    ps_o = [p.psum(f"ps_o{k}", [128, TT], F32) for k in range(2)]
    xre = [p.sbuf(f"xre{k}", [128, TT], F32) for k in range(2)]
    ot = [p.sbuf(f"ot{k}", [128, TT], F32) for k in range(2)]
    for tt in range(NTT):
        tsl = slice(tt * TT, (tt + 1) * TT)
        emit_ln_tile(p, c, ("hbm", x_in), [(ln[i], f"ln_ln{i}") for i in range(DC)], tt, "ln")
        for fi in range(NF // 128):
            k = fi % 2
            for di in range(DC):
                p.op("PE", (lambda e, fi=fi, di=di, tsl=tsl, k=k:
                            e.matmul(ps_h[k][:], wq1[di][:, fi * 128:(fi + 1) * 128],
                                     ln[di][:], start=(di == 0), stop=(di == DC - 1))),
                     reads=(f"q1_wq{di}", f"ln_ln{di}"), writes=(f"ps_h{k}",))
            p.op("ACT", (lambda e, fi=fi, k=k: e.activation(hbuf[fi][:], ps_h[k][:], AF.Relu,
                                                            bias=b1_sb[:, fi:fi + 1])),
                 reads=(f"ps_h{k}", "b1_sb"), writes=(f"h_sb{fi}",))
        for oi in range(DC):
            k = oi % 2
            for fi in range(NF // 128):
                p.op("PE", (lambda e, oi=oi, fi=fi, k=k:
                            e.matmul(ps_o[k][:], wq2[fi][:, oi * 128:(oi + 1) * 128],
                                     hbuf[fi][:], start=(fi == 0), stop=(fi == NF // 128 - 1))),
                     reads=(f"q2_wq{fi}", f"h_sb{fi}"), writes=(f"ps_o{k}",))
            p.dma("SP", xre[k][:], x_in[oi * 128:(oi + 1) * 128, tsl], writes=(f"xre{k}",))
            p.op("DVE", (lambda e, oi=oi, k=k: e.tensor_scalar_add(ot[k][:], ps_o[k][:],
                                                                   b2_sb[:, oi:oi + 1])),
                 reads=(f"ps_o{k}", "b2_sb"), writes=(f"ot{k}",))
            p.op("DVE", (lambda e, k=k: e.scalar_tensor_tensor(ot[k][:], ot[k][:], 0.5, xre[k][:],
                                                               ALU.mult, ALU.add)),
                 reads=(f"ot{k}", f"xre{k}"), writes=(f"ot{k}",))
            p.dma("SP", x_out[oi * 128:(oi + 1) * 128, tsl], ot[k][:],
                  reads=(f"ot{k}",), writes=(f"xo_{oi}_{tt}",))
    p.final_wait([f"xo_{oi}_{tt}" for oi in range(DC) for tt in range(NTT)])
    return p


def emit_moe_prep(p, c, wg_dram, bg_dram):
    wgf = p.sbuf("wgf_sb", [128, DC * 8], F32)
    for i in range(DC):
        p.dma("SP", wgf[:, i * 8:(i + 1) * 8], wg_dram[i * 128:(i + 1) * 128, :], writes=("wgf_sb",))
    colsum_b = p.sbuf("colsum_b", [128, 8], F32)
    crow = p.sbuf("crow", [1, 8], F32)
    for i in range(DC):
        p.op("PE", (lambda e, i=i: e.matmul(c.ps_b[0:1, 0:8], c.ones_col_f[:],
                                            wgf[:, i * 8:(i + 1) * 8],
                                            start=(i == 0), stop=(i == DC - 1))),
             reads=("c_ones_col_f", "wgf_sb"), writes=("ps_b",))
    p.op("DVE", lambda e: e.tensor_copy(crow[:], c.ps_b[0:1, 0:8]), reads=("ps_b",), writes=("crow",))
    p.op("PE", lambda e: e.matmul(c.ps_b[0:128, 0:8], c.ones_row_f[:], crow[:], start=True, stop=True),
         reads=("c_ones_row_f", "crow"), writes=("ps_b",))
    p.op("DVE", lambda e: e.tensor_copy(colsum_b[:], c.ps_b[0:128, 0:8]),
         reads=("ps_b",), writes=("colsum_b",))
    bg_b = None
    if bg_dram is not None:
        bg_row = p.sbuf("bg_row", [1, 8], F32)
        bg_b = p.sbuf("bg_b", [128, 8], F32)
        p.dma("SP", bg_row[:], bg_dram, writes=("bg_row",))
        p.op("PE", lambda e: e.matmul(c.ps_b[0:128, 0:8], c.ones_row_f[:], bg_row[:],
                                      start=True, stop=True),
             reads=("c_ones_row_f", "bg_row"), writes=("ps_b",))
        p.op("DVE", lambda e: e.tensor_copy(bg_b[:], c.ps_b[0:128, 0:8]),
             reads=("ps_b",), writes=("bg_b",))
    return wgf, colsum_b, bg_b


def emit_moe_gates(p, c, x_src, wgf, colsum_b, bg_b, tt, renorm, gfm_sb, gfm_bf=None):
    g = getattr(p, "_moe", None)
    if g is None:
        g = {}
        g["xb"] = [p.sbuf(f"moe_xb{k}", [128, TT], F32) for k in range(2)]
        g["gt"] = p.sbuf("moe_gt", [128, 8], F32)
        g["m1"] = p.sbuf("moe_m1", [128, 1], F32)
        g["m2"] = p.sbuf("moe_m2", [128, 1], F32)
        g["nm"] = p.sbuf("moe_nm", [128, 1], F32)
        g["sm"] = p.sbuf("moe_sm", [128, 1], F32)
        g["pr"] = p.sbuf("moe_pr", [128, 8], F32)
        g["msk"] = p.sbuf("moe_msk", [128, 8], F32)
        g["mr"] = p.sbuf("moe_mr", [128, 2], F32)
        g["nrow"] = p.sbuf("moe_nrow", [1, TT], F32)
        g["ident"] = p.sbuf("moe_ident", [128, 128], F32)
        p.op("POOL", lambda e: e.memset(g["ident"][:], 1.0), writes=("moe_ident",))
        p.op("POOL", lambda e: e.affine_select(out=g["ident"][:], in_=g["ident"][:],
                                               pattern=[[-1, 128]], compare_op=ALU.is_equal,
                                               fill=0.0, base=0, channel_multiplier=1),
             reads=("moe_ident",), writes=("moe_ident",))
        p._moe = g
    gt, m1, m2, nm, sm, pr, msk = g["gt"], g["m1"], g["m2"], g["nm"], g["sm"], g["pr"], g["msk"]
    mrow, qrow, rrow = p._lnrows
    nrow, mr, ps_t = g["nrow"], g["mr"], c.ps_b2
    gf, gfk = gfm_sb
    mode, x = x_src
    assert mode == "hbm"
    p.op("DVE", lambda e: e.tensor_scalar_mul(nrow[:], mrow[:], -1.0),
         reads=("lnmrow",), writes=("moe_nrow",))
    for i in range(DC):
        k = i % 2
        p.dma("SP", g["xb"][k][:], x[i * 128:(i + 1) * 128, tt * TT:(tt + 1) * TT],
              writes=(f"moe_xb{k}",))
        for j in range(TT // 128):
            p.op("PE", (lambda e, i=i, j=j, k=k:
                        e.matmul(ps_t[0:128, j * 8:j * 8 + 8],
                                 g["xb"][k][:, j * 128:(j + 1) * 128],
                                 wgf[:, i * 8:(i + 1) * 8],
                                 start=(i == 0 and j == 0), stop=(i == DC - 1))),
                 reads=(f"moe_xb{k}", "wgf_sb"), writes=("ps_b2",))
    for j in range(TT // 128):
        t0 = tt * TT + j * 128
        p.op("PE", (lambda e, j=j: e.transpose(ps_t[0:128, 34:35], nrow[0:1, j * 128:(j + 1) * 128],
                                               g["ident"][0:1, 0:1])),
             reads=("moe_nrow", "moe_ident"), writes=("ps_b2",))
        p.op("PE", (lambda e, j=j: e.transpose(ps_t[0:128, 35:36], rrow[0:1, j * 128:(j + 1) * 128],
                                               g["ident"][0:1, 0:1])),
             reads=("lnrrow", "moe_ident"), writes=("ps_b2",))
        p.op("DVE", lambda e: e.tensor_copy(mr[:], ps_t[0:128, 34:36]),
             reads=("ps_b2",), writes=("moe_mr",))
        p.op("DVE", (lambda e, j=j: e.scalar_tensor_tensor(gt[:], colsum_b[:], mr[:, 0:1],
                                                           ps_t[0:128, j * 8:j * 8 + 8],
                                                           ALU.mult, ALU.add)),
             reads=("colsum_b", "moe_mr", "ps_b2"), writes=("moe_gt",))
        if bg_b is not None:
            p.op("DVE", (lambda e: e.scalar_tensor_tensor(gt[:], gt[:], mr[:, 1:2], bg_b[0][:],
                                                          ALU.mult, ALU.add)),
                 reads=("moe_gt", "moe_mr", bg_b[1]), writes=("moe_gt",))
        else:
            p.op("DVE", (lambda e: e.tensor_scalar_mul(gt[:], gt[:], mr[:, 1:2])),
                 reads=("moe_gt", "moe_mr"), writes=("moe_gt",))
        p.op("DVE", lambda e: e.tensor_reduce(nm[:], gt[:], AX.X, ALU.max, negate=True),
             reads=("moe_gt",), writes=("moe_nm",))
        p.op("DVE", lambda e: e.tensor_reduce(m1[:], gt[:], AX.X, ALU.max),
             reads=("moe_gt",), writes=("moe_m1",))
        p.op("ACT", lambda e: e.activation(pr[:], gt[:], AF.Exp, bias=nm[:], accum_out=sm[:]),
             reads=("moe_gt", "moe_nm"), writes=("moe_pr", "moe_sm"))
        p.op("DVE", lambda e: e.tensor_scalar(msk[:], gt[:], m1[:], None, ALU.is_equal),
             reads=("moe_gt", "moe_m1"), writes=("moe_msk",))
        p.op("DVE", lambda e: e.scalar_tensor_tensor(msk[:], msk[:], -1e30, gt[:], ALU.mult, ALU.add),
             reads=("moe_msk", "moe_gt"), writes=("moe_msk",))
        p.op("DVE", lambda e: e.tensor_reduce(m2[:], msk[:], AX.X, ALU.max),
             reads=("moe_msk",), writes=("moe_m2",))
        p.op("DVE", lambda e: e.tensor_scalar(msk[:], gt[:], m2[:], None, ALU.is_ge),
             reads=("moe_gt", "moe_m2"), writes=("moe_msk",))
        p.op("DVE", lambda e: e.tensor_mul(pr[:], pr[:], msk[:]),
             reads=("moe_pr", "moe_msk"), writes=("moe_pr",))
        if renorm:
            p.op("DVE", lambda e: e.tensor_reduce(sm[:], pr[:], AX.X, ALU.add),
                 reads=("moe_pr",), writes=("moe_sm",))
        p.op("DVE", lambda e: e.reciprocal(sm[:], sm[:]), reads=("moe_sm",), writes=("moe_sm",))
        p.op("DVE", lambda e: e.tensor_scalar_mul(pr[:], pr[:], sm[:]),
             reads=("moe_pr", "moe_sm"), writes=("moe_pr",))
        p.op("PE", (lambda e: e.transpose(ps_t[0:8, 36:36 + 128], pr[:, 0:8], g["ident"][:])),
             reads=("moe_pr", "moe_ident"), writes=("ps_b2",))
        p.op("DVE", (lambda e, t0=t0: e.tensor_copy(gf[0:8, t0:t0 + 128], ps_t[0:8, 36:36 + 128])),
             reads=("ps_b2",), writes=(gfk,))
        if gfm_bf is not None:
            gb_, gbk_ = gfm_bf
            p.op("DVE", (lambda e, t0=t0: e.tensor_copy(gb_[0:8, t0:t0 + 128], ps_t[0:8, 36:36 + 128])),
                 reads=("ps_b2",), writes=(gbk_,))


def build_l0ffn_launch(p=None):
    p = p or Prog("l0ffn")
    c = Consts(p)
    x_in = p.dram("x_in", [D, TL], F32, kind="ExternalInput").ap()
    wr = p.dram("wr", [D, 8], F32, kind="ExternalInput").ap()
    wi = p.dram("wi", [8, D, D], F32, kind="ExternalInput").ap()
    wo = p.dram("wo", [8, D, D], F32, kind="ExternalInput").ap()
    x_out = p.dram("x_out", [D, TL], F32, kind="ExternalOutput").ap()

    sel8d = p.dram("sel8", [8, 1024], F32, kind="ExternalInput").ap()
    sel8 = p.sbuf("sel8_sb", [8, 1024], BF16)
    p.dma("POOL", sel8[:], sel8d, writes=("sel8_sb",))
    wgf, colsum_b, _bg = emit_moe_prep(p, c, wr, None)

    ln = [p.sbuf(f"ln{i}", [128, TL], BF16) for i in range(DC)]
    lnkeys = [(ln[i], f"ln{i}") for i in range(DC)]
    gfm = p.sbuf("gfm", [8, TL], BF16)
    for tt in range(NTT):
        emit_ln_tile(p, c, ("hbm", x_in), lnkeys, tt, "ln", out_off=tt * TT)
        emit_moe_gates(p, c, ("hbm", x_in), wgf, colsum_b, None, tt, False, (gfm, "gfm"))

    oacc = [p.sbuf(f"oacc{i}", [128, TL], F32) for i in range(DC)]
    wie_all = [p.sbuf(f"wie_all{k}", [128, DC * D], BF16) for k in range(2)]
    woe_all = [p.sbuf(f"woe_all{k}", [128, DC * D], BF16) for k in range(2)]
    wie = [[wie_all[k][:, i * D:(i + 1) * D] for i in range(DC)] for k in range(2)]
    woe = [[woe_all[k][:, i * D:(i + 1) * D] for i in range(DC)] for k in range(2)]
    hsb = [p.sbuf(f"hsb{i}", [128, TT], BF16) for i in range(DC)]
    gb_sb = p.sbuf("gb_sb", [128, TT], F32)
    ps_h = [p.psum(f"ps_h{k}", [128, TT], F32) for k in range(2)]
    ps_o = [p.psum(f"ps_o{k}", [128, TT], F32) for k in range(2)]
    for e in range(8):
        k = e % 2
        wie_ap = bass.AP(tensor=wi.tensor, offset=wi.offset + e * D * D,
                         ap=[[D, 128], [128 * D, DC], [1, D]])
        woe_ap = bass.AP(tensor=wo.tensor, offset=wo.offset + e * D * D,
                         ap=[[D, 128], [128 * D, DC], [1, D]])
        p.dma("POOL", wie_all[k][:], wie_ap, writes=tuple(f"wie{k}_{i}" for i in range(DC)))
        p.dma("POOL", woe_all[k][:], woe_ap, writes=tuple(f"woe{k}_{i}" for i in range(DC)))
        for tt in range(NTT):
            tsl = slice(tt * TT, (tt + 1) * TT)
            p.op("PE", (lambda e_, e=e, tsl=tsl: e_.matmul(c.ps_b[:], sel8[:, e * 128:(e + 1) * 128],
                                                           gfm[0:8, tsl], start=True, stop=True)),
                 reads=("sel8_sb", "gfm"), writes=("ps_b",))
            p.op("DVE", lambda e_: e_.tensor_copy(gb_sb[:], c.ps_b[:]),
                 reads=("ps_b",), writes=("gb_sb",))
            for fi in range(DC):
                kh = fi % 2
                for di in range(DC):
                    p.op("PE", (lambda e_, fi=fi, di=di, k=k, kh=kh, tsl=tsl:
                                e_.matmul(ps_h[kh][:], wie[k][di][0:128, fi * 128:(fi + 1) * 128],
                                          ln[di][:, tsl], start=(di == 0), stop=(di == DC - 1))),
                         reads=(f"wie{k}_{di}", f"ln{di}"), writes=(f"ps_h{kh}",))
                p.op("DVE", (lambda e_, fi=fi, kh=kh: e_.scalar_tensor_tensor(
                        hsb[fi][:], ps_h[kh][:], 1.0, gb_sb[:], ALU.mult, ALU.mult)),
                     reads=(f"ps_h{kh}", "gb_sb"), writes=(f"hsb{fi}",))
            for oi in range(DC):
                ko = oi % 2
                for fi in range(DC):
                    p.op("PE", (lambda e_, oi=oi, fi=fi, k=k, ko=ko:
                                e_.matmul(ps_o[ko][:], woe[k][fi][0:128, oi * 128:(oi + 1) * 128],
                                          hsb[fi][:], start=(fi == 0), stop=(fi == DC - 1))),
                         reads=(f"woe{k}_{fi}", f"hsb{fi}"), writes=(f"ps_o{ko}",))
                if e == 0:
                    p.op("DVE", (lambda e_, oi=oi, ko=ko, tsl=tsl:
                                 e_.tensor_copy(oacc[oi][:, tsl], ps_o[ko][:])),
                         reads=(f"ps_o{ko}",), writes=(f"oacc{oi}",))
                else:
                    p.op("DVE", (lambda e_, oi=oi, ko=ko, tsl=tsl:
                                 e_.tensor_add(oacc[oi][:, tsl], oacc[oi][:, tsl], ps_o[ko][:])),
                         reads=(f"ps_o{ko}", f"oacc{oi}"), writes=(f"oacc{oi}",))
    xre = [p.sbuf(f"xre{k}", [128, TT], F32) for k in range(2)]
    outs = []
    for oi in range(DC):
        for tt in range(NTT):
            k = (oi * NTT + tt) % 2
            tsl = slice(tt * TT, (tt + 1) * TT)
            p.dma("SP", xre[k][:], x_in[oi * 128:(oi + 1) * 128, tsl], writes=(f"xre{k}",))
            p.op("DVE", (lambda e_, oi=oi, k=k, tsl=tsl: e_.scalar_tensor_tensor(
                    xre[k][:], oacc[oi][:, tsl], 0.5, xre[k][:], ALU.mult, ALU.add)),
                 reads=(f"oacc{oi}", f"xre{k}"), writes=(f"xre{k}",))
            p.dma("SP", x_out[oi * 128:(oi + 1) * 128, tsl], xre[k][:],
                  reads=(f"xre{k}",), writes=(f"xo_{oi}_{tt}",))
            outs.append(f"xo_{oi}_{tt}")
    p.final_wait(outs)
    return p


def build_l3ffn_launch(p=None):
    p = p or Prog("l3ffn")
    c = Consts(p)
    x_in = p.dram("x_in", [D, TL], F32, kind="ExternalInput").ap()
    wg = p.dram("wgd", [D, 8], F32, kind="ExternalInput").ap()
    bg = p.dram("bgd", [1, 8], F32, kind="ExternalInput").ap()
    wet = p.dram("wet", [8, D, D], F32, kind="ExternalInput").ap()
    be = p.dram("bed", [8, D], F32, kind="ExternalInput").ap()
    sel8d = p.dram("sel8", [8, 1024], F32, kind="ExternalInput").ap()
    x_out = p.dram("x_out", [D, TL], F32, kind="ExternalOutput").ap()

    sel8 = p.sbuf("sel8_sb", [8, 1024], BF16)
    p.dma("POOL", sel8[:], sel8d, writes=("sel8_sb",))
    be_sb = p.sbuf("be_sb", [8, D], BF16)
    p.dma("POOL", be_sb[:], be, writes=("be_sb",))
    wgf, colsum_b, bg_b = emit_moe_prep(p, c, wg, bg)

    ln = [p.sbuf(f"ln{i}", [128, TT], BF16) for i in range(DC)]
    gfm = p.sbuf("gfm", [8, TL], BF16)
    gfm_bf = gfm
    lne = [[p.sbuf(f"lne{k}_{i}", [128, TT], BF16) for i in range(DC)] for k in range(2)]
    wes_all = [p.sbuf(f"wes_all{k}", [128, DC * D], BF16) for k in range(2)]
    wes = [[wes_all[k][:, i * D:(i + 1) * D] for i in range(DC)] for k in range(2)]
    oacc = [p.sbuf(f"oacc{i}", [128, TT], F32) for i in range(DC)]
    gb_sb = p.sbuf("gb_sb", [128, TT], F32)
    ps_o = [p.psum(f"ps_o{k}", [128, TT], F32) for k in range(2)]
    xre = [p.sbuf(f"xre{k}", [128, TT], F32) for k in range(2)]
    outs = []
    for tt in range(NTT):
        tsl = slice(tt * TT, (tt + 1) * TT)
        emit_ln_tile(p, c, ("hbm", x_in), [(ln[i], f"ln{i}") for i in range(DC)], tt, "ln",
                     out_off=0, n_tok_tile=TT)
        emit_moe_gates(p, c, ("hbm", x_in), wgf, colsum_b, (bg_b, "bg_b"), tt, True,
                       (gfm, "gfm"))
        for e in range(8):
            k = e % 2
            wes_ap = bass.AP(tensor=wet.tensor, offset=wet.offset + e * D * D,
                             ap=[[D, 128], [128 * D, DC], [1, D]])
            p.dma("POOL", wes_all[k][:], wes_ap, writes=tuple(f"wes{k}_{i}" for i in range(DC)))
            p.op("PE", (lambda e_, e=e, tsl=tsl: e_.matmul(c.ps_b[:], sel8[:, e * 128:(e + 1) * 128],
                                                           gfm[0:8, tsl], start=True, stop=True)),
                 reads=("sel8_sb", "gfm"), writes=("ps_b",))
            p.op("DVE", lambda e_: e_.tensor_copy(gb_sb[:], c.ps_b[:]),
                 reads=("ps_b",), writes=("gb_sb",))
            for i in range(DC):
                p.op("DVE", (lambda e_, k=k, i=i: e_.tensor_mul(lne[k][i][:], ln[i][:], gb_sb[:])),
                     reads=(f"ln{i}", "gb_sb"), writes=(f"lne{k}_{i}",))
            for oi in range(DC):
                ko = oi % 2
                for di in range(DC):
                    p.op("PE", (lambda e_, k=k, di=di, oi=oi, ko=ko, e=e:
                                e_.matmul(ps_o[ko][:], wes[k][di][0:128, oi * 128:(oi + 1) * 128],
                                          lne[k][di][:], start=(di == 0),
                                          stop=(di == DC - 1 and e != 7))),
                         reads=(f"wes{k}_{di}", f"lne{k}_{di}"), writes=(f"ps_o{ko}",))
                if e == 7:
                    p.op("PE", (lambda e_, oi=oi, ko=ko, tsl=tsl:
                                e_.matmul(ps_o[ko][:], be_sb[:, oi * 128:(oi + 1) * 128],
                                          gfm_bf[0:8, tsl], start=False, stop=True)),
                         reads=("be_sb", "gfm_bf"), writes=(f"ps_o{ko}",))
                if e == 0:
                    p.op("DVE", (lambda e_, oi=oi, ko=ko: e_.tensor_copy(oacc[oi][:], ps_o[ko][:])),
                         reads=(f"ps_o{ko}",), writes=(f"oacc{oi}",))
                else:
                    p.op("DVE", (lambda e_, oi=oi, ko=ko: e_.tensor_add(oacc[oi][:], oacc[oi][:], ps_o[ko][:])),
                         reads=(f"ps_o{ko}", f"oacc{oi}"), writes=(f"oacc{oi}",))
        for oi in range(DC):
            ko = oi % 2
            p.dma("SP", xre[ko][:], x_in[oi * 128:(oi + 1) * 128, tsl], writes=(f"xre{ko}",))
            p.op("DVE", (lambda e_, oi=oi, ko=ko: e_.scalar_tensor_tensor(
                    xre[ko][:], oacc[oi][:], 0.5, xre[ko][:], ALU.mult, ALU.add)),
                 reads=(f"oacc{oi}", f"xre{ko}"), writes=(f"xre{ko}",))
            p.dma("SP", x_out[oi * 128:(oi + 1) * 128, tsl], xre[ko][:],
                  reads=(f"xre{ko}",), writes=(f"xo_{oi}_{tt}",))
            outs.append(f"xo_{oi}_{tt}")
    p.final_wait(outs)
    return p


def build_l3attn_launch(p=None):
    p = p or Prog("l3attn")
    c = Consts(p)
    NH = 16
    x_in = p.dram("x_in", [D, TL], F32, kind="ExternalInput").ap()
    wqk = p.dram("wqk", [NH, D, 128], F32, kind="ExternalInput").ap()
    wv = p.dram("wv", [D, D], F32, kind="ExternalInput").ap()
    bqk = p.dram("bqk", [128, NH], F32, kind="ExternalInput").ap()
    bv = p.dram("bv", [1, D], F32, kind="ExternalInput").ap()
    wout = p.dram("wout", [D, D], F32, kind="ExternalInput").ap()
    bout = p.dram("boutd", [128, DC], F32, kind="ExternalInput").ap()
    moff = p.dram("moff", [1, 1], F32, kind="ExternalInput").ap()
    x_out = p.dram("x_out", [D, TL], F32, kind="ExternalOutput").ap()
    qs = p.dram("qs", [NH * 64, TL], BF16)
    ks = p.dram("ks", [NH * 64, TL], BF16)
    vs = p.dram("vs", [TL, D], BF16)
    kg = p.dram("kg", [2 * NH * 64, TL], BF16)
    vg = p.dram("vg", [2 * TL, D], BF16)

    onesb = p.sbuf("onesb", [128, 1], BF16)
    onesrb = p.sbuf("onesrb", [1, 128], BF16)
    p.op("DVE", lambda e: e.memset(onesb[:], 1.0), writes=("onesb",))
    p.op("DVE", lambda e: e.memset(onesrb[:], 1.0), writes=("onesrb",))
    eighth = c.col(0.125)
    moffc = p.sbuf("moffc", [1, 1], F32)
    p.dma("SP", moffc[:], moff, writes=("moffc",))
    bqk_sb = p.sbuf("bqk_sb", [128, NH], F32)
    p.dma("SP", bqk_sb[:], bqk, writes=("bqk_sb",))
    bqk8 = p.sbuf("bqk8", [128, NH], F32)
    p.op("DVE", lambda e: e.tensor_scalar_mul(bqk8[:], bqk_sb[:], 0.125),
         reads=("bqk_sb",), writes=("bqk8",))
    bv_row = p.sbuf("bv_row", [1, D], F32)
    bv_b = p.sbuf("bv_b", [128, D], F32)
    p.dma("SP", bv_row[:], bv, writes=("bv_row",))
    for j in range(2):
        p.op("PE", (lambda e, j=j: e.matmul(c.ps_b[:], c.ones_row_f[:], bv_row[0:1, j * TT:(j + 1) * TT],
                                            start=True, stop=True)),
             reads=("c_ones_row_f", "bv_row"), writes=("ps_b",))
        p.op("DVE", (lambda e, j=j: e.tensor_copy(bv_b[:, j * TT:(j + 1) * TT], c.ps_b[:])),
             reads=("ps_b",), writes=("bv_b",))

    ln = [p.sbuf(f"ln{i}", [128, TL], BF16) for i in range(DC)]
    lnkeys = [(ln[i], f"ln{i}") for i in range(DC)]
    for tt in range(NTT):
        emit_ln_tile(p, c, ("hbm", x_in), lnkeys, tt, "ln", out_off=tt * TT)

    wqk_sb = [p.sbuf(f"wqk_sb{k}", [128, DC * 128], BF16) for k in range(2)]
    ps_qk = [p.psum(f"ps_qk{k}", [128, TT], F32) for k in range(2)]
    qkev = p.sbuf("qkev", [128, TT], BF16)
    for h in range(NH):
        k = h % 2
        qk_ap = bass.AP(tensor=wqk.tensor, offset=wqk.offset + h * D * 128,
                        ap=[[128, 128], [128 * 128, DC], [1, 128]])
        p.dma("POOL", wqk_sb[k][:], qk_ap, writes=(f"wqk_sb{k}",))
        for tt in range(NTT):
            tsl = slice(tt * TT, (tt + 1) * TT)
            for i in range(DC):
                p.op("PE", (lambda e, k=k, i=i, tt=tt: e.matmul(
                        ps_qk[k][:], wqk_sb[k][:, i * 128:(i + 1) * 128], ln[i][:, tt * TT:(tt + 1) * TT],
                        start=(i == 0), stop=(i == DC - 1))),
                     reads=(f"wqk_sb{k}", f"ln{i}"), writes=(f"ps_qk{k}",))
            p.op("DVE", (lambda e, k=k, h=h: e.tensor_scalar(qkev[0:64, :], ps_qk[k][0:64, :],
                                                             bqk8[0:64, h:h + 1], 0.125,
                                                             ALU.add, ALU.mult)),
                 reads=(f"ps_qk{k}", "bqk8"), writes=("qkev",))
            p.op("DVE", (lambda e, k=k, h=h: e.tensor_scalar_add(qkev[64:128, :], ps_qk[k][64:128, :],
                                                                 bqk_sb[64:128, h:h + 1])),
                 reads=(f"ps_qk{k}", "bqk_sb", "qkev"), writes=("qkev",))
            p.dma("SP", qs[h * 64:(h + 1) * 64, tsl], qkev[0:64, :], reads=("qkev",), writes=("qs",))
            p.dma("SP", ks[h * 64:(h + 1) * 64, tsl], qkev[64:128, :], reads=("qkev",), writes=("ks",))
    wv_sb = p.sbuf("wv_sb", [128, DC * D], BF16)
    load_wchunks(p, "POOL", wv_sb[:], "wv_sb", wv, D, 0, DC, 0, D)
    vev = p.sbuf("vev", [128, D], BF16)
    for ts in range(TL // 128):
        k = ts % 2
        for half in range(2):
            for i in range(DC):
                p.op("PE", (lambda e, k=k, i=i, ts=ts, half=half: e.matmul(
                        ps_qk[k][:], ln[i][:, ts * 128:(ts + 1) * 128],
                        wv_sb[:, i * D + half * TT:i * D + (half + 1) * TT],
                        start=(i == 0), stop=(i == DC - 1))),
                     reads=("wv_sb", f"ln{i}"), writes=(f"ps_qk{k}",))
            p.op("DVE", (lambda e, k=k, half=half: e.tensor_add(vev[:, half * TT:(half + 1) * TT],
                                                                ps_qk[k][:], bv_b[:, half * TT:(half + 1) * TT])),
                 reads=(f"ps_qk{k}", "bv_b"), writes=("vev",))
        p.dma("SP", vs[ts * 128:(ts + 1) * 128, :], vev[:], reads=("vev",), writes=("vs",))

    p.collective("AllGather", ALU.bypass, [[0, 1], [2, 3], [4, 5], [6, 7]],
                 [ks.ap()], [kg.ap()], reads=("ks",), writes=("kg",))
    p.collective("AllGather", ALU.bypass, [[0, 1], [2, 3], [4, 5], [6, 7]],
                 [vs.ap()], [vg.ap()], reads=("vs",), writes=("vg",))

    ps_sc = ps_qk
    ps_av = c.ps_b
    ps_sum = c.ps_s
    q_sb = p.sbuf("q_sb", [64, TL], BF16)
    kl_sb = p.sbuf("kl_sb", [64, TL], BF16)
    kr_sb = p.sbuf("kr_sb", [64, TL], BF16)
    vl_sb = p.sbuf("vl_sb", [128, (TL // 128) * 64], BF16)
    vr_sb = p.sbuf("vr_sb", [128, (TL // 128) * 64], BF16)
    sqt = p.sbuf("sqt", [64, TL], BF16)
    n2r = p.sbuf("n2r", [1, TL], F32)
    nbl = p.sbuf("nbl", [1, TL], BF16)
    nbr = p.sbuf("nbr", [1, TL], BF16)
    mk_s = p.sbuf("mk_s", [1, 1], F32)
    NPT = 16
    pt_sb = [p.sbuf(f"pt_sb{k}", [128, TT], BF16) for k in range(NPT)]
    rs_row = p.sbuf("rs_row", [1, TT], F32)
    rb_sb = p.sbuf("rb_sb", [64, TT], F32)
    att = [p.sbuf(f"att{i}", [128, TL], BF16) for i in range(DC)]
    NT128 = TL // 128

    for h in range(NH):
        p.dma("SP", q_sb[:], qs[h * 64:(h + 1) * 64, :], reads=("qs",), writes=("q_sb",))
        p.dma("SP", kl_sb[:], ks[h * 64:(h + 1) * 64, :], reads=("ks",), writes=("kl_sb",))
        p.dma("SP", kr_sb[:], kg[h * 64:(h + 1) * 64, :], reads=("kg",), writes=("kr_sb",))
        vl_ap = bass.AP(tensor=vs.ap().tensor, offset=h * 64,
                        ap=[[D, 128], [128 * D, TL // 128], [1, 64]])
        vr_ap = bass.AP(tensor=vg.ap().tensor, offset=h * 64,
                        ap=[[D, 128], [128 * D, TL // 128], [1, 64]])
        p.dma("POOL", vl_sb[:], vl_ap, reads=("vs",), writes=("vl_sb",))
        p.dma("POOL", vr_sb[:], vr_ap, reads=("vg",), writes=("vr_sb",))
        for which, ksb, kkey, off in ((0, kl_sb, "kl_sb", 0), (1, kr_sb, "kr_sb", TL)):
            p.op("ACT", (lambda e, ksb=ksb: e.activation(sqt[:], ksb[:], AF.Square)),
                 reads=(kkey,), writes=("sqt",))
            for j in range(NTT):
                p.op("PE", (lambda e, j=j: e.matmul(ps_sum[:], onesb[0:64, :], sqt[:, j * TT:(j + 1) * TT],
                                                    start=True, stop=True)),
                     reads=("onesb", "sqt"), writes=("ps_s",))
                p.op("DVE", (lambda e, j=j, which=which: e.tensor_copy(
                        n2r[0:1, (j * TT):(j + 1) * TT] if which == 0 else n2r[0:1, j * TT:(j + 1) * TT],
                        ps_sum[:])),
                     reads=("ps_s",), writes=("n2r_a" if which == 0 else "n2r_b",))
            p.op("DVE", (lambda e, which=which: e.tensor_reduce(mk_s[:], n2r[:], AX.X, ALU.max)),
                 reads=("n2r_a", "n2r_b"), writes=(f"mk{which}",))
            if which == 0:
                if not hasattr(p, "_mk0"):
                    p._mk0 = p.sbuf("mk0_sb", [1, 1], F32)
                mk0 = p._mk0
                p.op("DVE", lambda e: e.tensor_copy(mk0[:], mk_s[:]), reads=("mk0",), writes=("mk0_sb",))
        p.op("DVE", lambda e: e.tensor_max(mk_s[:], mk_s[:], mk0[:]), reads=("mk1", "mk0_sb"), writes=("mk_c",))
        p.op("ACT", lambda e: e.activation(mk_s[:], mk_s[:], AF.Sqrt), reads=("mk_c",), writes=("mk_c",))
        p.op("DVE", lambda e: e.tensor_scalar_mul(mk_s[:], mk_s[:], -1.0), reads=("mk_c",), writes=("mk_c",))
        p.op("ACT", lambda e: e.activation(sqt[:], q_sb[:], AF.Square), reads=("q_sb",), writes=("sqt",))
        for j in range(NTT):
            p.op("PE", (lambda e, j=j: e.matmul(ps_sum[:], onesb[0:64, :], sqt[:, j * TT:(j + 1) * TT],
                                                start=True, stop=True)),
                 reads=("onesb", "sqt"), writes=("ps_s",))
            p.op("ACT", (lambda e, j=j: e.activation(n2r[0:1, j * TT:(j + 1) * TT], ps_sum[:], AF.Sqrt)),
                 reads=("ps_s",), writes=("qn_r",))
        p.op("DVE", lambda e: e.tensor_scalar_mul(nbl[:], n2r[:], mk_s[:]),
             reads=("qn_r", "mk_c"), writes=("nbl",))
        p.op("DVE", lambda e: e.tensor_scalar(nbr[:], n2r[:], mk_s[:], moffc[:], ALU.mult, ALU.add),
             reads=("qn_r", "mk_c", "moffc"), writes=("nbr",))
        for tqt in range(NTT):
            tql = slice(tqt * TT, (tqt + 1) * TT)
            blocks = [("r", i) for i in range(NT128)] + [("l", i) for i in range((tqt + 1) * (TT // 128))]
            first = True
            for bi, (src, tk) in enumerate(blocks):
                kk = bi % 2
                ksb, kkey = (kr_sb, "kr_sb") if src == "r" else (kl_sb, "kl_sb")
                vsb, vkey = (vr_sb, "vr_sb") if src == "r" else (vl_sb, "vl_sb")
                nb, nbkey = (nbr, "nbr") if src == "r" else (nbl, "nbl")
                p.op("PE", (lambda e, kk=kk, ksb=ksb, tk=tk, tql=tql: e.matmul(
                        ps_sc[kk][:], ksb[:, tk * 128:(tk + 1) * 128], q_sb[:, tql],
                        start=True, stop=False)),
                     reads=(kkey, "q_sb"), writes=(f"ps_qk{kk}",))
                p.op("PE", (lambda e, kk=kk, nb=nb, tql=tql: e.matmul(
                        ps_sc[kk][:], onesrb[0:1, :], nb[0:1, tql], start=False, stop=True)),
                     reads=("onesrb", nbkey), writes=(f"ps_qk{kk}",))
                p.op("ACT", (lambda e, kk=kk: e.activation(pt_sb[kk][:], ps_sc[kk][:], AF.Exp)),
                     reads=(f"ps_qk{kk}",), writes=(f"pt_sb{kk}",))
                if src == "l" and tk * 128 >= tqt * TT:
                    p.op("POOL", (lambda e, kk=kk, tqt=tqt, tk=tk: e.affine_select(
                            out=pt_sb[kk][:], in_=pt_sb[kk][:], pattern=[[1, TT]],
                            compare_op=ALU.is_ge, fill=0.0, base=tqt * TT - tk * 128,
                            channel_multiplier=-1)),
                         reads=(f"pt_sb{kk}",), writes=(f"pt_sb{kk}",))
                p.op("PE", (lambda e, kk=kk, first=first, last=(bi == len(blocks) - 1):
                            e.matmul(ps_sum[:], onesb[:], pt_sb[kk][:], start=first, stop=last)),
                     reads=("onesb", f"pt_sb{kk}"), writes=("ps_s",))
                p.op("PE", (lambda e, kk=kk, vsb=vsb, tk=tk, first=first, last=(bi == len(blocks) - 1):
                            e.matmul(ps_av[0:64, :], vsb[:, tk * 64:(tk + 1) * 64], pt_sb[kk][:],
                                     start=first, stop=last)),
                     reads=(vkey, f"pt_sb{kk}"), writes=("ps_b",))
                first = False
            p.op("DVE", lambda e: e.reciprocal(rs_row[:], ps_sum[:]), reads=("ps_s",), writes=("rs_row",))
            p.op("PE", (lambda e: e.matmul(ps_sc[0][0:64, :], c.ones_row_f[0:1, 0:64], rs_row[:],
                                           start=True, stop=True)),
                 reads=("c_ones_row_f", "rs_row"), writes=("ps_qk0",))
            p.op("DVE", lambda e: e.tensor_copy(rb_sb[:], ps_sc[0][0:64, :]), reads=("ps_qk0",), writes=("rb_sb",))
            ci, po = h // 2, (h % 2) * 64
            p.op("DVE", (lambda e, ci=ci, po=po, tql=tql: e.tensor_mul(att[ci][po:po + 64, tql],
                                                                      ps_av[0:64, :], rb_sb[:])),
                 reads=("ps_b", "rb_sb"), writes=(f"att{ci}",))

    wo_sb = p.sbuf("wo_sb", [128, DC * 128], BF16)
    bo_sb = p.sbuf("bo_sb", [128, DC], F32)
    p.dma("SP", bo_sb[:], bout, writes=("bo_sb",))
    ps_o = c.ps_b2
    xre = [p.sbuf(f"xre{k}", [128, TT], F32) for k in range(2)]
    outs = []
    for oi in range(DC):
        load_wchunks(p, "POOL", wo_sb[:], "wo_sb", wout, D, 0, DC, oi * 128, 128)
        for tt in range(NTT):
            k = tt % 2
            tsl = slice(tt * TT, (tt + 1) * TT)
            for i in range(DC):
                p.op("PE", (lambda e, i=i, tt=tt: e.matmul(ps_o[:], wo_sb[:, i * 128:(i + 1) * 128],
                                                           att[i][:, tt * TT:(tt + 1) * TT],
                                                           start=(i == 0), stop=(i == DC - 1))),
                     reads=("wo_sb", f"att{i}"), writes=("ps_b2",))
            p.dma("SP", xre[k][:], x_in[oi * 128:(oi + 1) * 128, tsl], writes=(f"xre{k}",))
            p.op("DVE", (lambda e, oi=oi, k=k: e.scalar_tensor_tensor(
                    xre[k][:], ps_o[:], 1.0, xre[k][:], ALU.mult, ALU.add)),
                 reads=("ps_b2", f"xre{k}"), writes=(f"xre{k}",))
            p.op("DVE", (lambda e, oi=oi, k=k: e.tensor_scalar_add(xre[k][:], xre[k][:],
                                                                   bo_sb[:, oi:oi + 1])),
                 reads=(f"xre{k}", "bo_sb"), writes=(f"xre{k}",))
            p.dma("SP", x_out[oi * 128:(oi + 1) * 128, tsl], xre[k][:],
                  reads=(f"xre{k}",), writes=(f"xo_{oi}_{tt}",))
            outs.append(f"xo_{oi}_{tt}")
    p.final_wait(outs)
    return p


def build_l2attn_launch(p=None):
    p = p or Prog("l2attn")
    c = Consts(p)
    NJ = 16
    SL = 256
    SP_ = 4096
    QSC = 1.0 / np.sqrt(2048.0)
    x_in = p.dram("x_in", [D, TL], F32, kind="ExternalInput").ap()
    wqk = p.dram("wqk", [NJ, D, 128], F32, kind="ExternalInput").ap()
    wv = p.dram("wv", [NJ, D, 64], F32, kind="ExternalInput").ap()
    bqk = p.dram("bqk", [128, NJ], F32, kind="ExternalInput").ap()
    bv = p.dram("bv", [64, NJ], F32, kind="ExternalInput").ap()
    x_out = p.dram("x_out", [D, TL], F32, kind="ExternalOutput").ap()

    onesb = p.sbuf("onesb", [128, 1], BF16)
    onesrb = p.sbuf("onesrb", [1, 128], BF16)
    p.op("DVE", lambda e: e.memset(onesb[:], 1.0), writes=("onesb",))
    p.op("DVE", lambda e: e.memset(onesrb[:], 1.0), writes=("onesrb",))
    bqk_sb = p.sbuf("bqk_sb", [128, NJ], F32)
    p.dma("SP", bqk_sb[:], bqk, writes=("bqk_sb",))
    bv_sb = p.sbuf("bv_sb", [64, NJ], F32)
    p.dma("SP", bv_sb[:], bv, writes=("bv_sb",))
    identb = p.sbuf("identb", [128, 128], BF16)
    p.op("POOL", lambda e: e.memset(identb[:], 1.0), writes=("identb",))
    p.op("POOL", lambda e: e.affine_select(out=identb[:], in_=identb[:], pattern=[[-1, 128]],
                                           compare_op=ALU.is_equal, fill=0.0, base=0,
                                           channel_multiplier=1),
         reads=("identb",), writes=("identb",))

    ln = [p.sbuf(f"ln{i}", [128, TL], BF16) for i in range(DC)]
    lnkeys = [(ln[i], f"ln{i}") for i in range(DC)]
    for tt in range(NTT):
        emit_ln_tile(p, c, ("hbm", x_in), lnkeys, tt, "ln", out_off=tt * TT)

    wqk_sb = [p.sbuf(f"wqk_sb{k}", [128, DC * 128], BF16) for k in range(2)]
    wv_sb = [p.sbuf(f"wv_sb{k}", [128, DC * 64], BF16) for k in range(2)]
    ps_qk = [p.psum(f"ps_qk{k}", [128, TT], F32) for k in range(2)]
    q_sb = p.sbuf("q_sb", [64, SP_], BF16)
    k_sb = p.sbuf("k_sb", [64, SP_], BF16)
    vf_sb = p.sbuf("vf_sb", [64, SP_], BF16)
    vt_sb = p.sbuf("vt_sb", [128, (SP_ // 128) * 64], BF16)
    sqt = p.sbuf("sqt", [64, SP_], BF16)
    n2r = p.sbuf("n2r", [1, SP_], F32)
    nbl = p.sbuf("nbl", [1, SP_], BF16)
    mk_s = p.sbuf("mk_s", [1, 1], F32)
    NPT = 16
    pt_sb = [p.sbuf(f"pt_sb{k}", [128, TT], BF16) for k in range(NPT)]
    rs_row = p.sbuf("rs_row", [1, TT], F32)
    rb_sb = p.sbuf("rb_sb", [64, TT], F32)
    att_h = p.sbuf("att_h", [64, SP_], BF16)
    xre = [p.sbuf(f"xre{k}", [64, SL], F32) for k in range(4)]
    ps_av = c.ps_b
    ps_sum = c.ps_s
    NT128 = SP_ // 128
    outs = []

    for h in range(8):
        ssl = slice(h * SL, (h + 1) * SL)
        for j in range(NJ):
            k = j % 2
            qk_ap = bass.AP(tensor=wqk.tensor, offset=wqk.offset + j * D * 128,
                            ap=[[128, 128], [128 * 128, DC], [1, 128]])
            p.dma("POOL", wqk_sb[k][:], qk_ap, writes=(f"wqk_sb{k}",))
            v_ap = bass.AP(tensor=wv.tensor, offset=wv.offset + j * D * 64,
                           ap=[[64, 128], [128 * 64, DC], [1, 64]])
            p.dma("POOL", wv_sb[k][:], v_ap, writes=(f"wv_sb{k}",))
            for i in range(DC):
                p.op("PE", (lambda e, k=k, i=i, ssl=ssl: e.matmul(
                        ps_qk[k][0:128, 0:SL], wqk_sb[k][:, i * 128:(i + 1) * 128], ln[i][:, ssl],
                        start=(i == 0), stop=(i == DC - 1))),
                     reads=(f"wqk_sb{k}", f"ln{i}"), writes=(f"ps_qk{k}",))
            for i in range(DC):
                p.op("PE", (lambda e, k=k, i=i, ssl=ssl: e.matmul(
                        ps_qk[k][0:64, SL:2 * SL], wv_sb[k][:, i * 64:(i + 1) * 64], ln[i][:, ssl],
                        start=(i == 0), stop=(i == DC - 1))),
                     reads=(f"wv_sb{k}", f"ln{i}"), writes=(f"ps_qk{k}",))
            qap = bass.AP(tensor=q_sb.ap().tensor, offset=j, ap=[[q_sb.ap().ap[0][0], 64], [NJ, SL]])
            kap = bass.AP(tensor=k_sb.ap().tensor, offset=j, ap=[[k_sb.ap().ap[0][0], 64], [NJ, SL]])
            vap = bass.AP(tensor=vf_sb.ap().tensor, offset=j, ap=[[vf_sb.ap().ap[0][0], 64], [NJ, SL]])
            p.op("DVE", (lambda e, k=k, j=j, qap=qap: e.tensor_scalar(
                    qap, ps_qk[k][0:64, 0:SL], bqk_sb[0:64, j:j + 1], QSC, ALU.add, ALU.mult)),
                 reads=(f"ps_qk{k}", "bqk_sb"), writes=("q_sb",))
            p.op("DVE", (lambda e, k=k, j=j, kap=kap: e.tensor_scalar_add(
                    kap, ps_qk[k][64:128, 0:SL], bqk_sb[64:128, j:j + 1])),
                 reads=(f"ps_qk{k}", "bqk_sb"), writes=("k_sb",))
            p.op("DVE", (lambda e, k=k, j=j, vap=vap: e.tensor_scalar_add(
                    vap, ps_qk[k][0:64, SL:2 * SL], bv_sb[0:64, j:j + 1])),
                 reads=(f"ps_qk{k}", "bv_sb"), writes=("vf_sb",))
        psb = [ps_qk[k].bitcast(BF16) for k in range(2)]
        for ts in range(NT128):
            k = ts % 2
            p.op("PE", (lambda e, ts=ts, k=k: e.transpose(psb[k][0:128, 0:64],
                                                          vf_sb[:, ts * 128:(ts + 1) * 128], identb[0:64, 0:64])),
                 reads=("vf_sb", "identb"), writes=(f"ps_qk{k}",))
            p.op("DVE", (lambda e, ts=ts, k=k: e.tensor_copy(vt_sb[:, ts * 64:(ts + 1) * 64],
                                                             psb[k][0:128, 0:64])),
                 reads=(f"ps_qk{k}",), writes=("vt_sb",))
        p.op("ACT", lambda e: e.activation(sqt[:], k_sb[:], AF.Square), reads=("k_sb",), writes=("sqt",))
        for j in range(SP_ // TT):
            p.op("PE", (lambda e, j=j: e.matmul(ps_sum[:], onesb[0:64, :], sqt[:, j * TT:(j + 1) * TT],
                                                start=True, stop=True)),
                 reads=("onesb", "sqt"), writes=("ps_s",))
            p.op("DVE", (lambda e, j=j: e.tensor_copy(n2r[0:1, j * TT:(j + 1) * TT], ps_sum[:])),
                 reads=("ps_s",), writes=("n2r",))
        p.op("DVE", lambda e: e.tensor_reduce(mk_s[:], n2r[:], AX.X, ALU.max), reads=("n2r",), writes=("mk_s",))
        p.op("ACT", lambda e: e.activation(mk_s[:], mk_s[:], AF.Sqrt), reads=("mk_s",), writes=("mk_s",))
        p.op("DVE", lambda e: e.tensor_scalar_mul(mk_s[:], mk_s[:], -1.0), reads=("mk_s",), writes=("mk_s",))
        p.op("ACT", lambda e: e.activation(sqt[:], q_sb[:], AF.Square), reads=("q_sb",), writes=("sqt",))
        for j in range(SP_ // TT):
            p.op("PE", (lambda e, j=j: e.matmul(ps_sum[:], onesb[0:64, :], sqt[:, j * TT:(j + 1) * TT],
                                                start=True, stop=True)),
                 reads=("onesb", "sqt"), writes=("ps_s",))
            p.op("ACT", (lambda e, j=j: e.activation(n2r[0:1, j * TT:(j + 1) * TT], ps_sum[:], AF.Sqrt)),
                 reads=("ps_s",), writes=("n2r",))
        p.op("DVE", lambda e: e.tensor_scalar_mul(nbl[:], n2r[:], mk_s[:]), reads=("n2r", "mk_s"), writes=("nbl",))
        NTQ = SP_ // TT
        for tqt in range(NTQ):
            tql = slice(tqt * TT, (tqt + 1) * TT)
            nblk = (tqt + 1) * (TT // 128)
            scb = [ps_qk[0], ps_qk[1], c.ps_b2]
            sck = ["ps_qk0", "ps_qk1", "ps_b2"]

            def phase1(tk):
                kk = tk % 3
                jj = tk % NPT
                p.op("PE", (lambda e, kk=kk, tk=tk, tql=tql: e.matmul(
                        scb[kk][0:128, :], k_sb[:, tk * 128:(tk + 1) * 128], q_sb[:, tql],
                        start=True, stop=False)),
                     reads=("k_sb", "q_sb"), writes=(sck[kk],))
                p.op("PE", (lambda e, kk=kk, tql=tql: e.matmul(
                        scb[kk][0:128, :], onesrb[0:1, :], nbl[0:1, tql], start=False, stop=True)),
                     reads=("onesrb", "nbl"), writes=(sck[kk],))
                p.op("ACT", (lambda e, kk=kk, jj=jj: e.activation(pt_sb[jj][:], scb[kk][0:128, :], AF.Exp)),
                     reads=(sck[kk],), writes=(f"pt_sb{jj}",))
                if tk * 128 >= tqt * TT:
                    p.op("POOL", (lambda e, jj=jj, tk=tk, tqt=tqt: e.affine_select(
                            out=pt_sb[jj][:], in_=pt_sb[jj][:], pattern=[[1, TT]],
                            compare_op=ALU.is_ge, fill=0.0, base=tqt * TT - tk * 128,
                            channel_multiplier=-1)),
                         reads=(f"pt_sb{jj}",), writes=(f"pt_sb{jj}",))

            def phase2(tk):
                jj = tk % NPT
                first = tk == 0
                last = tk == nblk - 1
                p.op("PE", (lambda e, jj=jj, first=first, last=last:
                            e.matmul(ps_sum[:], onesb[:], pt_sb[jj][:], start=first, stop=last)),
                     reads=("onesb", f"pt_sb{jj}"), writes=("ps_s",))
                p.op("PE", (lambda e, jj=jj, tk=tk, first=first, last=last:
                            e.matmul(ps_av[0:64, :], vt_sb[:, tk * 64:(tk + 1) * 64], pt_sb[jj][:],
                                     start=first, stop=last)),
                     reads=("vt_sb", f"pt_sb{jj}"), writes=("ps_b",))

            G = NPT // 2
            ngrp = (nblk + G - 1) // G
            for gidx in range(ngrp):
                for tk in range(gidx * G, min((gidx + 1) * G, nblk)):
                    phase1(tk)
                if gidx > 0:
                    for tk in range((gidx - 1) * G, gidx * G):
                        phase2(tk)
            for tk in range((ngrp - 1) * G, nblk):
                phase2(tk)
            p.op("DVE", lambda e: e.reciprocal(rs_row[:], ps_sum[:]), reads=("ps_s",), writes=("rs_row",))
            p.op("PE", (lambda e: e.matmul(ps_qk[0][0:64, :], c.ones_row_f[0:1, 0:64], rs_row[:],
                                           start=True, stop=True)),
                 reads=("c_ones_row_f", "rs_row"), writes=("ps_qk0",))
            p.op("DVE", lambda e: e.tensor_copy(rb_sb[:], ps_qk[0][0:64, :]), reads=("ps_qk0",), writes=("rb_sb",))
            p.op("DVE", (lambda e, tql=tql: e.tensor_mul(att_h[:, tql], ps_av[0:64, :], rb_sb[:])),
                 reads=("ps_b", "rb_sb"), writes=("att_h",))
        for ci in range(DC):
            for half in range(2):
                k = (2 * ci + half) % 4
                jj = 2 * ci + half
                r0 = ci * 128 + half * 64
                p.dma("SP", xre[k][:], x_in[r0:r0 + 64, ssl], writes=(f"xre{k}",))
                aap = bass.AP(tensor=att_h.ap().tensor, offset=jj,
                              ap=[[att_h.ap().ap[0][0], 64], [NJ, SL]])
                p.op("DVE", (lambda e, k=k, aap=aap: e.tensor_add(
                        xre[k][:], xre[k][:], aap)),
                     reads=(f"xre{k}", "att_h"), writes=(f"xre{k}",))
                p.dma("SP", x_out[r0:r0 + 64, ssl], xre[k][:],
                      reads=(f"xre{k}",), writes=(f"xo_{h}_{ci}_{half}",))
                outs.append(f"xo_{h}_{ci}_{half}")
    p.final_wait(outs)
    return p


def build_l0attn_launch(p=None):
    p = p or Prog("l0attn")
    c = Consts(p)
    NH = 16
    CH = 128
    NCH = TL // CH
    x_in = p.dram("x_in", [D, TL], F32, kind="ExternalInput").ap()
    wqkv = p.dram("wqkv", [D, 3 * D], F32, kind="ExternalInput").ap()
    bqkv = p.dram("bqkv", [128, 24], F32, kind="ExternalInput").ap()
    wout = p.dram("wout", [D, D], F32, kind="ExternalInput").ap()
    boutd = p.dram("boutd", [128, DC], F32, kind="ExternalInput").ap()
    n1w = p.dram("n1w", [128, DC], F32, kind="ExternalInput").ap()
    n2w = p.dram("n2w", [128, DC], F32, kind="ExternalInput").ap()
    w1 = p.dram("w1", [D, 4 * D], F32, kind="ExternalInput").ap()
    b1d = p.dram("b1d", [128, 32], F32, kind="ExternalInput").ap()
    w2 = p.dram("w2", [4 * D, D], F32, kind="ExternalInput").ap()
    b2d = p.dram("b2d", [128, DC], F32, kind="ExternalInput").ap()
    poscs = p.dram("poscs", [2, TL], F32, kind="ExternalInput").ap()
    moff = p.dram("moff", [64, 1], F32, kind="ExternalInput").ap()
    x_out = p.dram("x_out", [D, TL], F32, kind="ExternalOutput").ap()
    h_scr = p.dram("h_scr", [D, TL], F32)
    st_out = p.dram("st_out", [NH * 64, 130], F32)
    st_g = p.dram("st_g", [2 * NH * 64, 130], F32)

    onesb = p.sbuf("onesb", [128, 1], BF16)
    p.op("DVE", lambda e: e.memset(onesb[:], 1.0), writes=("onesb",))
    identb = p.sbuf("identb", [128, 128], BF16)
    p.op("POOL", lambda e: e.memset(identb[:], 1.0), writes=("identb",))
    p.op("POOL", lambda e: e.affine_select(out=identb[:], in_=identb[:], pattern=[[-1, 128]],
                                           compare_op=ALU.is_equal, fill=0.0, base=0,
                                           channel_multiplier=1),
         reads=("identb",), writes=("identb",))
    moffc = p.sbuf("moffc", [64, 1], F32)
    p.dma("SP", moffc[:], moff, writes=("moffc",))
    bq_sb = p.sbuf("bq_sb", [128, 24], F32)
    p.dma("SP", bq_sb[:], bqkv, writes=("bq_sb",))
    n1_sb = p.sbuf("n1_sb", [128, DC], F32)
    n2_sb = p.sbuf("n2_sb", [128, DC], F32)
    p.dma("SP", n1_sb[:], n1w, writes=("n1_sb",))
    p.dma("SP", n2_sb[:], n2w, writes=("n2_sb",))
    pos_sb = [p.sbuf(f"pos_sb{w}", [1, TL], F32) for w in range(2)]
    for w in range(2):
        p.dma("SP", pos_sb[w][:], poscs[w:w + 1, :], writes=(f"pos_sb{w}",))
    cos_b = p.sbuf("cos_b", [128, TL], BF16)
    sin_b = p.sbuf("sin_b", [128, TL], BF16)
    for w_i, (dst, key) in enumerate(((cos_b, "cos_b"), (sin_b, "sin_b"))):
        for tt in range(NTT):
            p.op("PE", (lambda e, w_i=w_i, tt=tt: e.matmul(c.ps_b[:], c.ones_row_f[:],
                                                           pos_sb[w_i][0:1, tt * TT:(tt + 1) * TT],
                                                           start=True, stop=True)),
                 reads=("c_ones_row_f", f"pos_sb{w_i}"), writes=("ps_b",))
            p.op("DVE", (lambda e, dst=dst, tt=tt, key=key: e.tensor_copy(dst[:, tt * TT:(tt + 1) * TT], c.ps_b[:])),
                 reads=("ps_b",), writes=(key,))

    z_scr = p.dram("z_scr", [D, TL], F32)
    zt = [p.sbuf(f"zt{i}", [128, TT], F32) for i in range(DC)]
    zkeys = [(zt[i], f"zt{i}") for i in range(DC)]
    arm = [p.sbuf(f"arm{i}", [128, TL], BF16) for i in range(DC)]
    for tt in range(NTT):
        emit_ln_tile(p, c, ("hbm", x_in), zkeys, tt, "ln", out_off=0, n_tok_tile=TT)
        for i in range(DC):
            p.dma("SP", z_scr[i * 128:(i + 1) * 128, tt * TT:(tt + 1) * TT], zt[i][:],
                  reads=(f"zt{i}",), writes=("z_scr",))
    for tt in range(NTT):
        emit_ln_tile(p, c, ("hbm", z_scr), [(arm[i], f"arm{i}") for i in range(DC)], tt, "rm",
                     kind="rms", eps=1e-6, gain=(n1_sb, "n1_sb", 0), out_off=tt * TT)

    wq_sb = [p.sbuf(f"wq_sb{k}", [128, DC * 128], BF16) for k in range(2)]
    ps_mm = [p.psum(f"ps_mm{k}", [128, TT], F32) for k in range(2)]
    qkv_scr = p.dram("qkv_scr", [5, D, TL], BF16)
    rl = p.sbuf("rl", [128, TT], BF16)
    stg = [p.sbuf(f"stg{k}", [128, TT], BF16) for k in range(4)]
    for fc in range(24):
        k = fc % 2
        load_wchunks(p, "POOL", wq_sb[k][:], f"wq_sb{k}", wqkv, 3 * D, 0, DC, fc * 128, 128)
        for tt in range(NTT):
            tsl = slice(tt * TT, (tt + 1) * TT)
            for i in range(DC):
                p.op("PE", (lambda e, k=k, i=i, tt=tt: e.matmul(
                        ps_mm[k][:], wq_sb[k][:, i * 128:(i + 1) * 128],
                        arm[i][:, tt * TT:(tt + 1) * TT], start=(i == 0), stop=(i == DC - 1))),
                     reads=(f"wq_sb{k}", f"arm{i}"), writes=(f"ps_mm{k}",))
            if fc < 16:
                p.op("ACT", (lambda e, k=k, fc=fc: e.activation(rl[:], ps_mm[k][:], AF.Relu,
                                                               bias=bq_sb[:, fc:fc + 1])),
                     reads=(f"ps_mm{k}", "bq_sb"), writes=("rl",))
                base = 0 if fc < 8 else 2
                ci = fc % 8
                for si, (wsb, wkey) in enumerate(((cos_b, "cos_b"), (sin_b, "sin_b"))):
                    sg = (2 * tt + si) % 4
                    p.op("DVE", (lambda e, sg=sg, tsl=tsl, wsb=wsb: e.tensor_mul(
                            stg[sg][:], rl[:], wsb[:, tsl])),
                         reads=("rl", wkey), writes=(f"stg{sg}",))
                    p.dma("SP", qkv_scr[base + si, ci * 128:(ci + 1) * 128, tsl], stg[sg][:],
                          reads=(f"stg{sg}",), writes=("qkv_scr",))
            else:
                sg = tt % 4
                p.op("DVE", (lambda e, k=k, fc=fc, sg=sg: e.tensor_scalar_add(
                        stg[sg][:], ps_mm[k][:], bq_sb[:, fc:fc + 1])),
                     reads=(f"ps_mm{k}", "bq_sb"), writes=(f"stg{sg}",))
                p.dma("SP", qkv_scr[4, (fc - 16) * 128:(fc - 15) * 128, tsl], stg[sg][:],
                      reads=(f"stg{sg}",), writes=("qkv_scr",))

    kct = p.sbuf("kct", [128, NCH * 64], BF16)
    kst = p.sbuf("kst", [128, NCH * 64], BF16)
    vt = p.sbuf("vt", [128, NCH * 64], BF16)
    kvt_scr = p.dram("kvt_scr", [NH, 3, 128, NCH * 64], BF16)
    pfx_scr = p.dram("pfx_scr", [NH, NCH, 64, 130], BF16)
    ps_st = p.psum("ps_st", [64, 130], F32)
    st_sb = p.sbuf("st_sb", [64, 130], F32)
    psb_mm = [ps_mm[k].bitcast(BF16) for k in range(2)]
    hq = [p.sbuf(f"hq{j}", [64, TL], BF16) for j in range(4)]
    for h in range(NH):
        ci, po = h // 2, (h % 2) * 64
        for j, scr_i in enumerate((2, 3, 4)):
            p.dma("SP", hq[j][:], qkv_scr[scr_i, h * 64:(h + 1) * 64, :],
                  reads=("qkv_scr",), writes=(f"hq{j}",))
        for src_j, dst, dkey in ((0, kct, "kct"), (1, kst, "kst"), (2, vt, "vt")):
            for ch in range(NCH):
                k = ch % 2
                p.op("PE", (lambda e, src_j=src_j, ch=ch, k=k: e.transpose(
                        psb_mm[k][0:128, 0:64], hq[src_j][:, ch * CH:(ch + 1) * CH],
                        identb[0:64, 0:64])),
                     reads=(f"hq{src_j}", "identb"), writes=(f"ps_mm{k}",))
                p.op("DVE", (lambda e, dst=dst, ch=ch, k=k: e.tensor_copy(
                        dst[:, ch * 64:(ch + 1) * 64], psb_mm[k][0:128, 0:64])),
                     reads=(f"ps_mm{k}",), writes=(dkey,))
        p.dma("SP", kvt_scr[h, 0], kct[:], reads=("kct",), writes=("kvt_scr",))
        p.dma("SP", kvt_scr[h, 1], kst[:], reads=("kst",), writes=("kvt_scr",))
        p.dma("SP", kvt_scr[h, 2], vt[:], reads=("vt",), writes=("kvt_scr",))
        p.op("DVE", lambda e: e.memset(st_sb[:], 0.0), writes=("st_sb",))
        for ch in range(NCH):
            p.dma("POOL", pfx_scr[h, ch], st_sb[:], reads=("st_sb",), writes=("pfx_scr",))
            csl = slice(ch * 64, (ch + 1) * 64)
            p.op("PE", (lambda e, csl=csl: e.matmul(ps_st[0:64, 0:64], kct[:, csl], vt[:, csl],
                                                    start=True, stop=False)),
                 reads=("kct", "vt"), writes=("ps_st",))
            p.op("PE", (lambda e, csl=csl: e.matmul(ps_st[0:64, 64:128], kst[:, csl], vt[:, csl],
                                                    start=False, stop=False)),
                 reads=("kst", "vt"), writes=("ps_st",))
            p.op("PE", (lambda e, csl=csl: e.matmul(ps_st[0:64, 128:129], kct[:, csl], onesb[:],
                                                    start=False, stop=False)),
                 reads=("kct", "onesb"), writes=("ps_st",))
            p.op("PE", (lambda e, csl=csl: e.matmul(ps_st[0:64, 129:130], kst[:, csl], onesb[:],
                                                    start=False, stop=True)),
                 reads=("kst", "onesb"), writes=("ps_st",))
            p.op("DVE", lambda e: e.tensor_add(st_sb[:], st_sb[:], ps_st[:]),
                 reads=("st_sb", "ps_st"), writes=("st_sb",))
        p.dma("SP", st_out.ap()[h * 64:(h + 1) * 64, :], st_sb[:], reads=("st_sb",), writes=("st_out",))

    p.collective("AllGather", ALU.bypass, [[0, 1], [2, 3], [4, 5], [6, 7]],
                 [st_out.ap()], [st_g.ap()], reads=("st_out",), writes=("st_g",))

    att = [p.sbuf(f"att{i}", [128, TL], BF16) for i in range(DC)]
    st0 = p.sbuf("st0", [64, 130], F32)
    stc = p.sbuf("stc", [64, 130], BF16)
    pfx = p.sbuf("pfx", [64, NCH * 130], BF16)
    wt_sb = p.sbuf("wt_sb", [128, CH], BF16)
    nr_row = p.sbuf("nr_row", [1, CH], F32)
    rb2 = p.sbuf("rb2", [64, CH], F32)
    ps_ctx = p.psum("ps_ctx", [128, 512], F32)
    for h in range(NH):
        ci, po = h // 2, (h % 2) * 64
        for j in range(4):
            p.dma("SP", hq[j][:], qkv_scr[j, h * 64:(h + 1) * 64, :],
                  reads=("qkv_scr",), writes=(f"hq{j}",))
        p.dma("SP", kct[:], kvt_scr[h, 0], reads=("kvt_scr",), writes=("kct",))
        p.dma("SP", kst[:], kvt_scr[h, 1], reads=("kvt_scr",), writes=("kst",))
        p.dma("SP", vt[:], kvt_scr[h, 2], reads=("kvt_scr",), writes=("vt",))
        pfx_ap = bass.AP(tensor=pfx_scr.ap().tensor, offset=h * (NCH * 64 * 130),
                         ap=[[130, 64], [64 * 130, NCH], [1, 130]])
        p.dma("SP", pfx[:], pfx_ap, reads=("pfx_scr",), writes=("pfx",))
        p.dma("SP", st0[:], st_g.ap()[h * 64:(h + 1) * 64, :], reads=("st_g",), writes=("st0",))
        p.op("DVE", lambda e: e.tensor_scalar_mul(st0[:], st0[:], moffc[:]),
             reads=("st0", "moffc"), writes=("st0",))
        for ch in range(NCH):
            csl = slice(ch * CH, (ch + 1) * CH)
            k = ch % 2
            p.op("DVE", (lambda e, ch=ch: e.tensor_add(stc[:], pfx[:, ch * 130:(ch + 1) * 130], st0[:])),
                 reads=("pfx", "st0"), writes=("stc",))
            p.op("PE", (lambda e, k=k, ci=ci, po=po, ch=ch, csl=csl: e.matmul(
                    ps_mm[k][0:128, 0:CH], hq[2][:, csl], hq[0][:, csl],
                    start=True, stop=False)),
                 reads=("hq2", "hq0"), writes=(f"ps_mm{k}",))
            p.op("PE", (lambda e, k=k, ci=ci, po=po, csl=csl: e.matmul(
                    ps_mm[k][0:128, 0:CH], hq[3][:, csl], hq[1][:, csl],
                    start=False, stop=True)),
                 reads=("hq3", "hq1"), writes=(f"ps_mm{k}",))
            p.op("DVE", (lambda e, k=k: e.tensor_copy(wt_sb[:], ps_mm[k][0:128, 0:CH])),
                 reads=(f"ps_mm{k}",), writes=("wt_sb",))
            p.op("POOL", lambda e: e.affine_select(out=wt_sb[:], in_=wt_sb[:], pattern=[[1, CH]],
                                                   compare_op=ALU.is_ge, fill=0.0, base=0,
                                                   channel_multiplier=-1),
                 reads=("wt_sb",), writes=("wt_sb",))
            p.op("PE", (lambda e, ch=ch: e.matmul(ps_ctx[0:64, 0:CH], vt[:, ch * 64:(ch + 1) * 64],
                                                  wt_sb[:], start=True, stop=False)),
                 reads=("vt", "wt_sb"), writes=("ps_ctx",))
            p.op("PE", (lambda e, ci=ci, po=po, csl=csl: e.matmul(
                    ps_ctx[0:64, 0:CH], stc[:, 0:64], hq[0][:, csl], start=False, stop=False)),
                 reads=("stc", "hq0"), writes=("ps_ctx",))
            p.op("PE", (lambda e, ci=ci, po=po, csl=csl: e.matmul(
                    ps_ctx[0:64, 0:CH], stc[:, 64:128], hq[1][:, csl], start=False, stop=True)),
                 reads=("stc", "hq1"), writes=("ps_ctx",))
            p.op("PE", (lambda e: e.matmul(ps_ctx[0:1, 128:128 + CH], onesb[:], wt_sb[:],
                                           start=True, stop=False)),
                 reads=("onesb", "wt_sb"), writes=("ps_ctx",))
            p.op("PE", (lambda e, ci=ci, po=po, csl=csl: e.matmul(
                    ps_ctx[0:1, 128:128 + CH], stc[:, 128:129], hq[0][:, csl],
                    start=False, stop=False)),
                 reads=("stc", "hq0"), writes=("ps_ctx",))
            p.op("PE", (lambda e, ci=ci, po=po, csl=csl: e.matmul(
                    ps_ctx[0:1, 128:128 + CH], stc[:, 129:130], hq[1][:, csl],
                    start=False, stop=True)),
                 reads=("stc", "hq1"), writes=("ps_ctx",))
            p.op("DVE", lambda e: e.tensor_scalar_add(nr_row[:], ps_ctx[0:1, 128:128 + CH], 1e-6),
                 reads=("ps_ctx",), writes=("nr_row",))
            p.op("DVE", lambda e: e.reciprocal(nr_row[:], nr_row[:]), reads=("nr_row",), writes=("nr_row",))
            p.op("PE", (lambda e: e.matmul(ps_ctx[0:64, 256:256 + CH], c.ones_row_f[0:1, 0:64], nr_row[:],
                                           start=True, stop=True)),
                 reads=("c_ones_row_f", "nr_row"), writes=("ps_ctx",))
            p.op("DVE", lambda e: e.tensor_copy(rb2[:], ps_ctx[0:64, 256:256 + CH]),
                 reads=("ps_ctx",), writes=("rb2",))
            p.op("DVE", (lambda e, ci=ci, po=po, csl=csl: e.tensor_mul(att[ci][po:po + 64, csl],
                                                                      ps_ctx[0:64, 0:CH], rb2[:])),
                 reads=("ps_ctx", "rb2"), writes=(f"att{ci}",))

    bo_sb = p.sbuf("bo_sb", [128, DC], F32)
    p.dma("SP", bo_sb[:], boutd, writes=("bo_sb",))
    wo_sb = [p.sbuf(f"wo_sb{k}", [128, DC * 128], BF16) for k in range(2)]
    hev = [p.sbuf(f"hev{k}", [128, TT], F32) for k in range(2)]
    for oi in range(DC):
        k = oi % 2
        load_wchunks(p, "POOL", wo_sb[k][:], f"wo_sb{k}", wout, D, 0, DC, oi * 128, 128)
        for tt in range(NTT):
            tsl = slice(tt * TT, (tt + 1) * TT)
            for i in range(DC):
                p.op("PE", (lambda e, k=k, i=i, tt=tt: e.matmul(ps_mm[k][:], wo_sb[k][:, i * 128:(i + 1) * 128],
                                                                att[i][:, tt * TT:(tt + 1) * TT],
                                                                start=(i == 0), stop=(i == DC - 1))),
                     reads=(f"wo_sb{k}", f"att{i}"), writes=(f"ps_mm{k}",))
            p.dma("SP", zt[k][:], z_scr[oi * 128:(oi + 1) * 128, tsl],
                  reads=("z_scr",), writes=(f"zt{k}",))
            p.op("DVE", (lambda e, k=k, oi=oi: e.tensor_scalar_add(hev[k][:], ps_mm[k][:],
                                                                   bo_sb[:, oi:oi + 1])),
                 reads=(f"ps_mm{k}", "bo_sb"), writes=(f"hev{k}",))
            p.op("DVE", (lambda e, k=k: e.tensor_add(hev[k][:], hev[k][:], zt[k][:])),
                 reads=(f"hev{k}", f"zt{k}"), writes=(f"hev{k}",))
            p.dma("SP", h_scr[oi * 128:(oi + 1) * 128, tsl], hev[k][:],
                  reads=(f"hev{k}",), writes=("h_scr",))

    mkeys = [f"m{i}" for i in range(DC)]
    hbkeys = [f"hb{fi}" for fi in range(32)]
    p.barrier([f"arm{i}" for i in range(DC)] + [f"att{i}" for i in range(DC)] + mkeys + hbkeys)
    m = arm
    for tt in range(NTT):
        emit_ln_tile(p, c, ("hbm", h_scr), [(m[i], f"m{i}") for i in range(DC)], tt, "rm2",
                     kind="rms", eps=1e-6, gain=(n2_sb, "n2_sb", 0), out_off=tt * TT)
    b1_sb = p.sbuf("b1_sb", [128, 32], F32)
    b2_sb = p.sbuf("b2_sb", [128, DC], F32)
    p.dma("SP", b1_sb[:], b1d, writes=("b1_sb",))
    p.dma("SP", b2_sb[:], b2d, writes=("b2_sb",))
    w1_sb = [p.sbuf(f"w1_sb{k}", [128, DC * 128], BF16) for k in range(2)]
    w2_sb = [p.sbuf(f"w2_sb{k}", [128, 32 * 128], BF16) for k in range(2)]
    hb = att
    xre = [p.sbuf(f"xre{k}", [128, TT], F32) for k in range(2)]
    hre = [p.sbuf(f"hre{k}", [128, TT], F32) for k in range(2)]
    outs = []
    for tt in range(NTT):
        tsl = slice(tt * TT, (tt + 1) * TT)
        for fi in range(32):
            k = fi % 2
            load_wchunks(p, "POOL", w1_sb[k][:], f"w1_sb{k}", w1, 4 * D, 0, DC, fi * 128, 128)
            for i in range(DC):
                p.op("PE", (lambda e, k=k, i=i, tt=tt: e.matmul(ps_mm[k][:], w1_sb[k][:, i * 128:(i + 1) * 128],
                                                                m[i][:, tt * TT:(tt + 1) * TT],
                                                                start=(i == 0), stop=(i == DC - 1))),
                     reads=(f"w1_sb{k}", f"m{i}"), writes=(f"ps_mm{k}",))
            p.op("ACT", (lambda e, k=k, fi=fi: e.activation(hb[fi // 4][:, (fi % 4) * TT:(fi % 4 + 1) * TT],
                                                            ps_mm[k][:], AF.Gelu_apprx_tanh,
                                                            bias=b1_sb[:, fi:fi + 1])),
                 reads=(f"ps_mm{k}", "b1_sb"), writes=(f"hb{fi}",))
        for oi in range(DC):
            k = oi % 2
            load_wchunks(p, "POOL", w2_sb[k][:], f"w2_sb{k}", w2, D, 0, 32, oi * 128, 128)
            for fi in range(32):
                p.op("PE", (lambda e, k=k, fi=fi: e.matmul(
                        ps_mm[k][:], w2_sb[k][:, fi * 128:(fi + 1) * 128],
                        hb[fi // 4][:, (fi % 4) * TT:(fi % 4 + 1) * TT],
                        start=(fi == 0), stop=(fi == 31))),
                     reads=(f"w2_sb{k}", f"hb{fi}"), writes=(f"ps_mm{k}",))
            p.dma("SP", xre[k][:], x_in[oi * 128:(oi + 1) * 128, tsl], writes=(f"xre{k}",))
            p.dma("SP", hre[k][:], h_scr[oi * 128:(oi + 1) * 128, tsl], reads=("h_scr",), writes=(f"hre{k}",))
            p.op("DVE", (lambda e, k=k, oi=oi: e.tensor_scalar_add(hre[k][:], hre[k][:],
                                                                   b2_sb[:, oi:oi + 1])),
                 reads=(f"hre{k}", "b2_sb"), writes=(f"hre{k}",))
            p.op("DVE", (lambda e, k=k: e.tensor_add(xre[k][:], xre[k][:], hre[k][:])),
                 reads=(f"xre{k}", f"hre{k}"), writes=(f"xre{k}",))
            p.op("DVE", (lambda e, k=k: e.tensor_add(xre[k][:], xre[k][:], ps_mm[k][:])),
                 reads=(f"xre{k}", f"ps_mm{k}"), writes=(f"xre{k}",))
            p.dma("SP", x_out[oi * 128:(oi + 1) * 128, tsl], xre[k][:],
                  reads=(f"xre{k}",), writes=(f"xo_{oi}_{tt}",))
            outs.append(f"xo_{oi}_{tt}")
    p.final_wait(outs)
    return p


def build_l1attn_launch(p=None):
    p = p or Prog("l1attn")
    c = Consts(p)
    x_in = p.dram("x_in", [D, TL], F32, kind="ExternalInput").ap()
    wkt = p.dram("wkt", [D, D], F32, kind="ExternalInput").ap()
    wvt = p.dram("wvt", [D, D], F32, kind="ExternalInput").ap()
    wrt = p.dram("wrt", [D, D], F32, kind="ExternalInput").ap()
    wot = p.dram("wot", [D, D], F32, kind="ExternalInput").ap()
    tdc = p.dram("tdc", [128, DC], F32, kind="ExternalInput").ap()
    tfc = p.dram("tfc", [128, DC], F32, kind="ExternalInput").ap()
    mkc = p.dram("mkc", [128, DC], F32, kind="ExternalInput").ap()
    mvc = p.dram("mvc", [128, DC], F32, kind="ExternalInput").ap()
    mrc = p.dram("mrc", [128, DC], F32, kind="ExternalInput").ap()
    moff = p.dram("moff", [128, 1], F32, kind="ExternalInput").ap()
    x_out = p.dram("x_out", [D, TL], F32, kind="ExternalOutput").ap()
    bnd_out = p.dram("bnd_out", [128, DC], F32)
    bnd_g = p.dram("bnd_g", [256, DC], F32)
    term_out = p.dram("term_out", [128, 2 * DC], F32)
    term_g = p.dram("term_g", [256, 2 * DC], F32)

    moffc = p.sbuf("moffc", [128, 1], F32)
    p.dma("SP", moffc[:], moff, writes=("moffc",))
    cols = {}
    for nm, dr in (("td", tdc), ("tf", tfc), ("mk", mkc), ("mv", mvc), ("mr", mrc)):
        t = p.sbuf(f"col_{nm}", [128, DC], F32)
        p.dma("SP", t[:], dr, writes=(f"col_{nm}",))
        cols[nm] = t
    ew = p.sbuf("col_ew", [128, DC], F32)
    eu = p.sbuf("col_eu", [128, DC], F32)
    p.op("ACT", lambda e: e.activation(ew[:], cols["td"][:], AF.Exp), reads=("col_td",), writes=("col_ew",))
    p.op("ACT", lambda e: e.activation(ew[:], ew[:], AF.Exp, scale=-1.0), reads=("col_ew",), writes=("col_ew",))
    p.op("ACT", lambda e: e.activation(eu[:], cols["tf"][:], AF.Exp), reads=("col_tf",), writes=("col_eu",))

    wcb = [p.sbuf(f"wcb{k}", [128, DC * 128], BF16) for k in range(2)]

    def load_wcb(dr, ei, k, key):
        load_wchunks(p, "POOL", wcb[k][:], key, dr, D, 0, DC, ei * 128, 128)

    z = [p.sbuf(f"z{i}", [128, TL], BF16) for i in range(DC)]
    zkeys = [(z[i], f"z{i}") for i in range(DC)]
    for tt in range(NTT):
        emit_ln_tile(p, c, ("hbm", x_in), zkeys, tt, "ln", out_off=tt * TT)
    zb_sb = p.sbuf("zb_sb", [128, DC], F32)
    for i in range(DC):
        p.op("DVE", (lambda e, i=i: e.tensor_copy(zb_sb[:, i:i + 1], z[i][:, TL - 1:TL])),
             reads=(f"z{i}",), writes=("zb_sb",))
    p.dma("SP", bnd_out.ap(), zb_sb[:], reads=("zb_sb",), writes=("bnd_out",))
    p.collective("AllGather", ALU.bypass, [[0, 1], [2, 3], [4, 5], [6, 7]],
                 [bnd_out.ap()], [bnd_g.ap()], reads=("bnd_out",), writes=("bnd_g",))
    zbi = p.sbuf("zbi", [128, DC], F32)
    p.dma("SP", zbi[:], bnd_g.ap()[0:128, :], reads=("bnd_g",), writes=("zbi",))
    p.op("DVE", lambda e: e.tensor_scalar_mul(zbi[:], zbi[:], moffc[:]),
         reads=("zbi", "moffc"), writes=("zbi",))

    zs = [p.sbuf(f"zs{i}", [128, TL], BF16) for i in range(DC)]
    xr = [p.sbuf(f"xr{i}", [128, TL], BF16) for i in range(DC)]
    dsc = p.sbuf("dsc", [128, TT], F32)
    zsc = p.sbuf("zsc", [128, TT], F32)
    for i in range(DC):
        p.op("DVE", (lambda e, i=i: e.tensor_copy(zs[i][:, 1:TL], z[i][:, 0:TL - 1])),
             reads=(f"z{i}",), writes=(f"zs{i}",))
        p.op("DVE", (lambda e, i=i: e.tensor_copy(zs[i][:, 0:1], zbi[:, i:i + 1])),
             reads=("zbi", f"zs{i}"), writes=(f"zs{i}",))
    for i in range(DC):
        for tt in range(NTT):
            tsl = slice(tt * TT, (tt + 1) * TT)
            p.op("DVE", (lambda e, i=i, tsl=tsl: e.tensor_copy(zsc[:], zs[i][:, tsl])),
                 reads=(f"zs{i}",), writes=("zsc",))
            p.op("DVE", (lambda e, i=i, tsl=tsl: e.tensor_sub(dsc[:], z[i][:, tsl], zsc[:])),
                 reads=(f"z{i}", "zsc"), writes=("dsc",))
            p.op("DVE", (lambda e, i=i, tsl=tsl: e.scalar_tensor_tensor(
                    z[i][:, tsl], dsc[:], cols["mk"][:, i:i + 1], zsc[:], ALU.mult, ALU.add)),
                 reads=("dsc", "col_mk", "zsc"), writes=(f"z{i}",))
            p.op("DVE", (lambda e, i=i, tsl=tsl: e.scalar_tensor_tensor(
                    xr[i][:, tsl], dsc[:], cols["mr"][:, i:i + 1], zsc[:], ALU.mult, ALU.add)),
                 reads=("dsc", "col_mr", "zsc"), writes=(f"xr{i}",))
            p.op("DVE", (lambda e, i=i, tsl=tsl: e.scalar_tensor_tensor(
                    zs[i][:, tsl], dsc[:], cols["mv"][:, i:i + 1], zsc[:], ALU.mult, ALU.add)),
                 reads=("dsc", "col_mv", "zsc"), writes=(f"zs{i}",))
    xk, xv = z, zs

    kv_scr = p.dram("kv_scr", [2, D, TL], BF16)
    ps_mm = [p.psum(f"ps_mm{k}", [128, TT], F32) for k in range(2)]
    ke = p.sbuf("ke", [128, TL], BF16)
    ve = p.sbuf("ve_", [128, TL], BF16)
    kxp = p.sbuf("kxp", [128, TL], BF16)
    zz = p.sbuf("zz", [128, TL], BF16)
    ab = p.sbuf("ab", [128, TL], F32)
    bb = p.sbuf("bb", [128, TL], F32)
    term_sb = p.sbuf("term_sb", [128, DC * 2], F32)
    zero_col = p.sbuf("zero_col", [128, 1], F32)
    p.op("DVE", lambda e: e.memset(zero_col[:], 0.0), writes=("zero_col",))
    for ei in range(DC):
        load_wcb(wkt, ei, 0, "wcb0")
        load_wcb(wvt, ei, 1, "wcb1")
        for tt in range(NTT):
            k = tt % 2
            tsl = slice(tt * TT, (tt + 1) * TT)
            for i in range(DC):
                p.op("PE", (lambda e, k=k, i=i, tsl=tsl: e.matmul(
                        ps_mm[k][:], wcb[0][:, i * 128:(i + 1) * 128],
                        xk[i][:, tsl], start=(i == 0), stop=(i == DC - 1))),
                     reads=("wcb0", f"z{i}"), writes=(f"ps_mm{k}",))
            p.op("DVE", (lambda e, k=k, tsl=tsl: e.tensor_copy(ke[:, tsl], ps_mm[k][:])),
                 reads=(f"ps_mm{k}",), writes=("ke",))
            for i in range(DC):
                p.op("PE", (lambda e, k=k, i=i, tsl=tsl: e.matmul(
                        ps_mm[k][:], wcb[1][:, i * 128:(i + 1) * 128],
                        xv[i][:, tsl], start=(i == 0), stop=(i == DC - 1))),
                     reads=("wcb1", f"zs{i}"), writes=(f"ps_mm{k}",))
            p.op("DVE", (lambda e, k=k, tsl=tsl: e.tensor_copy(ve[:, tsl], ps_mm[k][:])),
                 reads=(f"ps_mm{k}",), writes=("ve_",))
        p.dma("SP", kv_scr[0, ei * 128:(ei + 1) * 128, :], ke[:], reads=("ke",), writes=("kv_scr",))
        p.dma("SP", kv_scr[1, ei * 128:(ei + 1) * 128, :], ve[:], reads=("ve_",), writes=("kv_scr",))
        p.op("ACT", lambda e: e.activation(kxp[:], ke[:], AF.Exp), reads=("ke",), writes=("kxp",))
        p.op("DVE", lambda e: e.tensor_mul(zz[:], ve[:], kxp[:]), reads=("ve_", "kxp"), writes=("zz",))
        ewb = bass.AP(tensor=ew.ap().tensor, offset=ei, ap=[[ew.ap().ap[0][0], 128], [0, TL]])
        p.op("DVE", (lambda e, ewb=ewb: e.tensor_tensor_scan(ab[:], ewb, zz[:], 0.0, ALU.mult, ALU.add)),
             reads=("col_ew", "zz"), writes=("ab",))
        p.op("DVE", (lambda e, ewb=ewb: e.tensor_tensor_scan(bb[:], ewb, kxp[:], 0.0, ALU.mult, ALU.add)),
             reads=("col_ew", "kxp"), writes=("bb",))
        p.op("DVE", (lambda e, ei=ei: e.tensor_copy(term_sb[:, 2 * ei:2 * ei + 1], ab[:, TL - 1:TL])),
             reads=("ab",), writes=("term_sb",))
        p.op("DVE", (lambda e, ei=ei: e.tensor_copy(term_sb[:, 2 * ei + 1:2 * ei + 2], bb[:, TL - 1:TL])),
             reads=("bb",), writes=("term_sb",))
    p.dma("SP", term_out.ap(), term_sb[:], reads=("term_sb",), writes=("term_out",))
    p.collective("AllGather", ALU.bypass, [[0, 1], [2, 3], [4, 5], [6, 7]],
                 [term_out.ap()], [term_g.ap()], reads=("term_out",), writes=("term_g",))
    term_in = p.sbuf("term_in", [128, DC * 2], F32)
    p.dma("SP", term_in[:], term_g.ap()[0:128, :], reads=("term_g",), writes=("term_in",))
    p.op("DVE", lambda e: e.tensor_scalar_mul(term_in[:], term_in[:], moffc[:]),
         reads=("term_in", "moffc"), writes=("term_in",))

    rwkv = [p.sbuf(f"rwkv{i}", [128, TL], BF16) for i in range(DC)]
    num = p.sbuf("num", [128, TL], F32)
    sr = p.sbuf("sr", [128, TL], BF16)
    for ei in range(DC):
        p.dma("SP", ke[:], kv_scr[0, ei * 128:(ei + 1) * 128, :], reads=("kv_scr",), writes=("ke",))
        p.dma("SP", ve[:], kv_scr[1, ei * 128:(ei + 1) * 128, :], reads=("kv_scr",), writes=("ve_",))
        p.op("ACT", lambda e: e.activation(kxp[:], ke[:], AF.Exp), reads=("ke",), writes=("kxp",))
        p.op("DVE", lambda e: e.tensor_mul(zz[:], ve[:], kxp[:]), reads=("ve_", "kxp"), writes=("zz",))
        ewb = bass.AP(tensor=ew.ap().tensor, offset=ei, ap=[[ew.ap().ap[0][0], 128], [0, TL]])
        p.op("DVE", (lambda e, ewb=ewb, ei=ei: e.tensor_tensor_scan(
                ab[:], ewb, zz[:], term_in[:, 2 * ei:2 * ei + 1], ALU.mult, ALU.add)),
             reads=("col_ew", "zz", "term_in"), writes=("ab",))
        p.op("DVE", (lambda e, ewb=ewb, ei=ei: e.tensor_tensor_scan(
                bb[:], ewb, kxp[:], term_in[:, 2 * ei + 1:2 * ei + 2], ALU.mult, ALU.add)),
             reads=("col_ew", "kxp", "term_in"), writes=("bb",))
        p.op("DVE", (lambda e, ei=ei: e.scalar_tensor_tensor(num[:, 1:TL], zz[:, 1:TL],
                                                             eu[:, ei:ei + 1], ab[:, 0:TL - 1],
                                                             ALU.mult, ALU.add)),
             reads=("zz", "col_eu", "ab"), writes=("num",))
        p.op("DVE", (lambda e, ei=ei: e.scalar_tensor_tensor(num[:, 0:1], zz[:, 0:1],
                                                             eu[:, ei:ei + 1], term_in[:, 2 * ei:2 * ei + 1],
                                                             ALU.mult, ALU.add)),
             reads=("zz", "col_eu", "term_in", "num"), writes=("num",))
        p.op("DVE", (lambda e, ei=ei: e.scalar_tensor_tensor(ab[:, 1:TL], kxp[:, 1:TL],
                                                             eu[:, ei:ei + 1], bb[:, 0:TL - 1],
                                                             ALU.mult, ALU.add)),
             reads=("kxp", "col_eu", "bb", "num"), writes=("ab",))
        p.op("DVE", (lambda e, ei=ei: e.scalar_tensor_tensor(ab[:, 0:1], kxp[:, 0:1],
                                                             eu[:, ei:ei + 1], term_in[:, 2 * ei + 1:2 * ei + 2],
                                                             ALU.mult, ALU.add)),
             reads=("kxp", "col_eu", "term_in", "ab"), writes=("ab",))
        p.op("DVE", lambda e: e.reciprocal(ab[:], ab[:]), reads=("ab",), writes=("ab",))
        p.op("DVE", lambda e: e.tensor_mul(num[:], num[:], ab[:]), reads=("num", "ab"), writes=("num",))
        load_wcb(wrt, ei, 0, "wcb0")
        for tt in range(NTT):
            k = tt % 2
            tsl = slice(tt * TT, (tt + 1) * TT)
            for i in range(DC):
                p.op("PE", (lambda e, k=k, i=i, tsl=tsl: e.matmul(
                        ps_mm[k][:], wcb[0][:, i * 128:(i + 1) * 128],
                        xr[i][:, tsl], start=(i == 0), stop=(i == DC - 1))),
                     reads=("wcb0", f"xr{i}"), writes=(f"ps_mm{k}",))
            p.op("ACT", (lambda e, k=k, tsl=tsl: e.activation(sr[:, tsl], ps_mm[k][:], AF.Sigmoid)),
                 reads=(f"ps_mm{k}",), writes=("sr",))
        p.op("DVE", (lambda e, ei=ei: e.tensor_mul(rwkv[ei][:], num[:], sr[:])),
             reads=("num", "sr"), writes=(f"rwkv{ei}",))

    xre = [p.sbuf(f"xre{k}", [128, TT], F32) for k in range(2)]
    outs = []
    for oi in range(DC):
        load_wcb(wot, oi, oi % 2, f"wcb{oi % 2}")
        for tt in range(NTT):
            k = (oi * NTT + tt) % 2
            tsl = slice(tt * TT, (tt + 1) * TT)
            for i in range(DC):
                p.op("PE", (lambda e, k=k, i=i, oi=oi, tsl=tsl: e.matmul(
                        ps_mm[k][:], wcb[oi % 2][:, i * 128:(i + 1) * 128],
                        rwkv[i][:, tsl], start=(i == 0), stop=(i == DC - 1))),
                     reads=(f"wcb{oi % 2}", f"rwkv{i}"), writes=(f"ps_mm{k}",))
            p.dma("SP", xre[k][:], x_in[oi * 128:(oi + 1) * 128, tsl], writes=(f"xre{k}",))
            p.op("DVE", (lambda e, k=k: e.tensor_add(xre[k][:], xre[k][:], ps_mm[k][:])),
                 reads=(f"xre{k}", f"ps_mm{k}"), writes=(f"xre{k}",))
            p.dma("SP", x_out[oi * 128:(oi + 1) * 128, tsl], xre[k][:],
                  reads=(f"xre{k}",), writes=(f"xo_{oi}_{tt}",))
            outs.append(f"xo_{oi}_{tt}")
    p.final_wait(outs)
    return p


def build_l2ffn_inner(first, p=None):
    p = p or Prog("l2f" + ("a" if first else "b"))
    c = Consts(p)
    NHH = 8
    W_ = 512
    NW = TL // W_
    HID = 2730
    NFC = 22
    x_in = p.dram("x_in", [D, TL], F32, kind="ExternalInput").ap()
    if not first:
        y_in = p.dram("y_in", [D, TL], F32, kind="ExternalInput").ap()
    wqkv3 = p.dram("wqkv", [D, 8 * 320], F32, kind="ExternalInput").ap()
    tabs = p.dram("tabs", [4, 64, TL], F32, kind="ExternalInput").ap()
    qsc = p.dram("qsc", [64, 2], F32, kind="ExternalInput").ap()
    ksc = p.dram("ksc", [64, 2], F32, kind="ExternalInput").ap()
    wo = p.dram("wo", [512, D], F32, kind="ExternalInput").ap()
    lng = p.dram("lng", [128, DC], F32, kind="ExternalInput").ap()
    lnb = p.dram("lnb", [128, DC], F32, kind="ExternalInput").ap()
    w1 = p.dram("w1", [D, 5460], F32, kind="ExternalInput").ap()
    b1a = p.dram("b1a", [128, NFC], F32, kind="ExternalInput").ap()
    b1g = p.dram("b1g", [128, NFC], F32, kind="ExternalInput").ap()
    w2 = p.dram("w2", [HID, D], F32, kind="ExternalInput").ap()
    b2d = p.dram("b2d", [128, DC], F32, kind="ExternalInput").ap()
    moff = p.dram("moff", [128, 1], F32, kind="ExternalInput").ap()
    y_out = p.dram("y_out", [D, TL], F32, kind="ExternalOutput").ap()
    if not first:
        x_out = p.dram("x_out", [D, TL], F32, kind="ExternalOutput").ap()
    y_scr = p.dram("y_scr", [D, TL], F32)
    y1_scr = p.dram("y1_scr", [D, TL], F32)
    qkv_scr = p.dram("qkv_scr", [3, 512, TL], BF16)
    bnd1_o = p.dram("bnd1_o", [128, 4], F32)
    bnd1_g = p.dram("bnd1_g", [256, 4], F32)
    kvw_o = p.dram("kvw_o", [1024, W_], BF16)
    kvw_g = p.dram("kvw_g", [2048, W_], BF16)
    bnd2_o = p.dram("bnd2_o", [128, 4], F32)
    bnd2_g = p.dram("bnd2_g", [256, 4], F32)

    onesb = p.sbuf("onesb", [128, 1], BF16)
    p.op("DVE", lambda e: e.memset(onesb[:], 1.0), writes=("onesb",))
    onesrb = p.sbuf("onesrb", [1, 128], BF16)
    p.op("DVE", lambda e: e.memset(onesrb[:], 1.0), writes=("onesrb",))
    identb = p.sbuf("identb", [128, 128], BF16)
    p.op("POOL", lambda e: e.memset(identb[:], 1.0), writes=("identb",))
    p.op("POOL", lambda e: e.affine_select(out=identb[:], in_=identb[:], pattern=[[-1, 128]],
                                           compare_op=ALU.is_equal, fill=0.0, base=0,
                                           channel_multiplier=1),
         reads=("identb",), writes=("identb",))
    moffc = p.sbuf("moffc", [128, 1], F32)
    p.dma("SP", moffc[:], moff, writes=("moffc",))
    qs_sb = p.sbuf("qs_sb", [64, 2], F32)
    ks_sb = p.sbuf("ks_sb", [64, 2], F32)
    p.dma("SP", qs_sb[:], qsc, writes=("qs_sb",))
    p.dma("SP", ks_sb[:], ksc, writes=("ks_sb",))

    if first:
        yt = [p.sbuf(f"yt{i}", [128, TT], F32) for i in range(DC)]
        for tt in range(NTT):
            emit_ln_tile(p, c, ("hbm", x_in), [(yt[i], f"yt{i}") for i in range(DC)], tt, "ln",
                         out_off=0, n_tok_tile=TT)
            for i in range(DC):
                p.dma("SP", y_scr[i * 128:(i + 1) * 128, tt * TT:(tt + 1) * TT], yt[i][:],
                      reads=(f"yt{i}",), writes=("y_scr",))
        ysrc = y_scr
    else:
        ysrc = y_in

    bsc = p.sbuf("bsc", [128, 4], F32)
    for i in range(4):
        p.dma("SP", bsc[:, i:i + 1], ysrc[(4 + i) * 128:(5 + i) * 128, TL - 1:TL],
              reads=("y_scr",) if first else (), writes=("bsc",), allow_slow_non_contiguous=True)
    p.dma("SP", bnd1_o.ap(), bsc[:], reads=("bsc",), writes=("bnd1_o",))
    p.collective("AllGather", ALU.bypass, [[0, 1], [2, 3], [4, 5], [6, 7]],
                 [bnd1_o.ap()], [bnd1_g.ap()], reads=("bnd1_o",), writes=("bnd1_g",))
    bin1 = p.sbuf("bin1", [128, 4], F32)
    p.dma("SP", bin1[:], bnd1_g.ap()[0:128, :], reads=("bnd1_g",), writes=("bin1",))
    p.op("DVE", lambda e: e.tensor_scalar_mul(bin1[:], bin1[:], moffc[:]),
         reads=("bin1", "moffc"), writes=("bin1",))

    ysh = [p.sbuf(f"ysh{i}", [128, TL], BF16) for i in range(DC)]
    ytmp = p.sbuf("ytmp", [128, TT], F32)

    def build_shifted(src_dr, src_key, bcol, bkey, dst, dkey, i):
        sh = 0 if i < 4 else 1
        for tt in range(NTT):
            if sh == 0:
                p.dma("SP", ytmp[:], src_dr[i * 128:(i + 1) * 128, tt * TT:(tt + 1) * TT],
                      reads=src_key, writes=("ytmp",))
                p.op("DVE", (lambda e, dst=dst, tt=tt: e.tensor_copy(dst[:, tt * TT:(tt + 1) * TT], ytmp[:])),
                     reads=("ytmp",), writes=(dkey,))
            elif tt == 0:
                p.dma("SP", ytmp[:, 0:TT - 1], src_dr[i * 128:(i + 1) * 128, 0:TT - 1],
                      reads=src_key, writes=("ytmp",))
                p.op("DVE", (lambda e, dst=dst: e.tensor_copy(dst[:, 1:TT], ytmp[:, 0:TT - 1])),
                     reads=("ytmp",), writes=(dkey,))
                p.op("DVE", (lambda e, dst=dst, bcol=bcol, i=i: e.tensor_copy(dst[:, 0:1],
                                                                              bcol[:, i - 4:i - 3])),
                     reads=(bkey, dkey), writes=(dkey,))
            else:
                p.dma("SP", ytmp[:], src_dr[i * 128:(i + 1) * 128, tt * TT - 1:(tt + 1) * TT - 1],
                      reads=src_key, writes=("ytmp",))
                p.op("DVE", (lambda e, dst=dst, tt=tt: e.tensor_copy(dst[:, tt * TT:(tt + 1) * TT], ytmp[:])),
                     reads=("ytmp",), writes=(dkey,))

    for i in range(DC):
        build_shifted(ysrc, ("y_scr",) if first else (), bin1, "bin1", ysh[i], f"ysh{i}", i)

    ps_mm = [p.psum(f"ps_mm{k}", [128, TT], F32) for k in range(2)]
    psb_mm = [ps_mm[k].bitcast(BF16) for k in range(2)]
    yev = [p.sbuf(f"yev{k}", [128, TT], F32) for k in range(2)]
    p.push_scope()
    tab_sb = [p.sbuf(f"tab{j}", [64, TL], BF16) for j in range(4)]
    for j in range(4):
        p.dma("POOL", tab_sb[j][:], tabs[j], writes=(f"tab{j}",))
    wq_sb = [p.sbuf(f"wq_sb{k}", [128, DC * 128], BF16) for k in range(2)]
    prj = [p.sbuf(f"prj{j}", [64, TL], BF16) for j in range(3)]
    n2r = p.sbuf("n2r", [1, TL], F32)
    rn_row = p.sbuf("rn_row", [1, TL], F32)
    qh = p.sbuf("qh", [64, TL], BF16)
    qph = p.sbuf("qph", [64, TL], BF16)
    qrot = p.sbuf("qrot", [64, TL], BF16)
    krot = p.sbuf("krot", [64, TL], BF16)
    ts1 = p.sbuf("ts1", [64, TL], BF16)
    sqt = p.sbuf("sqt", [64, TL], BF16)
    kmax = p.sbuf("kmax", [1, 1], F32)
    ps_r = c.ps_b
    def project(h, grp, dsts):
        k = grp % 2
        ncol = 128 if grp < 2 else 64
        pr_ap = bass.AP(tensor=wqkv3.tensor, offset=wqkv3.offset + h * 320 + grp * 128,
                        ap=[[2560, 128], [128 * 2560, DC], [1, ncol]])
        p.dma("POOL", wq_sb[k][:, 0:DC * ncol], pr_ap, writes=(f"wq_sb{k}",))
        for tt in range(NTT):
            tsl = slice(tt * TT, (tt + 1) * TT)
            for i in range(DC):
                p.op("PE", (lambda e, k=k, i=i, tsl=tsl, ncol=ncol: e.matmul(
                        ps_mm[k][0:ncol, :], wq_sb[k][:, i * ncol:(i + 1) * ncol], ysh[i][:, tsl],
                        start=(i == 0), stop=(i == DC - 1))),
                     reads=(f"wq_sb{k}", f"ysh{i}"), writes=(f"ps_mm{k}",))
            for di, (dst, dkey, row0) in enumerate(dsts):
                p.op("DVE", (lambda e, dst=dst, k=k, tsl=tsl, row0=row0: e.tensor_copy(
                        dst[:, tsl], ps_mm[k][row0:row0 + 64, :])),
                     reads=(f"ps_mm{k}",), writes=(dkey,))

    def norms_of(srck, dst_row):
        p.op("ACT", (lambda e: e.activation(sqt[:], prj[0][:] if srck == "prj0" else prj[1][:], AF.Square)),
             reads=(srck,), writes=("sqt",))
        for tt in range(NTT):
            p.op("PE", (lambda e, tt=tt: e.matmul(c.ps_s[:], onesb[0:64, :],
                                                  sqt[:, tt * TT:(tt + 1) * TT], start=True, stop=True)),
                 reads=("onesb", "sqt"), writes=("ps_s",))
            p.op("ACT", (lambda e, tt=tt, dst_row=dst_row: e.activation(
                    (rn_row if dst_row == "rn_row" else n2r)[0:1, tt * TT:(tt + 1) * TT],
                    c.ps_s[:], AF.Sqrt)),
                 reads=("ps_s",), writes=(dst_row,))
        p.op("DVE", (lambda e, dst_row=dst_row: e.tensor_scalar_max(
                (rn_row if dst_row == "rn_row" else n2r)[:],
                (rn_row if dst_row == "rn_row" else n2r)[:], 1e-12)),
             reads=(dst_row,), writes=(dst_row,))
        p.op("DVE", (lambda e, dst_row=dst_row: e.reciprocal(
                (rn_row if dst_row == "rn_row" else n2r)[:],
                (rn_row if dst_row == "rn_row" else n2r)[:])),
             reads=(dst_row,), writes=(dst_row,))

    def scaled_hat(srct, srck, row, rowkey, scol, sckey, dst, dkey):
        for tt in range(NTT):
            tsl = slice(tt * TT, (tt + 1) * TT)
            p.op("PE", (lambda e, row=row, tsl=tsl: e.matmul(ps_r[0:64, :], c.ones_row_f[0:1, 0:64],
                                                             row[0:1, tsl], start=True, stop=True)),
                 reads=("c_ones_row_f", rowkey), writes=("ps_b",))
            p.op("DVE", (lambda e, srct=srct, scol=scol, dst=dst, tsl=tsl: e.scalar_tensor_tensor(
                    dst[:, tsl], srct[:, tsl], scol, ps_r[0:64, :], ALU.mult, ALU.mult)),
                 reads=(srck, sckey, "ps_b"), writes=(dkey,))

    for h in range(NHH):
        project(h, 0, [(prj[0], "prj0", 0), (prj[1], "prj1", 64)])
        norms_of("prj0", "rn_row")
        scaled_hat(prj[0], "prj0", rn_row, "rn_row", qs_sb[:, 0:1], "qs_sb", qh, "qh")
        scaled_hat(prj[1], "prj1", rn_row, "rn_row", qs_sb[:, 1:2], "qs_sb", qph, "qph")
        project(h, 1, [(prj[0], "prj0", 0), (prj[1], "prj1", 64)])
        norms_of("prj0", "n2r")
        scaled_hat(prj[0], "prj0", n2r, "n2r", ks_sb[:, 0:1], "ks_sb", ts1, "ts1")
        scaled_hat(prj[1], "prj1", n2r, "n2r", ks_sb[:, 1:2], "ks_sb", sqt, "sqt")
        project(h, 2, [(prj[2], "prj2", 0)])
        p.op("DVE", lambda e: e.tensor_mul(qrot[:], qh[:], tab_sb[0][:]),
             reads=("qh", "tab0"), writes=("qrot",))
        p.op("DVE", lambda e: e.tensor_mul(qph[:], qph[:], tab_sb[1][:]),
             reads=("qph", "tab1"), writes=("qph",))
        p.op("DVE", lambda e: e.tensor_add(qrot[:], qrot[:], qph[:]),
             reads=("qrot", "qph"), writes=("qrot",))
        p.op("DVE", lambda e: e.tensor_mul(krot[:], ts1[:], tab_sb[2][:]),
             reads=("ts1", "tab2"), writes=("krot",))
        p.op("DVE", lambda e: e.tensor_mul(sqt[:], sqt[:], tab_sb[3][:]),
             reads=("sqt", "tab3"), writes=("sqt",))
        p.op("DVE", lambda e: e.tensor_add(krot[:], krot[:], sqt[:]),
             reads=("krot", "sqt"), writes=("krot",))
        p.op("ACT", lambda e: e.activation(sqt[:], qh[:], AF.Square), reads=("qh",), writes=("sqt",))
        for tt in range(NTT):
            p.op("PE", (lambda e, tt=tt: e.matmul(c.ps_s[:], onesb[0:64, :],
                                                  sqt[:, tt * TT:(tt + 1) * TT], start=True, stop=True)),
                 reads=("onesb", "sqt"), writes=("ps_s",))
            p.op("ACT", (lambda e, tt=tt: e.activation(rn_row[0:1, tt * TT:(tt + 1) * TT],
                                                       c.ps_s[:], AF.Sqrt)),
                 reads=("ps_s",), writes=("rn_row",))
        p.op("ACT", lambda e: e.activation(sqt[:], ts1[:], AF.Square), reads=("ts1",), writes=("sqt",))
        for tt in range(NTT):
            p.op("PE", (lambda e, tt=tt: e.matmul(c.ps_s[:], onesb[0:64, :],
                                                  sqt[:, tt * TT:(tt + 1) * TT], start=True, stop=True)),
                 reads=("onesb", "sqt"), writes=("ps_s",))
            p.op("DVE", (lambda e, tt=tt: e.tensor_copy(n2r[0:1, tt * TT:(tt + 1) * TT], c.ps_s[:])),
                 reads=("ps_s",), writes=("n2r",))
        p.op("DVE", lambda e: e.tensor_reduce(kmax[:], n2r[:], AX.X, ALU.max), reads=("n2r",), writes=("kmax",))
        p.op("ACT", lambda e: e.activation(kmax[:], kmax[:], AF.Sqrt), reads=("kmax",), writes=("kmax",))
        p.op("DVE", lambda e: e.tensor_scalar_mul(kmax[:], kmax[:], -8.0), reads=("kmax",), writes=("kmax",))
        p.op("DVE", lambda e: e.tensor_scalar_mul(rn_row[:], rn_row[:], kmax[:]),
             reads=("rn_row", "kmax"), writes=("rn_row",))
        p.dma("SP", qkv_scr[0, h * 64:(h + 1) * 64, :], qrot[:], reads=("qrot",), writes=("qkv_scr",))
        p.dma("SP", qkv_scr[1, h * 64:(h + 1) * 64, :], krot[:], reads=("krot",), writes=("qkv_scr",))
        p.dma("SP", qkv_scr[2, h * 64:(h + 1) * 64, :], prj[2][:], reads=("prj2",), writes=("qkv_scr",))
        if not hasattr(p, "_nb_scr"):
            p._nb_scr = p.dram("nb_scr", [8, TL], F32)
        p.dma("SP", p._nb_scr.ap()[h:h + 1, :], rn_row[:], reads=("rn_row",), writes=("nb_scr",))
        p.dma("POOL", kvw_o.ap()[h * 64:(h + 1) * 64, :], krot[:, TL - W_:TL],
              reads=("krot",), writes=("kvw_o",))
        p.dma("POOL", kvw_o.ap()[512 + h * 64:512 + (h + 1) * 64, :], prj[2][:, TL - W_:TL],
              reads=("prj2",), writes=("kvw_o",))
    p.collective("AllGather", ALU.bypass, [[0, 1], [2, 3], [4, 5], [6, 7]],
                 [kvw_o.ap()], [kvw_g.ap()], reads=("kvw_o",), writes=("kvw_g",))

    att_scr = p.dram("att_scr", [512, TL], BF16)
    atw = p.sbuf("atw", [64, W_], BF16)
    kin = p.sbuf("kin", [64, W_], BF16)
    vin = p.sbuf("vin", [64, W_], BF16)
    vt_sb = p.sbuf("vt_sb", [128, (TL // 128) * 64], BF16)
    vti = p.sbuf("vti", [128, (W_ // 128) * 64], BF16)
    nb_row = p.sbuf("nb_row", [1, TL], BF16)
    nbm_row = p.sbuf("nbm_row", [1, TL], BF16)
    pt_sb = [p.sbuf(f"pt_sb{k}", [128, W_], BF16) for k in range(2)]
    rs_row = p.sbuf("rs_row", [1, W_], F32)
    rb_sb = p.sbuf("rb_sb", [64, W_], F32)
    moff2 = p.sbuf("moff2", [1, 1], F32)
    p.op("DVE", lambda e: e.tensor_scalar(moff2[:], moffc[0:1, :], 1e30, -1e30,
                                          ALU.mult, ALU.add),
         reads=("moffc",), writes=("moff2",))
    ps_sum = c.ps_s
    ps_av = c.ps_b2
    for h in range(NHH):
        p.dma("SP", qrot[:], qkv_scr[0, h * 64:(h + 1) * 64, :], reads=("qkv_scr",), writes=("qrot",))
        p.dma("SP", krot[:], qkv_scr[1, h * 64:(h + 1) * 64, :], reads=("qkv_scr",), writes=("krot",))
        p.dma("SP", ts1[:], qkv_scr[2, h * 64:(h + 1) * 64, :], reads=("qkv_scr",), writes=("ts1",))
        p.dma("SP", kin[:], kvw_g.ap()[h * 64:(h + 1) * 64, :], reads=("kvw_g",), writes=("kin",))
        p.dma("SP", vin[:], kvw_g.ap()[512 + h * 64:512 + (h + 1) * 64, :], reads=("kvw_g",), writes=("vin",))
        p.dma("POOL", nb_row[:], p._nb_scr.ap()[h:h + 1, :], reads=("nb_scr",), writes=("nb_row",))
        p.op("DVE", (lambda e: e.tensor_scalar_add(nbm_row[:], nb_row[:], moff2[:])),
             reads=("nb_row", "moff2"), writes=("nbm_row",))
        for ts in range(TL // 128):
            k = ts % 2
            p.op("PE", (lambda e, ts=ts, k=k: e.transpose(psb_mm[k][0:128, 0:64],
                                                          ts1[:, ts * 128:(ts + 1) * 128], identb[0:64, 0:64])),
                 reads=("ts1", "identb"), writes=(f"ps_mm{k}",))
            p.op("DVE", (lambda e, ts=ts, k=k: e.tensor_copy(vt_sb[:, ts * 64:(ts + 1) * 64],
                                                             psb_mm[k][0:128, 0:64])),
                 reads=(f"ps_mm{k}",), writes=("vt_sb",))
        for ts in range(W_ // 128):
            k = ts % 2
            p.op("PE", (lambda e, ts=ts, k=k: e.transpose(psb_mm[k][0:128, 0:64],
                                                          vin[:, ts * 128:(ts + 1) * 128], identb[0:64, 0:64])),
                 reads=("vin", "identb"), writes=(f"ps_mm{k}",))
            p.op("DVE", (lambda e, ts=ts, k=k: e.tensor_copy(vti[:, ts * 64:(ts + 1) * 64],
                                                             psb_mm[k][0:128, 0:64])),
                 reads=(f"ps_mm{k}",), writes=("vti",))
        for w in range(NW):
            tql = slice(w * W_, (w + 1) * W_)
            blocks = []
            for tk in range(4):
                blocks.append(("gin", tk) if w == 0 else ("prev", (w - 1) * 4 + tk))
            for tk in range(4):
                blocks.append(("own", w * 4 + tk))
            first_b = True
            for bi, (src, tk) in enumerate(blocks):
                kk = bi % 2
                if src == "gin":
                    kap, kkey = kin[:, tk * 128:(tk + 1) * 128], "kin"
                    vap, vkey = vti[:, tk * 64:(tk + 1) * 64], "vti"
                    nrow, nkey = nbm_row, "nbm_row"
                else:
                    kap, kkey = krot[:, tk * 128:(tk + 1) * 128], "krot"
                    vap, vkey = vt_sb[:, tk * 64:(tk + 1) * 64], "vt_sb"
                    nrow, nkey = nb_row, "nb_row"
                p.op("PE", (lambda e, kk=kk, kap=kap, tql=tql: e.matmul(
                        ps_mm[kk][:], kap, qrot[:, tql], start=True, stop=False)),
                     reads=(kkey, "qrot"), writes=(f"ps_mm{kk}",))
                p.op("PE", (lambda e, kk=kk, nrow=nrow, tql=tql: e.matmul(
                        ps_mm[kk][:], onesrb[0:1, :], nrow[0:1, tql], start=False, stop=True)),
                     reads=("onesrb", nkey), writes=(f"ps_mm{kk}",))
                p.op("ACT", (lambda e, kk=kk: e.activation(pt_sb[kk][:], ps_mm[kk][:], AF.Exp)),
                     reads=(f"ps_mm{kk}",), writes=(f"pt_sb{kk}",))
                if src == "gin":
                    p.op("POOL", (lambda e, kk=kk, tk=tk: e.affine_select(
                            out=pt_sb[kk][:], in_=pt_sb[kk][:], pattern=[[-1, W_]],
                            compare_op=ALU.is_ge, fill=0.0, base=tk * 128,
                            channel_multiplier=1)),
                         reads=(f"pt_sb{kk}",), writes=(f"pt_sb{kk}",))
                elif src == "own":
                    p.op("POOL", (lambda e, kk=kk, tk=tk: e.affine_select(
                            out=pt_sb[kk][:], in_=pt_sb[kk][:], pattern=[[1, W_]],
                            compare_op=ALU.is_ge, fill=0.0, base=-(tk % 4) * 128,
                            channel_multiplier=-1)),
                         reads=(f"pt_sb{kk}",), writes=(f"pt_sb{kk}",))
                else:
                    p.op("POOL", (lambda e, kk=kk, tk=tk: e.affine_select(
                            out=pt_sb[kk][:], in_=pt_sb[kk][:], pattern=[[-1, W_]],
                            compare_op=ALU.is_ge, fill=0.0, base=(tk % 4) * 128,
                            channel_multiplier=1)),
                         reads=(f"pt_sb{kk}",), writes=(f"pt_sb{kk}",))
                p.op("PE", (lambda e, kk=kk, first_b=first_b, last=(bi == len(blocks) - 1):
                            e.matmul(ps_sum[:], onesb[:], pt_sb[kk][:], start=first_b, stop=last)),
                     reads=("onesb", f"pt_sb{kk}"), writes=("ps_s",))
                p.op("PE", (lambda e, kk=kk, vap=vap, first_b=first_b, last=(bi == len(blocks) - 1):
                            e.matmul(ps_av[0:64, 0:W_], vap, pt_sb[kk][:], start=first_b, stop=last)),
                     reads=(vkey, f"pt_sb{kk}"), writes=("ps_b2",))
                first_b = False
            p.op("DVE", lambda e: e.reciprocal(rs_row[:], ps_sum[0:1, 0:W_]), reads=("ps_s",), writes=("rs_row",))
            p.op("PE", (lambda e: e.matmul(ps_mm[0][0:64, :], c.ones_row_f[0:1, 0:64], rs_row[:],
                                           start=True, stop=True)),
                 reads=("c_ones_row_f", "rs_row"), writes=("ps_mm0",))
            p.op("DVE", lambda e: e.tensor_copy(rb_sb[:], ps_mm[0][0:64, 0:W_]),
                 reads=("ps_mm0",), writes=("rb_sb",))
            p.op("DVE", (lambda e, tql=tql: e.tensor_mul(atw[:], ps_av[0:64, 0:W_], rb_sb[:])),
                 reads=("ps_b2", "rb_sb"), writes=("atw",))
            p.dma("SP", att_scr.ap()[h * 64:(h + 1) * 64, tql], atw[:],
                  reads=("atw",), writes=("att_scr",))

    att = [p.sbuf(f"att{i}", [128, TT], BF16) for i in range(4)]
    wo_sb = p.sbuf("wo_sb", [128, 4 * 128 * DC], BF16)
    for oi in range(DC):
        load_wchunks(p, "POOL", wo_sb[:, oi * 512:(oi + 1) * 512], "wo_sb", wo, D, 0, 4, oi * 128, 128)
    for tt in range(NTT):
        tsl = slice(tt * TT, (tt + 1) * TT)
        for i in range(4):
            p.dma("SP", att[i][:], att_scr.ap()[i * 128:(i + 1) * 128, tsl],
                  reads=("att_scr",), writes=(f"att{i}",))
        for oi in range(DC):
            k = oi % 2
            for i in range(4):
                p.op("PE", (lambda e, k=k, i=i, oi=oi: e.matmul(
                        ps_mm[k][:], wo_sb[:, oi * 512 + i * 128:oi * 512 + (i + 1) * 128], att[i][:],
                        start=(i == 0), stop=(i == 3))),
                     reads=("wo_sb", f"att{i}"), writes=(f"ps_mm{k}",))
            p.dma("SP", yev[k][:], ysrc[oi * 128:(oi + 1) * 128, tsl],
                  reads=("y_scr",) if first else (), writes=(f"yev{k}",))
            p.op("DVE", (lambda e, k=k: e.tensor_add(yev[k][:], yev[k][:], ps_mm[k][:])),
                 reads=(f"yev{k}", f"ps_mm{k}"), writes=(f"yev{k}",))
            p.dma("SP", y1_scr[oi * 128:(oi + 1) * 128, tsl], yev[k][:],
                  reads=(f"yev{k}",), writes=("y1_scr",))

    p.section_barrier()
    p.pop_scope()
    p.push_scope()

    for i in range(4):
        p.dma("SP", bsc[:, i:i + 1], y1_scr[(4 + i) * 128:(5 + i) * 128, TL - 1:TL],
              reads=("y1_scr",), writes=("bsc",), allow_slow_non_contiguous=True)
    p.dma("SP", bnd2_o.ap(), bsc[:], reads=("bsc",), writes=("bnd2_o",))
    p.collective("AllGather", ALU.bypass, [[0, 1], [2, 3], [4, 5], [6, 7]],
                 [bnd2_o.ap()], [bnd2_g.ap()], reads=("bnd2_o",), writes=("bnd2_g",))
    bin2 = p.sbuf("bin2", [128, 4], F32)
    p.dma("SP", bin2[:], bnd2_g.ap()[0:128, :], reads=("bnd2_g",), writes=("bin2",))
    p.op("DVE", lambda e: e.tensor_scalar_mul(bin2[:], bin2[:], moffc[:]),
         reads=("bin2", "moffc"), writes=("bin2",))
    ysh2t = p.sbuf("ysh2t", [128, TT], F32)
    for i in range(DC):
        for tt in range(NTT):
            if i < 4:
                p.dma("SP", ytmp[:], y1_scr[i * 128:(i + 1) * 128, tt * TT:(tt + 1) * TT],
                      reads=("y1_scr",), writes=("ytmp",))
                p.op("DVE", lambda e: e.tensor_copy(ysh2t[:], ytmp[:]), reads=("ytmp",), writes=("ysh2t",))
            elif tt == 0:
                p.dma("SP", ytmp[:, 0:TT - 1], y1_scr[i * 128:(i + 1) * 128, 0:TT - 1],
                      reads=("y1_scr",), writes=("ytmp",))
                p.op("DVE", lambda e: e.tensor_copy(ysh2t[:, 1:TT], ytmp[:, 0:TT - 1]),
                     reads=("ytmp",), writes=("ysh2t",))
                p.op("DVE", (lambda e, i=i: e.tensor_copy(ysh2t[:, 0:1], bin2[:, i - 4:i - 3])),
                     reads=("bin2", "ysh2t"), writes=("ysh2t",))
            else:
                p.dma("SP", ytmp[:], y1_scr[i * 128:(i + 1) * 128, tt * TT - 1:(tt + 1) * TT - 1],
                      reads=("y1_scr",), writes=("ytmp",))
                p.op("DVE", lambda e: e.tensor_copy(ysh2t[:], ytmp[:]), reads=("ytmp",), writes=("ysh2t",))
            p.dma("SP", y_scr[i * 128:(i + 1) * 128, tt * TT:(tt + 1) * TT], ysh2t[:],
                  reads=("ysh2t",), writes=("y_scr2",))
    lng_sb = p.sbuf("lng_sb", [128, DC], F32)
    lnb_sb = p.sbuf("lnb_sb", [128, DC], F32)
    p.dma("SP", lng_sb[:], lng, writes=("lng_sb",))
    p.dma("SP", lnb_sb[:], lnb, writes=("lnb_sb",))
    mkeys = [f"m{i}" for i in range(DC)]
    p.barrier([f"ysh{i}" for i in range(DC)] + mkeys + ["y_scr2", "lnxb0", "lnxb1"])
    m = ysh
    for tt in range(NTT):
        emit_ln_tile(p, c, ("hbm", y_scr), [(m[i], f"m{i}") for i in range(DC)], tt, "lnm",
                     gain=(lng_sb, "lng_sb", 0), beta=(lnb_sb, "lnb_sb", 0), out_off=tt * TT)

    b1a_sb = p.sbuf("b1a_sb", [128, NFC], F32)
    b1g_sb = p.sbuf("b1g_sb", [128, NFC], F32)
    b2_sb = p.sbuf("b2_sb", [128, DC], F32)
    p.dma("SP", b1a_sb[:], b1a, writes=("b1a_sb",))
    p.dma("SP", b1g_sb[:], b1g, writes=("b1g_sb",))
    p.dma("SP", b2_sb[:], b2d, writes=("b2_sb",))
    w1_sb = [p.sbuf(f"w1_sb{k}", [128, DC * 128], BF16) for k in range(4)]
    w2_sb = [p.sbuf(f"w2_sb{k}", [128, NFC * 128], BF16) for k in range(1)]
    gl = p.sbuf("gl", [128, TT], F32)
    gl2 = p.sbuf("gl2", [128, TT], F32)
    hbf = [p.sbuf(f"hbf{j}", [128, 4 * TT], BF16) for j in range(6)]
    xre = None if first else [p.sbuf(f"xre{k}", [128, TT], F32) for k in range(2)]
    outs = []
    for tt in range(NTT):
        tsl = slice(tt * TT, (tt + 1) * TT)
        for fi in range(NFC):
            k = fi % 2
            fw = 128 if fi < 21 else 42
            pa, pak = (ps_mm[0], "ps_mm0") if k == 0 else (c.ps_b, "ps_b")
            pg, pgk = (ps_mm[1], "ps_mm1") if k == 0 else (c.ps_b2, "ps_b2")
            glb, glk = (gl, "gl") if k == 0 else (gl2, "gl2")
            for half, base in ((0, 0), (1, 2730)):
                kw = k if half == 0 else k + 2
                w1dst = bass.AP(tensor=w1_sb[kw].ap().tensor, offset=0,
                                ap=[[w1_sb[kw].ap().ap[0][0], 128], [128, DC], [1, fw]])
                load_wchunks(p, "POOL", w1dst, f"w1_sb{kw}", w1, 5460, 0, DC, base + fi * 128, fw)
                pp, ppk = (pa, pak) if half == 0 else (pg, pgk)
                for i in range(DC):
                    p.op("PE", (lambda e, kw=kw, i=i, tsl=tsl, fw=fw, pp=pp: e.matmul(
                            pp[0:fw, 0:TT],
                            w1_sb[kw][:, i * 128:i * 128 + fw], m[i][:, tsl],
                            start=(i == 0), stop=(i == DC - 1))),
                         reads=(f"w1_sb{kw}", f"m{i}"), writes=(ppk,))
            p.op("ACT", (lambda e, fi=fi, fw=fw, pg=pg, glb=glb: e.activation(
                    glb[0:fw, :], pg[0:fw, 0:TT], AF.Gelu, bias=b1g_sb[0:fw, fi:fi + 1])),
                 reads=(pgk, "b1g_sb"), writes=(glk,))
            p.op("DVE", (lambda e, fi=fi, fw=fw, pa=pa: e.tensor_scalar(
                    pa[0:fw, 0:TT], pa[0:fw, 0:TT], b1a_sb[0:fw, fi:fi + 1], None,
                    ALU.add)),
                 reads=(pak, "b1a_sb"), writes=(pak,))
            p.op("DVE", (lambda e, fi=fi, fw=fw, pa=pa, glb=glb: e.tensor_mul(
                    hbf[fi % 6][0:fw, (fi // 6) * TT:(fi // 6 + 1) * TT],
                    pa[0:fw, 0:TT], glb[0:fw, :])),
                 reads=(pak, glk), writes=(f"hbf{fi}",))
        for oi in range(DC):
            k = oi % 2
            kw = 0
            load_wchunks(p, "POOL", w2_sb[kw][:, 0:21 * 128], f"w2_sb{kw}", w2, D, 0, 21, oi * 128, 128)
            p.dma("POOL", w2_sb[kw][0:42, 21 * 128:22 * 128],
                  w2[21 * 128:21 * 128 + 42, oi * 128:(oi + 1) * 128], writes=(f"w2_sb{kw}",))
            for fi in range(NFC):
                fw = 128 if fi < 21 else 42
                p.op("PE", (lambda e, k=k, kw=kw, fi=fi, fw=fw: e.matmul(
                        ps_mm[k][:], w2_sb[kw][0:fw, fi * 128:fi * 128 + fw] if False else
                        w2_sb[kw][0:fw, fi * 128:fi * 128 + 128][0:fw, 0:128],
                        hbf[fi % 6][0:fw, (fi // 6) * TT:(fi // 6 + 1) * TT],
                        start=(fi == 0), stop=(fi == NFC - 1))),
                     reads=(f"w2_sb{kw}", f"hbf{fi}"), writes=(f"ps_mm{k}",))
            p.dma("SP", yev[k][:], y1_scr[oi * 128:(oi + 1) * 128, tsl], reads=("y1_scr",), writes=(f"yev{k}",))
            p.op("DVE", (lambda e, k=k, oi=oi: e.tensor_scalar_add(yev[k][:], yev[k][:],
                                                                   b2_sb[:, oi:oi + 1])),
                 reads=(f"yev{k}", "b2_sb"), writes=(f"yev{k}",))
            p.op("DVE", (lambda e, k=k: e.tensor_add(yev[k][:], yev[k][:], ps_mm[k][:])),
                 reads=(f"yev{k}", f"ps_mm{k}"), writes=(f"yev{k}",))
            p.dma("SP", y_out[oi * 128:(oi + 1) * 128, tsl], yev[k][:],
                  reads=(f"yev{k}",), writes=(f"yo_{oi}_{tt}",))
            outs.append(f"yo_{oi}_{tt}")
            if not first:
                p.dma("SP", xre[k][:], x_in[oi * 128:(oi + 1) * 128, tsl], writes=(f"xre{k}",))
                p.op("DVE", (lambda e, k=k: e.scalar_tensor_tensor(xre[k][:], yev[k][:], 0.5,
                                                                   xre[k][:], ALU.mult, ALU.add)),
                     reads=(f"yev{k}", f"xre{k}"), writes=(f"xre{k}",))
                p.dma("SP", x_out[oi * 128:(oi + 1) * 128, tsl], xre[k][:],
                      reads=(f"xre{k}",), writes=(f"xf_{oi}_{tt}",))
                outs.append(f"xf_{oi}_{tt}")
    p.pop_scope()
    p.final_wait(outs)
    return p



_PERM = np.concatenate([np.arange(32, 64), np.arange(0, 32)])


def _cshape(a):
    return np.ascontiguousarray(np.asarray(a, np.float32).reshape(-1, 128).T)


def _l2ffn_static(il, hh, inputs):
    wqkv = inputs["L2_ffn_wqkv"][il]
    q_scale = inputs["L2_ffn_q_scale"][il]
    k_scale = inputs["L2_ffn_k_scale"][il]
    Wq, Wk, Wv = wqkv[:, :512], wqkv[:, 512:1024], wqkv[:, 1024:]
    permute = lambda A: np.ascontiguousarray(
        A.reshape(1024, 8, 64)[:, :, _PERM].reshape(1024, 512))
    Wqp, Wkp = permute(Wq), permute(Wk)
    parts = []
    for hh2 in range(8):
        parts += [Wq[:, hh2 * 64:(hh2 + 1) * 64], Wqp[:, hh2 * 64:(hh2 + 1) * 64],
                  Wk[:, hh2 * 64:(hh2 + 1) * 64], Wkp[:, hh2 * 64:(hh2 + 1) * 64],
                  Wv[:, hh2 * 64:(hh2 + 1) * 64]]
    wqkv5 = np.ascontiguousarray(np.concatenate(parts, 1).astype(np.float32))
    tg = (np.arange(2048) + hh * 2048).astype(np.float64)
    hi = np.arange(0, 64, 2, dtype=np.float64)
    invf = 1.0 / 10000.0 ** (hi / 64)
    fr = tg[None, :] * np.concatenate([invf, invf])[:, None]
    base = (hi + 0.4 * 64) / (1.4 * 64)
    power = (tg - 2048.0) / 256.0
    sc = np.concatenate([base, base])[:, None] ** power[None, :]
    sig = np.concatenate([-np.ones(32), np.ones(32)])[:, None]
    tabs = np.ascontiguousarray(np.stack(
        [8.0 * np.cos(fr) * sc, 8.0 * sig * np.sin(fr) * sc,
         np.cos(fr) / sc, sig * np.sin(fr) / sc]).astype(np.float32))
    pad = lambda v, n: np.concatenate([v, np.zeros(n - len(v), v.dtype)])
    b1 = inputs["L2_ffn_b1"][il]
    return {
        "wqkv": wqkv5, "tabs": tabs,
        "qsc": np.ascontiguousarray(np.stack([q_scale, q_scale[_PERM]], 1).astype(np.float32)),
        "ksc": np.ascontiguousarray(np.stack([k_scale, k_scale[_PERM]], 1).astype(np.float32)),
        "wo": np.ascontiguousarray(inputs["L2_ffn_wo"][il]),
        "lng": _cshape(inputs["L2_ffn_ln_g"][il]), "lnb": _cshape(inputs["L2_ffn_ln_b"][il]),
        "w1": np.ascontiguousarray(inputs["L2_ffn_w1"][il]),
        "b1a": np.ascontiguousarray(pad(b1[:2730], 22 * 128).reshape(22, 128).T),
        "b1g": np.ascontiguousarray(pad(b1[2730:], 22 * 128).reshape(22, 128).T),
        "w2": np.ascontiguousarray(inputs["L2_ffn_w2"][il]),
        "b2d": _cshape(inputs["L2_ffn_b2"][il]),
        "moff": np.full((128, 1), 1.0 if hh == 1 else 0.0, np.float32),
    }


def _section_inputs(inputs):
    sel8 = np.zeros((8, 1024), np.float32)
    for e in range(8):
        sel8[e, e * 128:(e + 1) * 128] = 1.0
    l0f = {"wr": inputs["L0_ffn_Wr"], "wi": inputs["L0_ffn_Wi"], "wo": inputs["L0_ffn_Wo"],
           "sel8": sel8}
    l0a = {"wqkv": inputs["L0_attn_qkv_w"], "bqkv": _cshape(inputs["L0_attn_qkv_b"]),
           "wout": inputs["L0_attn_out_w"], "boutd": _cshape(inputs["L0_attn_out_b"]),
           "n1w": _cshape(inputs["L0_attn_norm1_w"]), "n2w": _cshape(inputs["L0_attn_norm2_w"]),
           "w1": inputs["L0_attn_mlp_w1"], "b1d": _cshape(inputs["L0_attn_mlp_b1"]),
           "w2": inputs["L0_attn_mlp_w2"], "b2d": _cshape(inputs["L0_attn_mlp_b2"])}
    l1f = {"w1t": np.ascontiguousarray(inputs["L1_ffn_W1"].T),
           "w2t": np.ascontiguousarray(inputs["L1_ffn_W2"].T),
           "b1d": _cshape(inputs["L1_ffn_b1"]), "b2d": _cshape(inputs["L1_ffn_b2"])}
    l1a = {"wkt": np.ascontiguousarray(inputs["L1_attn_Wk"].T),
           "wvt": np.ascontiguousarray(inputs["L1_attn_Wv"].T),
           "wrt": np.ascontiguousarray(inputs["L1_attn_Wr"].T),
           "wot": np.ascontiguousarray(inputs["L1_attn_Wo"].T),
           "tdc": _cshape(inputs["L1_attn_time_decay"]), "tfc": _cshape(inputs["L1_attn_time_first"]),
           "mkc": _cshape(inputs["L1_attn_time_mix_k"]), "mvc": _cshape(inputs["L1_attn_time_mix_v"]),
           "mrc": _cshape(inputs["L1_attn_time_mix_r"])}
    l2a = {"wqk": np.ascontiguousarray(np.stack(
               [inputs["L2_attn_qkv_kernel"][:, j * 192:j * 192 + 128] for j in range(16)])),
           "wv": np.ascontiguousarray(np.stack(
               [inputs["L2_attn_qkv_kernel"][:, j * 192 + 128:j * 192 + 192] for j in range(16)])),
           "bqk": np.ascontiguousarray(np.stack(
               [inputs["L2_attn_qkv_bias"][0, j * 192:j * 192 + 128] for j in range(16)], 1)),
           "bv": np.ascontiguousarray(np.stack(
               [inputs["L2_attn_qkv_bias"][0, j * 192 + 128:j * 192 + 192] for j in range(16)], 1))}
    l3a = {"wqk": np.ascontiguousarray(inputs["L3_attn_Wqkv"][:, :, 0:128]),
           "wv": np.ascontiguousarray(
               inputs["L3_attn_Wqkv"][:, :, 128:192].transpose(1, 0, 2).reshape(1024, 1024)),
           "bqk": np.ascontiguousarray(inputs["L3_attn_bqkv"][:, 0:128].T),
           "bv": np.ascontiguousarray(inputs["L3_attn_bqkv"][:, 128:192].reshape(1, 1024)),
           "wout": np.ascontiguousarray(inputs["L3_attn_Wout"]),
           "boutd": _cshape(inputs["L3_attn_bout"])}
    l3f = {"wgd": inputs["L3_ffn_Wg"], "bgd": inputs["L3_ffn_bg"].reshape(1, 8),
           "wet": np.ascontiguousarray(np.transpose(inputs["L3_ffn_We"], (0, 2, 1))),
           "bed": inputs["L3_ffn_be"], "sel8": sel8}

    def pc_moff128(cidx):
        return {"moff": np.full((128, 1), 1.0 if (cidx % 2) == 1 else 0.0, np.float32)}

    def pc_l0a(cidx):
        hh = cidx % 2
        t_loc = (np.arange(2048) + hh * 2048) * (np.pi / 2 / 4096)
        return {"poscs": np.stack([np.cos(t_loc), np.sin(t_loc)]).astype(np.float32),
                "moff": np.full((64, 1), 1.0 if hh == 1 else 0.0, np.float32)}

    def pc_l3a(cidx):
        return {"moff": np.array([[0.0 if (cidx % 2) == 1 else -30000.0]], np.float32)}

    l2A = [[_l2ffn_static(0, cidx % 2, inputs) for cidx in range(NCORES)],
           [_l2ffn_static(1, cidx % 2, inputs) for cidx in range(NCORES)]]
    secs = [
        ("s00_", l0f, None), ("s01_", l0a, pc_l0a), ("s02_", l0f, None),
        ("s03_", l1f, None), ("s04_", l1a, pc_moff128), ("s05_", l1f, None),
        ("s06_", l2A[0], None), ("s07_", l2A[1], None),
        ("s08_", l2a, None),
        ("s09_", l2A[0], None), ("s10_", l2A[1], None),
        ("s11_", l3f, None), ("s12_", l3a, pc_l3a), ("s13_", l3f, None),
    ]
    return secs


_SEC_BUILDERS = None


def _sec_builders():
    global _SEC_BUILDERS
    if _SEC_BUILDERS is None:
        _SEC_BUILDERS = [
            build_l0ffn_launch, build_l0attn_launch, build_l0ffn_launch,
            build_l1ffn_launch, build_l1attn_launch, build_l1ffn_launch,
            lambda p: build_l2ffn_inner(True, p), lambda p: build_l2ffn_inner(False, p),
            build_l2attn_launch,
            lambda p: build_l2ffn_inner(True, p), lambda p: build_l2ffn_inner(False, p),
            build_l3ffn_launch, build_l3attn_launch, build_l3ffn_launch,
        ]
    return _SEC_BUILDERS


_MEGA = None


def build_megakernel():
    global _MEGA
    if _MEGA is not None:
        return _MEGA
    p = Prog("mega")
    nc = p.nc
    x0 = nc.dram_tensor("x0", [D, TL], F32, kind="ExternalInput")
    xcur = nc.dram_tensor("xcur", [D, TL], F32)
    xout = nc.dram_tensor("xout", [D, TL], F32, kind="ExternalOutput")
    yps = [nc.dram_tensor(f"yps{j}", [D, TL], F32) for j in range(2)]
    ydump = [nc.dram_tensor(f"ydump{j}", [D, TL], F32) for j in range(2)]
    p.dma("SP", xcur.ap(), x0.ap(), writes=("xcur0",))
    builders = _sec_builders()
    for i, b in enumerate(builders):
        p.begin_section(f"s{i:02d}_")
        dm = {"x_in": xcur, "x_out": xcur}
        if i == 13:
            dm["x_out"] = xout
        if i in (6, 9):
            dm["y_out"] = yps[0 if i == 6 else 1]
        if i in (7, 10):
            dm["y_in"] = yps[0 if i == 7 else 1]
            dm["y_out"] = ydump[0 if i == 7 else 1]
        p.dram_map = dm
        b(p)
        p.end_section()
    p.finalize()
    _MEGA = p
    return p


def kernel(**inputs):
    inputs = {k: np.asarray(v) for k, v in inputs.items()}
    x = inputs["L0_attn_x"]
    secs = _section_inputs(inputs)
    in_maps = []
    for cidx in range(NCORES):
        b_, hh = cidx // 2, cidx % 2
        m = {"x0": np.ascontiguousarray(x[b_, hh * 2048:(hh + 1) * 2048, :].T)}
        for pref, static, pcfn in secs:
            st = static[cidx] if isinstance(static, list) else static
            for kk, vv in st.items():
                m[pref + kk] = vv
            if pcfn is not None:
                for kk, vv in pcfn(cidx).items():
                    m[pref + kk] = vv
        in_maps.append(m)
    p = build_megakernel()
    res = run_bass_kernel_spmd(p.nc, in_maps, core_ids=list(range(NCORES)))
    out = np.empty((4, 4096, 1024), np.float32)
    for cidx in range(NCORES):
        b_, hh = cidx // 2, cidx % 2
        out[b_, hh * 2048:(hh + 1) * 2048, :] = np.asarray(res.results[cidx]["xout"]).T
    return out


# version 08 (final)
# speedup vs baseline: 1.0748x; 1.0076x over previous
import contextlib
import numpy as np
import ml_dtypes
import concourse.bass as bass
import concourse.mybir as mybir
from concourse.bass_utils import run_bass_kernel_spmd

F32 = mybir.dt.float32
BF16 = mybir.dt.bfloat16
AF = mybir.ActivationFunctionType
ALU = mybir.AluOpType
AX = mybir.AxisListType

B, T, D = 4, 4096, 1024
NCORES = 8
TL = T // 2
DC = D // 128
TT = 512
NTT = TL // TT

ENGINES = ("PE", "ACT", "DVE", "POOL", "SP")
_ENG_ATTR = {"PE": "tensor", "ACT": "scalar", "DVE": "vector", "POOL": "gpsimd", "SP": "sync"}


class _Op:
    __slots__ = ("engine", "idx", "kind", "fn", "deps", "marked", "semval", "self_wait")

    def __init__(self, engine, idx, kind, fn):
        self.engine = engine
        self.idx = idx
        self.kind = kind
        self.self_wait = None
        self.fn = fn
        self.deps = []
        self.marked = False
        self.semval = None


class Prog:
    def __init__(self, name="mk", paranoid=False, paranoid_range=None):
        self.nc = bass.Bass("TRN2", target_bir_lowering=False, debug=False)
        self.name = name
        self.paranoid = paranoid
        self.paranoid_range = paranoid_range
        self._gidx = 0
        self._prev_op = None
        self.stack = contextlib.ExitStack()
        self._scopes = [self.stack]
        self.streams = {e: [] for e in ENGINES}
        self.last_w = {}
        self.readers = {}
        self.n_sbuf_bytes = 0
        self._finalized = False
        self._cc_count = 0
        self.dram_prefix = ""
        self.dram_map = {}
        self._sec_bar = None
        self._last_per_engine = {}
        self._secbar_tile = self.stack.enter_context(
            self.nc.sbuf_tensor("_secbar", [1, 1], F32))
        self._bar_tile = self.stack.enter_context(
            self.nc.sbuf_tensor("_bar_tile", [1, 1], F32))

    def push_scope(self):
        self._scopes.append(contextlib.ExitStack())

    def pop_scope(self):
        self._scopes.pop().close()

    def section_barrier(self):
        t = self._secbar_tile
        lasts = dict(self._last_per_engine)
        def fn(eng):
            return eng.memset(t[:], 0.0)
        o = self.op("DVE", fn, reads=(), writes=("_secbar",))
        for d in lasts.values():
            if d is not o:
                o.deps.append(d)
        self._sec_bar = o

    def dram(self, name, shape, dtype, kind=None):
        if name in self.dram_map:
            return self.dram_map[name]
        name = self.dram_prefix + name
        if kind is None:
            t = self.nc.dram_tensor(name, list(shape), dtype)
        else:
            t = self.nc.dram_tensor(name, list(shape), dtype, kind=kind)
        return t

    def sbuf(self, name, shape, dtype):
        t = self._scopes[-1].enter_context(
            self.nc.sbuf_tensor(self.dram_prefix + name, list(shape), dtype))
        self.n_sbuf_bytes += shape[1] * (4 if dtype == F32 else 2)
        return t

    def psum(self, name, shape, dtype=F32):
        return self._scopes[-1].enter_context(
            self.nc.psum_tensor(self.dram_prefix + name, list(shape), dtype))

    def op(self, engine, fn, reads=(), writes=(), kind="compute"):
        o = _Op(engine, len(self.streams[engine]), kind, fn)
        deps = {}
        gi = self._gidx
        self._gidx += 1
        par = self.paranoid or (self.paranoid_range is not None and
                                self.paranoid_range[0] <= gi < self.paranoid_range[1])
        if par and self._prev_op is not None:
            d = self._prev_op
            deps[(d.engine, d.idx)] = d
        if self._sec_bar is not None:
            deps[(self._sec_bar.engine, self._sec_bar.idx)] = self._sec_bar
        for r in reads:
            w = self.last_w.get(r)
            if w is not None:
                deps[(w.engine, w.idx)] = w
        for w_ in writes:
            w = self.last_w.get(w_)
            if w is not None:
                deps[(w.engine, w.idx)] = w
            for rd in self.readers.get(w_, ()):
                deps[(rd.engine, rd.idx)] = rd
        for d in deps.values():
            if d.engine == engine:
                if engine == "PE":
                    continue
                if d.idx == o.idx - 1 and d.kind == "compute" and kind == "dma":
                    pass
            o.deps.append(d)
        self.streams[engine].append(o)
        for r in reads:
            self.readers.setdefault(r, []).append(o)
        for w_ in writes:
            self.last_w[w_] = o
            self.readers[w_] = []
        self._prev_op = o
        self._last_per_engine[engine] = o
        return o

    def dma(self, queue, out, in_, reads=(), writes=(), **kw):
        def fn(eng):
            return eng.dma_start(out=out, in_=in_, **kw)
        return self.op(queue, fn, reads=reads, writes=writes, kind="dma")

    def begin_section(self, prefix):
        self.section_barrier()
        self.push_scope()
        self.dram_prefix = prefix
        for attr in ("_moe", "_lnxb", "_lnsq", "_lnrows", "_mk0", "_nb_scr", "_cols_sec"):
            if hasattr(self, attr):
                delattr(self, attr)

    def end_section(self):
        self.pop_scope()
        self.dram_prefix = ""
        self.dram_map = {}

    def barrier(self, keys):
        t = self._bar_tile
        def fn(eng):
            return eng.memset(t[:], 0.0)
        return self.op("DVE", fn, reads=(), writes=tuple(keys) + ("_bar_tile",))

    def collective(self, kind_s, alu, groups, ins, outs, reads=(), writes=()):
        def fn(eng):
            return eng.collective_compute(kind_s, alu, replica_groups=groups, ins=ins, outs=outs)
        return self.op("POOL", fn, reads=reads, writes=writes, kind="cc")

    def finalize(self):
        assert not self._finalized
        self._finalized = True
        nc = self.nc
        for e in ENGINES:
            for o in self.streams[e]:
                for d in o.deps:
                    d.marked = True
        NS = 6
        for e in ENGINES:
            cnt = 0
            dmacnt = 0
            uses = {}
            for o in self.streams[e]:
                if o.kind == "dma":
                    sn = "D%s%d" % (e, dmacnt % NS)
                    dmacnt += 1
                    u = uses.get(sn, 0) + 1
                    uses[sn] = u
                    o.semval = (sn, u * 16)
                    o.self_wait = (sn, (u - 1) * 16) if u > 1 else None
                    o.marked = True
                elif o.kind == "cc":
                    sn = "CC%d" % (self._cc_count % 2)
                    self._cc_count += 1
                    u = uses.get(sn, 0) + 1
                    uses[sn] = u
                    o.semval = (sn, u)
                    o.self_wait = (sn, u - 1) if u > 1 else None
                    o.marked = True
                elif o.marked:
                    cnt += 1
                    o.semval = ("E" + e, cnt)
        sems = {}
        semnames = set()
        for e in ENGINES:
            for o in self.streams[e]:
                if o.semval:
                    semnames.add(o.semval[0])
        for sn in sorted(semnames):
            sems[sn] = self.stack.enter_context(nc.semaphore(sn))
        self.sems = sems

        stream_ops = {e: list(self.streams[e]) for e in ENGINES}
        waits_for = {e: [] for e in ENGINES}
        for e in ENGINES:
            clock = {}
            for o in stream_ops[e]:
                ws = {}
                if getattr(o, "self_wait", None) is not None:
                    sn, val = o.self_wait
                    ws[sn] = val
                for d in o.deps:
                    s = self.streams[d.engine]
                    j = d.idx
                    while j < len(s) and s[j].semval is None:
                        j += 1
                    assert j < len(s), f"dep of {e}#{o.idx} on unmarked tail {d.engine}#{d.idx}"
                    sn, val = s[j].semval
                    if ws.get(sn, -1) < val:
                        ws[sn] = val
                waits = []
                for sn, val in ws.items():
                    if clock.get(sn, -1) < val:
                        waits.append((sn, val))
                        clock[sn] = val
                waits_for[e].append(waits)

        block = self.stack.enter_context(nc.Block())
        prog = self

        def make_body(ename):
            def body(eng):
                for o, waits in zip(stream_ops[ename], waits_for[ename]):
                    for sn, val in waits:
                        eng.wait_ge(prog.sems[sn], val)
                    inst = o.fn(eng)
                    if o.semval is not None:
                        sn, val = o.semval
                        if o.kind == "dma":
                            inst.then_inc(prog.sems[sn], 16)
                        elif o.kind == "cc":
                            inst.then_inc(prog.sems[sn])
                        else:
                            inst.then_inc(prog.sems[sn], 1)
            return body

        for ename in ENGINES:
            if not stream_ops[ename]:
                continue
            getattr(block, _ENG_ATTR[ename])(make_body(ename))

    def close(self):
        self.stack.close()

    def final_wait(self, regions):
        def fn(eng):
            return eng.nop()
        self.op("SP", fn, reads=tuple(regions), kind="compute")


def run_prog(prog, in_maps, **kw):
    prog.finalize()
    res = run_bass_kernel_spmd(prog.nc, in_maps, core_ids=list(range(NCORES)), **kw)
    prog.close()
    return res



def load_wchunks(p, queue, dst, dkey, wt_ap, row_stride, d0, nchunk, f0, fw):
    in_ap = bass.AP(tensor=wt_ap.tensor, offset=wt_ap.offset + d0 * row_stride + f0,
                    ap=[[row_stride, 128], [128 * row_stride, nchunk], [1, fw]])
    p.dma(queue, dst, in_ap, writes=(dkey,))


MAGIC_RNE = 12582912.0


class Consts:
    def __init__(self, p: Prog):
        self.ones_col_f = p.sbuf("c_ones_col_f", [128, 1], F32)
        self.ones_row_f = p.sbuf("c_ones_row_f", [1, 128], F32)
        p.op("DVE", lambda e: e.memset(self.ones_col_f[:], 1.0), writes=("c_ones_col_f",))
        p.op("DVE", lambda e: e.memset(self.ones_row_f[:], 1.0), writes=("c_ones_row_f",))
        self._p = p
        self._cols = {}
        self.ps_s = p.psum("ps_s", [1, TT], F32)
        self.ps_s2 = p.psum("ps_s2", [1, TT], F32)
        self.ps_b = p.psum("ps_b", [128, TT], F32)
        self.ps_b2 = p.psum("ps_b2", [128, TT], F32)

    def col(self, value, parts=128):
        key = (float(value), parts)
        if key not in self._cols:
            p = self._p
            name = f"c_col{len(self._cols)}"
            t = p.sbuf(name, [parts, 1], F32)
            p.op("DVE", (lambda e, t=t, value=value: e.memset(t[:], float(value))), writes=(name,))
            self._cols[key] = (t, name)
        return self._cols[key]


def emit_ln_tile(p, c, x_src, ln_sb, tt, pref, kind="ln", eps=1e-5, gain=None, beta=None,
                 out_off=0, n_tok_tile=TT):
    u = f"{pref}_t{tt}"
    mode, x = x_src
    tsl = slice(tt * n_tok_tile, (tt + 1) * n_tok_tile)
    if mode == "hbm":
        xb = [p.sbuf(f"{u}_xb{k}", [128, n_tok_tile], F32) for k in range(3)] \
            if not hasattr(p, "_lnxb") else p._lnxb
        p._lnxb = xb
        xkeys = [f"lnxb{k}" for k in range(3)]
    sq4 = getattr(p, "_lnsq", None)
    if sq4 is None:
        sq4 = [p.sbuf(f"lnsq{j}", [128, n_tok_tile], F32) for j in range(3)]
        p._lnsq = sq4
        p._lnrows = [p.sbuf(f"lnrow{j}", [1, n_tok_tile], F32) for j in range(3)]
    mrow, qrow, rrow = p._lnrows
    ps_s, ps_b, ps_b2 = c.ps_s, c.ps_b, c.ps_b2
    ps_s2 = c.ps_s2

    def get_x(i, k):
        if mode == "hbm":
            p.dma("SP", xb[k][:], x[i * 128:(i + 1) * 128, tsl], writes=(xkeys[k],))
            return xb[k][:], xkeys[k]
        t, key = x[i]
        return t[:, tsl], key

    for i in range(DC):
        xv, xkey = get_x(i, i % 3)
        sq = sq4[i % 3]
        if kind == "ln":
            p.op("PE", (lambda e, xv=xv, i=i: e.matmul(ps_s[:], c.ones_col_f[:], xv,
                                                       start=(i == 0), stop=(i == DC - 1))),
                 reads=("c_ones_col_f", xkey), writes=("ps_s",))
        p.op("ACT", (lambda e, xv=xv, sq=sq: e.activation(sq[:], xv, AF.Square)),
             reads=(xkey,), writes=(f"lnsq{i % 3}",))
        p.op("PE", (lambda e, i=i, sq=sq: e.matmul(ps_s2[:], c.ones_col_f[:], sq[:],
                                            start=(i == 0), stop=(i == DC - 1))),
             reads=("c_ones_col_f", f"lnsq{i % 3}"), writes=("ps_s2",))
    if kind == "ln":
        p.op("DVE", lambda e: e.tensor_scalar_mul(mrow[:], ps_s[:], 1.0 / D),
             reads=("ps_s",), writes=("lnmrow",))
        p.op("DVE", lambda e: e.tensor_scalar_mul(qrow[:], ps_s2[:], 1.0 / D),
             reads=("ps_s2",), writes=("lnqrow",))
        p.op("DVE", lambda e: e.tensor_mul(rrow[:], mrow[:], mrow[:]),
             reads=("lnmrow",), writes=("lnrrow",))
        p.op("DVE", lambda e: e.tensor_sub(qrow[:], qrow[:], rrow[:]),
             reads=("lnqrow", "lnrrow"), writes=("lnqrow",))
    else:
        p.op("DVE", lambda e: e.tensor_scalar_mul(qrow[:], ps_s2[:], 1.0 / D),
             reads=("ps_s2",), writes=("lnqrow",))
    epst, epsk = c.col(eps, 1)
    p.op("ACT", lambda e: e.activation(rrow[:], qrow[:], AF.Sqrt, bias=epst[:]),
         reads=("lnqrow", epsk), writes=("lnrrow",))
    p.op("DVE", lambda e: e.reciprocal(rrow[:], rrow[:]), reads=("lnrrow",), writes=("lnrrow",))
    if kind == "ln":
        p.op("PE", lambda e: e.matmul(ps_b[:], c.ones_row_f[:], mrow[:], start=True, stop=True),
             reads=("c_ones_row_f", "lnmrow"), writes=("ps_b",))
    p.op("PE", lambda e: e.matmul(ps_b2[:], c.ones_row_f[:], rrow[:], start=True, stop=True),
         reads=("c_ones_row_f", "lnrrow"), writes=("ps_b2",))
    osl = slice(out_off, out_off + n_tok_tile)
    for i in range(DC):
        xv, xkey = get_x(i, i % 3)
        sq = sq4[i % 3]
        sqk = f"lnsq{i % 3}"
        lt, lkey = ln_sb[i]
        if kind == "ln":
            p.op("DVE", (lambda e, xv=xv, sq=sq: e.tensor_sub(sq[:], xv, ps_b[:])),
                 reads=(xkey, "ps_b"), writes=(sqk,))
            if gain is not None:
                gt, gkey, gc0 = gain
                bt, bkey, bc0 = beta
                p.op("DVE", (lambda e, sq=sq: e.tensor_mul(sq[:], sq[:], ps_b2[:])),
                     reads=(sqk, "ps_b2"), writes=(sqk,))
                p.op("DVE", (lambda e, lt=lt, osl=osl, i=i, sq=sq: e.tensor_scalar(
                        lt[:, osl], sq[:], gt[:, gc0 + i:gc0 + i + 1], bt[:, bc0 + i:bc0 + i + 1],
                        ALU.mult, ALU.add)),
                     reads=(sqk, gkey, bkey), writes=(lkey,))
            else:
                p.op("DVE", (lambda e, lt=lt, osl=osl, sq=sq: e.tensor_mul(lt[:, osl], sq[:], ps_b2[:])),
                     reads=(sqk, "ps_b2"), writes=(lkey,))
        else:
            if gain is not None:
                gt, gkey, gc0 = gain
                p.op("DVE", (lambda e, xv=xv, sq=sq: e.tensor_mul(sq[:], xv, ps_b2[:])),
                     reads=(xkey, "ps_b2"), writes=(sqk,))
                p.op("DVE", (lambda e, lt=lt, osl=osl, i=i, gt=gt, gc0=gc0, sq=sq:
                             e.tensor_scalar_mul(lt[:, osl], sq[:], gt[:, gc0 + i:gc0 + i + 1])),
                     reads=(sqk, gkey), writes=(lkey,))
            else:
                p.op("DVE", (lambda e, xv=xv, lt=lt, osl=osl: e.tensor_mul(lt[:, osl], xv, ps_b2[:])),
                     reads=(xkey, "ps_b2"), writes=(lkey,))


def emit_l1ffn_quant(p, c, wdram, wq_sb, nf, pref):
    acc = p.sbuf(f"{pref}_acc", [128, 1], F32)
    red = p.sbuf(f"{pref}_red", [128, 1], F32)
    wt = [p.sbuf(f"{pref}_wt{k}", [128, TT], F32) for k in range(2)]
    p.op("DVE", lambda e: e.memset(acc[:], 0.0), writes=(f"{pref}_acc",))
    nfb = nf // TT
    for i in range(DC):
        for fb in range(nfb):
            k = (i * nfb + fb) % 2
            p.dma("SP", wt[k][:], wdram[i * 128:(i + 1) * 128, fb * TT:(fb + 1) * TT],
                  writes=(f"{pref}_wt{k}",))
            p.op("DVE", (lambda e, k=k: e.tensor_reduce(red[:], wt[k][:], AX.X, ALU.add,
                                                        apply_absolute_value=True)),
                 reads=(f"{pref}_wt{k}",), writes=(f"{pref}_red",))
            p.op("DVE", lambda e: e.tensor_add(acc[:], acc[:], red[:]),
                 reads=(f"{pref}_red", f"{pref}_acc"), writes=(f"{pref}_acc",))
    return acc, wt


def emit_l1ffn_quant_pass2(p, c, wdram, wq_sb, nf, pref, recip_b, wt):
    tmp = p.sbuf(f"{pref}_qtmp", [128, TT], F32)
    onet = p.sbuf(f"{pref}_ones", [128, TT], F32)
    p._q1_qtmp, p._q1_ones = tmp, onet
    p.op("DVE", lambda e: e.memset(onet[:], 1.0), writes=(f"{pref}_ones",))
    nfb = nf // TT
    for i in range(DC):
        for fb in range(nfb):
            k = (i * nfb + fb) % 2
            fsl = slice(fb * TT, (fb + 1) * TT)
            p.dma("SP", wt[k][:], wdram[i * 128:(i + 1) * 128, fsl], writes=(f"{pref}_wt{k}",))
            p.op("DVE", (lambda e, k=k: e.tensor_scalar(tmp[:], wt[k][:], recip_b[0][:],
                                                        MAGIC_RNE, ALU.mult, ALU.add)),
                 reads=(f"{pref}_wt{k}", recip_b[1]), writes=("q1_qtmp",))
            p.op("DVE", (lambda e: e.scalar_tensor_tensor(tmp[:], tmp[:], MAGIC_RNE, onet[:],
                                                          ALU.subtract, ALU.min)),
                 reads=("q1_qtmp", "q1_ones"), writes=("q1_qtmp",))
            p.op("DVE", (lambda e, i=i, fsl=fsl: e.tensor_scalar_max(wq_sb[i][:, fsl], tmp[:], -1.0)),
                 reads=("q1_qtmp",), writes=(f"{pref}_wq{i}",))


def build_l1ffn_launch(p=None):
    p = p or Prog("l1ffn")
    c = Consts(p)
    NF = 4096
    x_in = p.dram("x_in", [D, TL], F32, kind="ExternalInput").ap()
    w1t = p.dram("w1t", [D, NF], F32, kind="ExternalInput").ap()
    w2t = p.dram("w2t", [NF, D], F32, kind="ExternalInput").ap()
    b1d = p.dram("b1d", [128, NF // 128], F32, kind="ExternalInput").ap()
    b2d = p.dram("b2d", [128, DC], F32, kind="ExternalInput").ap()
    x_out = p.dram("x_out", [D, TL], F32, kind="ExternalOutput").ap()

    wq1 = [p.sbuf(f"q1_wq{i}", [128, NF], BF16) for i in range(DC)]
    wq2 = [p.sbuf(f"q2_wq{i}", [128, D], BF16) for i in range(NF // 128)]

    acc1, wt1 = emit_l1ffn_quant(p, c, w1t, wq1, NF, "q1")
    acc2 = p.sbuf("q2_acc", [128, 1], F32)
    red2 = p.sbuf("q2_red", [128, 1], F32)
    wt2 = [p.sbuf(f"q2_wt{k}", [128, TT], F32) for k in range(2)]
    p.op("DVE", lambda e: e.memset(acc2[:], 0.0), writes=("q2_acc",))
    for i in range(NF // 128):
        for fb in range(D // TT):
            k = (i * (D // TT) + fb) % 2
            p.dma("SP", wt2[k][:], w2t[i * 128:(i + 1) * 128, fb * TT:(fb + 1) * TT],
                  writes=(f"q2_wt{k}",))
            p.op("DVE", (lambda e, k=k: e.tensor_reduce(red2[:], wt2[k][:], AX.X, ALU.add,
                                                        apply_absolute_value=True)),
                 reads=(f"q2_wt{k}",), writes=("q2_red",))
            p.op("DVE", lambda e: e.tensor_add(acc2[:], acc2[:], red2[:]),
                 reads=("q2_red", "q2_acc"), writes=("q2_acc",))

    ps_g = c.ps_s
    grow = p.sbuf("grow", [1, 2], F32)
    gcol = p.sbuf("gcol", [128, 2], F32)
    ps_gb = c.ps_b
    accb = p.sbuf("accb", [128, 2], F32)
    p.op("DVE", lambda e: e.tensor_copy(accb[:, 0:1], acc1[:]), reads=("q1_acc",), writes=("accb",))
    p.op("DVE", lambda e: e.tensor_copy(accb[:, 1:2], acc2[:]), reads=("q2_acc",), writes=("accb",))
    p.op("PE", lambda e: e.matmul(ps_g[0:1, 0:2], c.ones_col_f[:], accb[:], start=True, stop=True),
         reads=("c_ones_col_f", "accb"), writes=("ps_s",))
    p.op("DVE", lambda e: e.tensor_scalar(grow[:], ps_g[0:1, 0:2], 1.0 / (NF * D), 1e-5, ALU.mult, ALU.add),
         reads=("ps_s",), writes=("grow",))
    p.op("DVE", lambda e: e.reciprocal(grow[:], grow[:]), reads=("grow",), writes=("grow",))
    p.op("PE", lambda e: e.matmul(ps_gb[0:128, 0:2], c.ones_row_f[:], grow[:], start=True, stop=True),
         reads=("c_ones_row_f", "grow"), writes=("ps_b",))
    p.op("DVE", lambda e: e.tensor_copy(gcol[:], ps_gb[0:128, 0:2]), reads=("ps_b",), writes=("gcol",))

    emit_l1ffn_quant_pass2(p, c, w1t, wq1, NF, "q1", (gcol[:, 0:1], "gcol"), wt1)
    tmp2 = p._q1_qtmp
    ones2 = p._q1_ones
    for i in range(NF // 128):
        for fb in range(D // TT):
            k = (i * (D // TT) + fb) % 2
            fsl = slice(fb * TT, (fb + 1) * TT)
            p.dma("SP", wt2[k][:], w2t[i * 128:(i + 1) * 128, fsl], writes=(f"q2_wt{k}",))
            p.op("DVE", (lambda e, k=k: e.tensor_scalar(tmp2[:], wt2[k][:], gcol[:, 1:2],
                                                        MAGIC_RNE, ALU.mult, ALU.add)),
                 reads=(f"q2_wt{k}", "gcol"), writes=("q1_qtmp",))
            p.op("DVE", (lambda e: e.scalar_tensor_tensor(tmp2[:], tmp2[:], MAGIC_RNE, ones2[:],
                                                          ALU.subtract, ALU.min)),
                 reads=("q1_qtmp", "q1_ones"), writes=("q1_qtmp",))
            p.op("DVE", (lambda e, i=i, fsl=fsl: e.tensor_scalar_max(wq2[i][:, fsl], tmp2[:], -1.0)),
                 reads=("q1_qtmp",), writes=(f"q2_wq{i}",))

    b1_sb = p.sbuf("b1_sb", [128, NF // 128], F32)
    b2_sb = p.sbuf("b2_sb", [128, DC], F32)
    p.dma("SP", b1_sb[:], b1d, writes=("b1_sb",))
    p.dma("SP", b2_sb[:], b2d, writes=("b2_sb",))

    ln = [p.sbuf(f"ln_ln{i}", [128, TT], BF16) for i in range(DC)]

    hbuf = [p.sbuf(f"h_sb{i}", [128, TT], BF16) for i in range(NF // 128)]
    ps_h = [p.psum(f"ps_h{k}", [128, TT], F32) for k in range(2)]
    ps_o = [p.psum(f"ps_o{k}", [128, TT], F32) for k in range(2)]
    xre = [p.sbuf(f"xre{k}", [128, TT], F32) for k in range(2)]
    ot = [p.sbuf(f"ot{k}", [128, TT], F32) for k in range(2)]
    for tt in range(NTT):
        tsl = slice(tt * TT, (tt + 1) * TT)
        emit_ln_tile(p, c, ("hbm", x_in), [(ln[i], f"ln_ln{i}") for i in range(DC)], tt, "ln")
        for fi in range(NF // 128):
            k = fi % 2
            for di in range(DC):
                p.op("PE", (lambda e, fi=fi, di=di, tsl=tsl, k=k:
                            e.matmul(ps_h[k][:], wq1[di][:, fi * 128:(fi + 1) * 128],
                                     ln[di][:], start=(di == 0), stop=(di == DC - 1))),
                     reads=(f"q1_wq{di}", f"ln_ln{di}"), writes=(f"ps_h{k}",))
            p.op("ACT", (lambda e, fi=fi, k=k: e.activation(hbuf[fi][:], ps_h[k][:], AF.Relu,
                                                            bias=b1_sb[:, fi:fi + 1])),
                 reads=(f"ps_h{k}", "b1_sb"), writes=(f"h_sb{fi}",))
        for oi in range(DC):
            k = oi % 2
            for fi in range(NF // 128):
                p.op("PE", (lambda e, oi=oi, fi=fi, k=k:
                            e.matmul(ps_o[k][:], wq2[fi][:, oi * 128:(oi + 1) * 128],
                                     hbuf[fi][:], start=(fi == 0), stop=(fi == NF // 128 - 1))),
                     reads=(f"q2_wq{fi}", f"h_sb{fi}"), writes=(f"ps_o{k}",))
            p.dma("SP", xre[k][:], x_in[oi * 128:(oi + 1) * 128, tsl], writes=(f"xre{k}",))
            p.op("DVE", (lambda e, oi=oi, k=k: e.tensor_scalar_add(ot[k][:], ps_o[k][:],
                                                                   b2_sb[:, oi:oi + 1])),
                 reads=(f"ps_o{k}", "b2_sb"), writes=(f"ot{k}",))
            p.op("DVE", (lambda e, k=k: e.scalar_tensor_tensor(ot[k][:], ot[k][:], 0.5, xre[k][:],
                                                               ALU.mult, ALU.add)),
                 reads=(f"ot{k}", f"xre{k}"), writes=(f"ot{k}",))
            p.dma("SP", x_out[oi * 128:(oi + 1) * 128, tsl], ot[k][:],
                  reads=(f"ot{k}",), writes=(f"xo_{oi}_{tt}",))
    p.final_wait([f"xo_{oi}_{tt}" for oi in range(DC) for tt in range(NTT)])
    return p


def emit_moe_prep(p, c, wg_dram, bg_dram):
    wgf = p.sbuf("wgf_sb", [128, DC * 8], F32)
    for i in range(DC):
        p.dma("SP", wgf[:, i * 8:(i + 1) * 8], wg_dram[i * 128:(i + 1) * 128, :], writes=("wgf_sb",))
    colsum_b = p.sbuf("colsum_b", [128, 8], F32)
    crow = p.sbuf("crow", [1, 8], F32)
    for i in range(DC):
        p.op("PE", (lambda e, i=i: e.matmul(c.ps_b[0:1, 0:8], c.ones_col_f[:],
                                            wgf[:, i * 8:(i + 1) * 8],
                                            start=(i == 0), stop=(i == DC - 1))),
             reads=("c_ones_col_f", "wgf_sb"), writes=("ps_b",))
    p.op("DVE", lambda e: e.tensor_copy(crow[:], c.ps_b[0:1, 0:8]), reads=("ps_b",), writes=("crow",))
    p.op("PE", lambda e: e.matmul(c.ps_b[0:128, 0:8], c.ones_row_f[:], crow[:], start=True, stop=True),
         reads=("c_ones_row_f", "crow"), writes=("ps_b",))
    p.op("DVE", lambda e: e.tensor_copy(colsum_b[:], c.ps_b[0:128, 0:8]),
         reads=("ps_b",), writes=("colsum_b",))
    bg_b = None
    if bg_dram is not None:
        bg_row = p.sbuf("bg_row", [1, 8], F32)
        bg_b = p.sbuf("bg_b", [128, 8], F32)
        p.dma("SP", bg_row[:], bg_dram, writes=("bg_row",))
        p.op("PE", lambda e: e.matmul(c.ps_b[0:128, 0:8], c.ones_row_f[:], bg_row[:],
                                      start=True, stop=True),
             reads=("c_ones_row_f", "bg_row"), writes=("ps_b",))
        p.op("DVE", lambda e: e.tensor_copy(bg_b[:], c.ps_b[0:128, 0:8]),
             reads=("ps_b",), writes=("bg_b",))
    return wgf, colsum_b, bg_b


def emit_moe_gates(p, c, x_src, wgf, colsum_b, bg_b, tt, renorm, gfm_sb, gfm_bf=None):
    g = getattr(p, "_moe", None)
    if g is None:
        g = {}
        g["xb"] = [p.sbuf(f"moe_xb{k}", [128, TT], F32) for k in range(2)]
        g["gt"] = p.sbuf("moe_gt", [128, 8], F32)
        g["m1"] = p.sbuf("moe_m1", [128, 1], F32)
        g["m2"] = p.sbuf("moe_m2", [128, 1], F32)
        g["nm"] = p.sbuf("moe_nm", [128, 1], F32)
        g["sm"] = p.sbuf("moe_sm", [128, 1], F32)
        g["pr"] = p.sbuf("moe_pr", [128, 8], F32)
        g["msk"] = p.sbuf("moe_msk", [128, 8], F32)
        g["mr"] = p.sbuf("moe_mr", [128, 2], F32)
        g["nrow"] = p.sbuf("moe_nrow", [1, TT], F32)
        g["ident"] = p.sbuf("moe_ident", [128, 128], F32)
        p.op("POOL", lambda e: e.memset(g["ident"][:], 1.0), writes=("moe_ident",))
        p.op("POOL", lambda e: e.affine_select(out=g["ident"][:], in_=g["ident"][:],
                                               pattern=[[-1, 128]], compare_op=ALU.is_equal,
                                               fill=0.0, base=0, channel_multiplier=1),
             reads=("moe_ident",), writes=("moe_ident",))
        p._moe = g
    gt, m1, m2, nm, sm, pr, msk = g["gt"], g["m1"], g["m2"], g["nm"], g["sm"], g["pr"], g["msk"]
    mrow, qrow, rrow = p._lnrows
    nrow, mr, ps_t = g["nrow"], g["mr"], c.ps_b2
    gf, gfk = gfm_sb
    mode, x = x_src
    assert mode == "hbm"
    p.op("DVE", lambda e: e.tensor_scalar_mul(nrow[:], mrow[:], -1.0),
         reads=("lnmrow",), writes=("moe_nrow",))
    for i in range(DC):
        k = i % 2
        p.dma("SP", g["xb"][k][:], x[i * 128:(i + 1) * 128, tt * TT:(tt + 1) * TT],
              writes=(f"moe_xb{k}",))
        for j in range(TT // 128):
            p.op("PE", (lambda e, i=i, j=j, k=k:
                        e.matmul(ps_t[0:128, j * 8:j * 8 + 8],
                                 g["xb"][k][:, j * 128:(j + 1) * 128],
                                 wgf[:, i * 8:(i + 1) * 8],
                                 start=(i == 0 and j == 0), stop=(i == DC - 1))),
                 reads=(f"moe_xb{k}", "wgf_sb"), writes=("ps_b2",))
    for j in range(TT // 128):
        t0 = tt * TT + j * 128
        p.op("PE", (lambda e, j=j: e.transpose(ps_t[0:128, 34:35], nrow[0:1, j * 128:(j + 1) * 128],
                                               g["ident"][0:1, 0:1])),
             reads=("moe_nrow", "moe_ident"), writes=("ps_b2",))
        p.op("PE", (lambda e, j=j: e.transpose(ps_t[0:128, 35:36], rrow[0:1, j * 128:(j + 1) * 128],
                                               g["ident"][0:1, 0:1])),
             reads=("lnrrow", "moe_ident"), writes=("ps_b2",))
        p.op("DVE", lambda e: e.tensor_copy(mr[:], ps_t[0:128, 34:36]),
             reads=("ps_b2",), writes=("moe_mr",))
        p.op("DVE", (lambda e, j=j: e.scalar_tensor_tensor(gt[:], colsum_b[:], mr[:, 0:1],
                                                           ps_t[0:128, j * 8:j * 8 + 8],
                                                           ALU.mult, ALU.add)),
             reads=("colsum_b", "moe_mr", "ps_b2"), writes=("moe_gt",))
        if bg_b is not None:
            p.op("DVE", (lambda e: e.scalar_tensor_tensor(gt[:], gt[:], mr[:, 1:2], bg_b[0][:],
                                                          ALU.mult, ALU.add)),
                 reads=("moe_gt", "moe_mr", bg_b[1]), writes=("moe_gt",))
        else:
            p.op("DVE", (lambda e: e.tensor_scalar_mul(gt[:], gt[:], mr[:, 1:2])),
                 reads=("moe_gt", "moe_mr"), writes=("moe_gt",))
        p.op("DVE", lambda e: e.tensor_reduce(nm[:], gt[:], AX.X, ALU.max, negate=True),
             reads=("moe_gt",), writes=("moe_nm",))
        p.op("DVE", lambda e: e.tensor_reduce(m1[:], gt[:], AX.X, ALU.max),
             reads=("moe_gt",), writes=("moe_m1",))
        p.op("ACT", lambda e: e.activation(pr[:], gt[:], AF.Exp, bias=nm[:], accum_out=sm[:]),
             reads=("moe_gt", "moe_nm"), writes=("moe_pr", "moe_sm"))
        p.op("DVE", lambda e: e.tensor_scalar(msk[:], gt[:], m1[:], None, ALU.is_equal),
             reads=("moe_gt", "moe_m1"), writes=("moe_msk",))
        p.op("DVE", lambda e: e.scalar_tensor_tensor(msk[:], msk[:], -1e30, gt[:], ALU.mult, ALU.add),
             reads=("moe_msk", "moe_gt"), writes=("moe_msk",))
        p.op("DVE", lambda e: e.tensor_reduce(m2[:], msk[:], AX.X, ALU.max),
             reads=("moe_msk",), writes=("moe_m2",))
        p.op("DVE", lambda e: e.tensor_scalar(msk[:], gt[:], m2[:], None, ALU.is_ge),
             reads=("moe_gt", "moe_m2"), writes=("moe_msk",))
        p.op("DVE", lambda e: e.tensor_mul(pr[:], pr[:], msk[:]),
             reads=("moe_pr", "moe_msk"), writes=("moe_pr",))
        if renorm:
            p.op("DVE", lambda e: e.tensor_reduce(sm[:], pr[:], AX.X, ALU.add),
                 reads=("moe_pr",), writes=("moe_sm",))
        p.op("DVE", lambda e: e.reciprocal(sm[:], sm[:]), reads=("moe_sm",), writes=("moe_sm",))
        p.op("DVE", lambda e: e.tensor_scalar_mul(pr[:], pr[:], sm[:]),
             reads=("moe_pr", "moe_sm"), writes=("moe_pr",))
        p.op("PE", (lambda e: e.transpose(ps_t[0:8, 36:36 + 128], pr[:, 0:8], g["ident"][:])),
             reads=("moe_pr", "moe_ident"), writes=("ps_b2",))
        p.op("DVE", (lambda e, t0=t0: e.tensor_copy(gf[0:8, t0:t0 + 128], ps_t[0:8, 36:36 + 128])),
             reads=("ps_b2",), writes=(gfk,))
        if gfm_bf is not None:
            gb_, gbk_ = gfm_bf
            p.op("DVE", (lambda e, t0=t0: e.tensor_copy(gb_[0:8, t0:t0 + 128], ps_t[0:8, 36:36 + 128])),
                 reads=("ps_b2",), writes=(gbk_,))


def build_l0ffn_launch(p=None):
    p = p or Prog("l0ffn")
    c = Consts(p)
    x_in = p.dram("x_in", [D, TL], F32, kind="ExternalInput").ap()
    wr = p.dram("wr", [D, 8], F32, kind="ExternalInput").ap()
    wi = p.dram("wi", [8, D, D], F32, kind="ExternalInput").ap()
    wo = p.dram("wo", [8, D, D], F32, kind="ExternalInput").ap()
    x_out = p.dram("x_out", [D, TL], F32, kind="ExternalOutput").ap()

    sel8d = p.dram("sel8", [8, 1024], F32, kind="ExternalInput").ap()
    sel8 = p.sbuf("sel8_sb", [8, 1024], BF16)
    p.dma("POOL", sel8[:], sel8d, writes=("sel8_sb",))
    wgf, colsum_b, _bg = emit_moe_prep(p, c, wr, None)

    ln = [p.sbuf(f"ln{i}", [128, TL], BF16) for i in range(DC)]
    lnkeys = [(ln[i], f"ln{i}") for i in range(DC)]
    gfm = p.sbuf("gfm", [8, TL], BF16)
    for tt in range(NTT):
        emit_ln_tile(p, c, ("hbm", x_in), lnkeys, tt, "ln", out_off=tt * TT)
        emit_moe_gates(p, c, ("hbm", x_in), wgf, colsum_b, None, tt, False, (gfm, "gfm"))

    oacc = [p.sbuf(f"oacc{i}", [128, TL], F32) for i in range(DC)]
    wie_all = [p.sbuf(f"wie_all{k}", [128, DC * D], BF16) for k in range(2)]
    woe_all = [p.sbuf(f"woe_all{k}", [128, DC * D], BF16) for k in range(2)]
    wie = [[wie_all[k][:, i * D:(i + 1) * D] for i in range(DC)] for k in range(2)]
    woe = [[woe_all[k][:, i * D:(i + 1) * D] for i in range(DC)] for k in range(2)]
    hsb = [p.sbuf(f"hsb{i}", [128, TT], BF16) for i in range(DC)]
    gb_sb = p.sbuf("gb_sb", [128, TT], F32)
    ps_h = [p.psum(f"ps_h{k}", [128, TT], F32) for k in range(2)]
    ps_o = [p.psum(f"ps_o{k}", [128, TT], F32) for k in range(2)]
    for e in range(8):
        k = e % 2
        wie_ap = bass.AP(tensor=wi.tensor, offset=wi.offset + e * D * D,
                         ap=[[D, 128], [128 * D, DC], [1, D]])
        woe_ap = bass.AP(tensor=wo.tensor, offset=wo.offset + e * D * D,
                         ap=[[D, 128], [128 * D, DC], [1, D]])
        p.dma("POOL", wie_all[k][:], wie_ap, writes=tuple(f"wie{k}_{i}" for i in range(DC)))
        p.dma("POOL", woe_all[k][:], woe_ap, writes=tuple(f"woe{k}_{i}" for i in range(DC)))
        for tt in range(NTT):
            tsl = slice(tt * TT, (tt + 1) * TT)
            p.op("PE", (lambda e_, e=e, tsl=tsl: e_.matmul(c.ps_b[:], sel8[:, e * 128:(e + 1) * 128],
                                                           gfm[0:8, tsl], start=True, stop=True)),
                 reads=("sel8_sb", "gfm"), writes=("ps_b",))
            p.op("DVE", lambda e_: e_.tensor_copy(gb_sb[:], c.ps_b[:]),
                 reads=("ps_b",), writes=("gb_sb",))
            for fi in range(DC):
                kh = fi % 2
                for di in range(DC):
                    p.op("PE", (lambda e_, fi=fi, di=di, k=k, kh=kh, tsl=tsl:
                                e_.matmul(ps_h[kh][:], wie[k][di][0:128, fi * 128:(fi + 1) * 128],
                                          ln[di][:, tsl], start=(di == 0), stop=(di == DC - 1))),
                         reads=(f"wie{k}_{di}", f"ln{di}"), writes=(f"ps_h{kh}",))
                p.op("DVE", (lambda e_, fi=fi, kh=kh: e_.scalar_tensor_tensor(
                        hsb[fi][:], ps_h[kh][:], 1.0, gb_sb[:], ALU.mult, ALU.mult)),
                     reads=(f"ps_h{kh}", "gb_sb"), writes=(f"hsb{fi}",))
            for oi in range(DC):
                ko = oi % 2
                for fi in range(DC):
                    p.op("PE", (lambda e_, oi=oi, fi=fi, k=k, ko=ko:
                                e_.matmul(ps_o[ko][:], woe[k][fi][0:128, oi * 128:(oi + 1) * 128],
                                          hsb[fi][:], start=(fi == 0), stop=(fi == DC - 1))),
                         reads=(f"woe{k}_{fi}", f"hsb{fi}"), writes=(f"ps_o{ko}",))
                if e == 0:
                    p.op("DVE", (lambda e_, oi=oi, ko=ko, tsl=tsl:
                                 e_.tensor_copy(oacc[oi][:, tsl], ps_o[ko][:])),
                         reads=(f"ps_o{ko}",), writes=(f"oacc{oi}",))
                else:
                    p.op("DVE", (lambda e_, oi=oi, ko=ko, tsl=tsl:
                                 e_.tensor_add(oacc[oi][:, tsl], oacc[oi][:, tsl], ps_o[ko][:])),
                         reads=(f"ps_o{ko}", f"oacc{oi}"), writes=(f"oacc{oi}",))
    xre = [p.sbuf(f"xre{k}", [128, TT], F32) for k in range(2)]
    outs = []
    for oi in range(DC):
        for tt in range(NTT):
            k = (oi * NTT + tt) % 2
            tsl = slice(tt * TT, (tt + 1) * TT)
            p.dma("SP", xre[k][:], x_in[oi * 128:(oi + 1) * 128, tsl], writes=(f"xre{k}",))
            p.op("DVE", (lambda e_, oi=oi, k=k, tsl=tsl: e_.scalar_tensor_tensor(
                    xre[k][:], oacc[oi][:, tsl], 0.5, xre[k][:], ALU.mult, ALU.add)),
                 reads=(f"oacc{oi}", f"xre{k}"), writes=(f"xre{k}",))
            p.dma("SP", x_out[oi * 128:(oi + 1) * 128, tsl], xre[k][:],
                  reads=(f"xre{k}",), writes=(f"xo_{oi}_{tt}",))
            outs.append(f"xo_{oi}_{tt}")
    p.final_wait(outs)
    return p


def build_l3ffn_launch(p=None):
    p = p or Prog("l3ffn")
    c = Consts(p)
    x_in = p.dram("x_in", [D, TL], F32, kind="ExternalInput").ap()
    wg = p.dram("wgd", [D, 8], F32, kind="ExternalInput").ap()
    bg = p.dram("bgd", [1, 8], F32, kind="ExternalInput").ap()
    wet = p.dram("wet", [8, D, D], F32, kind="ExternalInput").ap()
    be = p.dram("bed", [8, D], F32, kind="ExternalInput").ap()
    sel8d = p.dram("sel8", [8, 1024], F32, kind="ExternalInput").ap()
    x_out = p.dram("x_out", [D, TL], F32, kind="ExternalOutput").ap()

    sel8 = p.sbuf("sel8_sb", [8, 1024], BF16)
    p.dma("POOL", sel8[:], sel8d, writes=("sel8_sb",))
    be_sb = p.sbuf("be_sb", [8, D], BF16)
    p.dma("POOL", be_sb[:], be, writes=("be_sb",))
    wgf, colsum_b, bg_b = emit_moe_prep(p, c, wg, bg)

    ln = [p.sbuf(f"ln{i}", [128, TT], BF16) for i in range(DC)]
    gfm = p.sbuf("gfm", [8, TL], BF16)
    gfm_bf = gfm
    lne = [[p.sbuf(f"lne{k}_{i}", [128, TT], BF16) for i in range(DC)] for k in range(2)]
    wes_all = [p.sbuf(f"wes_all{k}", [128, DC * D], BF16) for k in range(2)]
    wes = [[wes_all[k][:, i * D:(i + 1) * D] for i in range(DC)] for k in range(2)]
    oacc = [p.sbuf(f"oacc{i}", [128, TT], F32) for i in range(DC)]
    gb_sb = p.sbuf("gb_sb", [128, TT], F32)
    ps_o = [p.psum(f"ps_o{k}", [128, TT], F32) for k in range(2)]
    xre = [p.sbuf(f"xre{k}", [128, TT], F32) for k in range(2)]
    outs = []
    for tt in range(NTT):
        tsl = slice(tt * TT, (tt + 1) * TT)
        emit_ln_tile(p, c, ("hbm", x_in), [(ln[i], f"ln{i}") for i in range(DC)], tt, "ln",
                     out_off=0, n_tok_tile=TT)
        emit_moe_gates(p, c, ("hbm", x_in), wgf, colsum_b, (bg_b, "bg_b"), tt, True,
                       (gfm, "gfm"))
        for e in range(8):
            k = e % 2
            wes_ap = bass.AP(tensor=wet.tensor, offset=wet.offset + e * D * D,
                             ap=[[D, 128], [128 * D, DC], [1, D]])
            p.dma("POOL", wes_all[k][:], wes_ap, writes=tuple(f"wes{k}_{i}" for i in range(DC)))
            p.op("PE", (lambda e_, e=e, tsl=tsl: e_.matmul(c.ps_b[:], sel8[:, e * 128:(e + 1) * 128],
                                                           gfm[0:8, tsl], start=True, stop=True)),
                 reads=("sel8_sb", "gfm"), writes=("ps_b",))
            p.op("DVE", lambda e_: e_.tensor_copy(gb_sb[:], c.ps_b[:]),
                 reads=("ps_b",), writes=("gb_sb",))
            for i in range(DC):
                p.op("DVE", (lambda e_, k=k, i=i: e_.tensor_mul(lne[k][i][:], ln[i][:], gb_sb[:])),
                     reads=(f"ln{i}", "gb_sb"), writes=(f"lne{k}_{i}",))
            for oi in range(DC):
                ko = oi % 2
                for di in range(DC):
                    p.op("PE", (lambda e_, k=k, di=di, oi=oi, ko=ko, e=e:
                                e_.matmul(ps_o[ko][:], wes[k][di][0:128, oi * 128:(oi + 1) * 128],
                                          lne[k][di][:], start=(di == 0),
                                          stop=(di == DC - 1 and e != 7))),
                         reads=(f"wes{k}_{di}", f"lne{k}_{di}"), writes=(f"ps_o{ko}",))
                if e == 7:
                    p.op("PE", (lambda e_, oi=oi, ko=ko, tsl=tsl:
                                e_.matmul(ps_o[ko][:], be_sb[:, oi * 128:(oi + 1) * 128],
                                          gfm_bf[0:8, tsl], start=False, stop=True)),
                         reads=("be_sb", "gfm_bf"), writes=(f"ps_o{ko}",))
                if e == 0:
                    p.op("DVE", (lambda e_, oi=oi, ko=ko: e_.tensor_copy(oacc[oi][:], ps_o[ko][:])),
                         reads=(f"ps_o{ko}",), writes=(f"oacc{oi}",))
                else:
                    p.op("DVE", (lambda e_, oi=oi, ko=ko: e_.tensor_add(oacc[oi][:], oacc[oi][:], ps_o[ko][:])),
                         reads=(f"ps_o{ko}", f"oacc{oi}"), writes=(f"oacc{oi}",))
        for oi in range(DC):
            ko = oi % 2
            p.dma("SP", xre[ko][:], x_in[oi * 128:(oi + 1) * 128, tsl], writes=(f"xre{ko}",))
            p.op("DVE", (lambda e_, oi=oi, ko=ko: e_.scalar_tensor_tensor(
                    xre[ko][:], oacc[oi][:], 0.5, xre[ko][:], ALU.mult, ALU.add)),
                 reads=(f"oacc{oi}", f"xre{ko}"), writes=(f"xre{ko}",))
            p.dma("SP", x_out[oi * 128:(oi + 1) * 128, tsl], xre[ko][:],
                  reads=(f"xre{ko}",), writes=(f"xo_{oi}_{tt}",))
            outs.append(f"xo_{oi}_{tt}")
    p.final_wait(outs)
    return p


def build_l3attn_launch(p=None):
    p = p or Prog("l3attn")
    c = Consts(p)
    NH = 16
    x_in = p.dram("x_in", [D, TL], F32, kind="ExternalInput").ap()
    wqk = p.dram("wqk", [NH, D, 128], F32, kind="ExternalInput").ap()
    wv = p.dram("wv", [D, D], F32, kind="ExternalInput").ap()
    bqk = p.dram("bqk", [128, NH], F32, kind="ExternalInput").ap()
    bv = p.dram("bv", [1, D], F32, kind="ExternalInput").ap()
    wout = p.dram("wout", [D, D], F32, kind="ExternalInput").ap()
    bout = p.dram("boutd", [128, DC], F32, kind="ExternalInput").ap()
    moff = p.dram("moff", [1, 1], F32, kind="ExternalInput").ap()
    x_out = p.dram("x_out", [D, TL], F32, kind="ExternalOutput").ap()
    qs = p.dram("qs", [NH * 64, TL], BF16)
    ks = p.dram("ks", [NH * 64, TL], BF16)
    vs = p.dram("vs", [TL, D], BF16)
    kg = p.dram("kg", [2 * NH * 64, TL], BF16)
    vg = p.dram("vg", [2 * TL, D], BF16)

    onesb = p.sbuf("onesb", [128, 1], BF16)
    onesrb = p.sbuf("onesrb", [1, 128], BF16)
    p.op("DVE", lambda e: e.memset(onesb[:], 1.0), writes=("onesb",))
    p.op("DVE", lambda e: e.memset(onesrb[:], 1.0), writes=("onesrb",))
    eighth = c.col(0.125)
    moffc = p.sbuf("moffc", [1, 1], F32)
    p.dma("SP", moffc[:], moff, writes=("moffc",))
    bqk_sb = p.sbuf("bqk_sb", [128, NH], F32)
    p.dma("SP", bqk_sb[:], bqk, writes=("bqk_sb",))
    bqk8 = p.sbuf("bqk8", [128, NH], F32)
    p.op("DVE", lambda e: e.tensor_scalar_mul(bqk8[:], bqk_sb[:], 0.125),
         reads=("bqk_sb",), writes=("bqk8",))
    bv_row = p.sbuf("bv_row", [1, D], F32)
    bv_b = p.sbuf("bv_b", [128, D], F32)
    p.dma("SP", bv_row[:], bv, writes=("bv_row",))
    for j in range(2):
        p.op("PE", (lambda e, j=j: e.matmul(c.ps_b[:], c.ones_row_f[:], bv_row[0:1, j * TT:(j + 1) * TT],
                                            start=True, stop=True)),
             reads=("c_ones_row_f", "bv_row"), writes=("ps_b",))
        p.op("DVE", (lambda e, j=j: e.tensor_copy(bv_b[:, j * TT:(j + 1) * TT], c.ps_b[:])),
             reads=("ps_b",), writes=("bv_b",))

    ln = [p.sbuf(f"ln{i}", [128, TL], BF16) for i in range(DC)]
    lnkeys = [(ln[i], f"ln{i}") for i in range(DC)]
    for tt in range(NTT):
        emit_ln_tile(p, c, ("hbm", x_in), lnkeys, tt, "ln", out_off=tt * TT)

    wqk_sb = [p.sbuf(f"wqk_sb{k}", [128, DC * 128], BF16) for k in range(2)]
    ps_qk = [p.psum(f"ps_qk{k}", [128, TT], F32) for k in range(2)]
    qkev = p.sbuf("qkev", [128, TT], BF16)
    for h in range(NH):
        k = h % 2
        qk_ap = bass.AP(tensor=wqk.tensor, offset=wqk.offset + h * D * 128,
                        ap=[[128, 128], [128 * 128, DC], [1, 128]])
        p.dma("POOL", wqk_sb[k][:], qk_ap, writes=(f"wqk_sb{k}",))
        for tt in range(NTT):
            tsl = slice(tt * TT, (tt + 1) * TT)
            for i in range(DC):
                p.op("PE", (lambda e, k=k, i=i, tt=tt: e.matmul(
                        ps_qk[k][:], wqk_sb[k][:, i * 128:(i + 1) * 128], ln[i][:, tt * TT:(tt + 1) * TT],
                        start=(i == 0), stop=(i == DC - 1))),
                     reads=(f"wqk_sb{k}", f"ln{i}"), writes=(f"ps_qk{k}",))
            p.op("DVE", (lambda e, k=k, h=h: e.tensor_scalar(qkev[0:64, :], ps_qk[k][0:64, :],
                                                             bqk8[0:64, h:h + 1], 0.125,
                                                             ALU.add, ALU.mult)),
                 reads=(f"ps_qk{k}", "bqk8"), writes=("qkev",))
            p.op("DVE", (lambda e, k=k, h=h: e.tensor_scalar_add(qkev[64:128, :], ps_qk[k][64:128, :],
                                                                 bqk_sb[64:128, h:h + 1])),
                 reads=(f"ps_qk{k}", "bqk_sb", "qkev"), writes=("qkev",))
            p.dma("SP", qs[h * 64:(h + 1) * 64, tsl], qkev[0:64, :], reads=("qkev",), writes=("qs",))
            p.dma("SP", ks[h * 64:(h + 1) * 64, tsl], qkev[64:128, :], reads=("qkev",), writes=("ks",))
    wv_sb = p.sbuf("wv_sb", [128, DC * D], BF16)
    load_wchunks(p, "POOL", wv_sb[:], "wv_sb", wv, D, 0, DC, 0, D)
    vev = p.sbuf("vev", [128, D], BF16)
    for ts in range(TL // 128):
        k = ts % 2
        for half in range(2):
            for i in range(DC):
                p.op("PE", (lambda e, k=k, i=i, ts=ts, half=half: e.matmul(
                        ps_qk[k][:], ln[i][:, ts * 128:(ts + 1) * 128],
                        wv_sb[:, i * D + half * TT:i * D + (half + 1) * TT],
                        start=(i == 0), stop=(i == DC - 1))),
                     reads=("wv_sb", f"ln{i}"), writes=(f"ps_qk{k}",))
            p.op("DVE", (lambda e, k=k, half=half: e.tensor_add(vev[:, half * TT:(half + 1) * TT],
                                                                ps_qk[k][:], bv_b[:, half * TT:(half + 1) * TT])),
                 reads=(f"ps_qk{k}", "bv_b"), writes=("vev",))
        p.dma("SP", vs[ts * 128:(ts + 1) * 128, :], vev[:], reads=("vev",), writes=("vs",))

    p.collective("AllGather", ALU.bypass, [[0, 1], [2, 3], [4, 5], [6, 7]],
                 [ks.ap()], [kg.ap()], reads=("ks",), writes=("kg",))
    p.collective("AllGather", ALU.bypass, [[0, 1], [2, 3], [4, 5], [6, 7]],
                 [vs.ap()], [vg.ap()], reads=("vs",), writes=("vg",))

    ps_sc = ps_qk
    ps_av = c.ps_b
    ps_sum = c.ps_s
    q_sb = p.sbuf("q_sb", [64, TL], BF16)
    kl_sb = p.sbuf("kl_sb", [64, TL], BF16)
    kr_sb = p.sbuf("kr_sb", [64, TL], BF16)
    vl_sb = p.sbuf("vl_sb", [128, (TL // 128) * 64], BF16)
    vr_sb = p.sbuf("vr_sb", [128, (TL // 128) * 64], BF16)
    sqt = p.sbuf("sqt", [64, TL], BF16)
    n2r = p.sbuf("n2r", [1, TL], F32)
    nbl = p.sbuf("nbl", [1, TL], BF16)
    nbr = p.sbuf("nbr", [1, TL], BF16)
    mk_s = p.sbuf("mk_s", [1, 1], F32)
    NPT = 16
    pt_sb = [p.sbuf(f"pt_sb{k}", [128, TT], BF16) for k in range(NPT)]
    rs_row = p.sbuf("rs_row", [1, TT], F32)
    rb_sb = p.sbuf("rb_sb", [64, TT], F32)
    att = [p.sbuf(f"att{i}", [128, TL], BF16) for i in range(DC)]
    NT128 = TL // 128

    for h in range(NH):
        p.dma("SP", q_sb[:], qs[h * 64:(h + 1) * 64, :], reads=("qs",), writes=("q_sb",))
        p.dma("SP", kl_sb[:], ks[h * 64:(h + 1) * 64, :], reads=("ks",), writes=("kl_sb",))
        p.dma("SP", kr_sb[:], kg[h * 64:(h + 1) * 64, :], reads=("kg",), writes=("kr_sb",))
        vl_ap = bass.AP(tensor=vs.ap().tensor, offset=h * 64,
                        ap=[[D, 128], [128 * D, TL // 128], [1, 64]])
        vr_ap = bass.AP(tensor=vg.ap().tensor, offset=h * 64,
                        ap=[[D, 128], [128 * D, TL // 128], [1, 64]])
        p.dma("POOL", vl_sb[:], vl_ap, reads=("vs",), writes=("vl_sb",))
        p.dma("POOL", vr_sb[:], vr_ap, reads=("vg",), writes=("vr_sb",))
        for which, ksb, kkey, off in ((0, kl_sb, "kl_sb", 0), (1, kr_sb, "kr_sb", TL)):
            p.op("ACT", (lambda e, ksb=ksb: e.activation(sqt[:], ksb[:], AF.Square)),
                 reads=(kkey,), writes=("sqt",))
            for j in range(NTT):
                p.op("PE", (lambda e, j=j: e.matmul(ps_sum[:], onesb[0:64, :], sqt[:, j * TT:(j + 1) * TT],
                                                    start=True, stop=True)),
                     reads=("onesb", "sqt"), writes=("ps_s",))
                p.op("DVE", (lambda e, j=j, which=which: e.tensor_copy(
                        n2r[0:1, (j * TT):(j + 1) * TT] if which == 0 else n2r[0:1, j * TT:(j + 1) * TT],
                        ps_sum[:])),
                     reads=("ps_s",), writes=("n2r_a" if which == 0 else "n2r_b",))
            p.op("DVE", (lambda e, which=which: e.tensor_reduce(mk_s[:], n2r[:], AX.X, ALU.max)),
                 reads=("n2r_a", "n2r_b"), writes=(f"mk{which}",))
            if which == 0:
                if not hasattr(p, "_mk0"):
                    p._mk0 = p.sbuf("mk0_sb", [1, 1], F32)
                mk0 = p._mk0
                p.op("DVE", lambda e: e.tensor_copy(mk0[:], mk_s[:]), reads=("mk0",), writes=("mk0_sb",))
        p.op("DVE", lambda e: e.tensor_max(mk_s[:], mk_s[:], mk0[:]), reads=("mk1", "mk0_sb"), writes=("mk_c",))
        p.op("ACT", lambda e: e.activation(mk_s[:], mk_s[:], AF.Sqrt), reads=("mk_c",), writes=("mk_c",))
        p.op("DVE", lambda e: e.tensor_scalar_mul(mk_s[:], mk_s[:], -1.0), reads=("mk_c",), writes=("mk_c",))
        p.op("ACT", lambda e: e.activation(sqt[:], q_sb[:], AF.Square), reads=("q_sb",), writes=("sqt",))
        for j in range(NTT):
            p.op("PE", (lambda e, j=j: e.matmul(ps_sum[:], onesb[0:64, :], sqt[:, j * TT:(j + 1) * TT],
                                                start=True, stop=True)),
                 reads=("onesb", "sqt"), writes=("ps_s",))
            p.op("ACT", (lambda e, j=j: e.activation(n2r[0:1, j * TT:(j + 1) * TT], ps_sum[:], AF.Sqrt)),
                 reads=("ps_s",), writes=("qn_r",))
        p.op("DVE", lambda e: e.tensor_scalar_mul(nbl[:], n2r[:], mk_s[:]),
             reads=("qn_r", "mk_c"), writes=("nbl",))
        p.op("DVE", lambda e: e.tensor_scalar(nbr[:], n2r[:], mk_s[:], moffc[:], ALU.mult, ALU.add),
             reads=("qn_r", "mk_c", "moffc"), writes=("nbr",))
        for tqt in range(NTT):
            tql = slice(tqt * TT, (tqt + 1) * TT)
            blocks = [("r", i) for i in range(NT128)] + [("l", i) for i in range((tqt + 1) * (TT // 128))]
            first = True
            for bi, (src, tk) in enumerate(blocks):
                kk = bi % 2
                ksb, kkey = (kr_sb, "kr_sb") if src == "r" else (kl_sb, "kl_sb")
                vsb, vkey = (vr_sb, "vr_sb") if src == "r" else (vl_sb, "vl_sb")
                nb, nbkey = (nbr, "nbr") if src == "r" else (nbl, "nbl")
                p.op("PE", (lambda e, kk=kk, ksb=ksb, tk=tk, tql=tql: e.matmul(
                        ps_sc[kk][:], ksb[:, tk * 128:(tk + 1) * 128], q_sb[:, tql],
                        start=True, stop=False)),
                     reads=(kkey, "q_sb"), writes=(f"ps_qk{kk}",))
                p.op("PE", (lambda e, kk=kk, nb=nb, tql=tql: e.matmul(
                        ps_sc[kk][:], onesrb[0:1, :], nb[0:1, tql], start=False, stop=True)),
                     reads=("onesrb", nbkey), writes=(f"ps_qk{kk}",))
                p.op("ACT", (lambda e, kk=kk: e.activation(pt_sb[kk][:], ps_sc[kk][:], AF.Exp)),
                     reads=(f"ps_qk{kk}",), writes=(f"pt_sb{kk}",))
                if src == "l" and tk * 128 >= tqt * TT:
                    p.op("POOL", (lambda e, kk=kk, tqt=tqt, tk=tk: e.affine_select(
                            out=pt_sb[kk][:], in_=pt_sb[kk][:], pattern=[[1, TT]],
                            compare_op=ALU.is_ge, fill=0.0, base=tqt * TT - tk * 128,
                            channel_multiplier=-1)),
                         reads=(f"pt_sb{kk}",), writes=(f"pt_sb{kk}",))
                p.op("PE", (lambda e, kk=kk, first=first, last=(bi == len(blocks) - 1):
                            e.matmul(ps_sum[:], onesb[:], pt_sb[kk][:], start=first, stop=last)),
                     reads=("onesb", f"pt_sb{kk}"), writes=("ps_s",))
                p.op("PE", (lambda e, kk=kk, vsb=vsb, tk=tk, first=first, last=(bi == len(blocks) - 1):
                            e.matmul(ps_av[0:64, :], vsb[:, tk * 64:(tk + 1) * 64], pt_sb[kk][:],
                                     start=first, stop=last)),
                     reads=(vkey, f"pt_sb{kk}"), writes=("ps_b",))
                first = False
            p.op("DVE", lambda e: e.reciprocal(rs_row[:], ps_sum[:]), reads=("ps_s",), writes=("rs_row",))
            p.op("PE", (lambda e: e.matmul(ps_sc[0][0:64, :], c.ones_row_f[0:1, 0:64], rs_row[:],
                                           start=True, stop=True)),
                 reads=("c_ones_row_f", "rs_row"), writes=("ps_qk0",))
            p.op("DVE", lambda e: e.tensor_copy(rb_sb[:], ps_sc[0][0:64, :]), reads=("ps_qk0",), writes=("rb_sb",))
            ci, po = h // 2, (h % 2) * 64
            p.op("DVE", (lambda e, ci=ci, po=po, tql=tql: e.tensor_mul(att[ci][po:po + 64, tql],
                                                                      ps_av[0:64, :], rb_sb[:])),
                 reads=("ps_b", "rb_sb"), writes=(f"att{ci}",))

    wo_sb = p.sbuf("wo_sb", [128, DC * 128], BF16)
    bo_sb = p.sbuf("bo_sb", [128, DC], F32)
    p.dma("SP", bo_sb[:], bout, writes=("bo_sb",))
    ps_o = c.ps_b2
    xre = [p.sbuf(f"xre{k}", [128, TT], F32) for k in range(2)]
    outs = []
    for oi in range(DC):
        load_wchunks(p, "POOL", wo_sb[:], "wo_sb", wout, D, 0, DC, oi * 128, 128)
        for tt in range(NTT):
            k = tt % 2
            tsl = slice(tt * TT, (tt + 1) * TT)
            for i in range(DC):
                p.op("PE", (lambda e, i=i, tt=tt: e.matmul(ps_o[:], wo_sb[:, i * 128:(i + 1) * 128],
                                                           att[i][:, tt * TT:(tt + 1) * TT],
                                                           start=(i == 0), stop=(i == DC - 1))),
                     reads=("wo_sb", f"att{i}"), writes=("ps_b2",))
            p.dma("SP", xre[k][:], x_in[oi * 128:(oi + 1) * 128, tsl], writes=(f"xre{k}",))
            p.op("DVE", (lambda e, oi=oi, k=k: e.scalar_tensor_tensor(
                    xre[k][:], ps_o[:], 1.0, xre[k][:], ALU.mult, ALU.add)),
                 reads=("ps_b2", f"xre{k}"), writes=(f"xre{k}",))
            p.op("DVE", (lambda e, oi=oi, k=k: e.tensor_scalar_add(xre[k][:], xre[k][:],
                                                                   bo_sb[:, oi:oi + 1])),
                 reads=(f"xre{k}", "bo_sb"), writes=(f"xre{k}",))
            p.dma("SP", x_out[oi * 128:(oi + 1) * 128, tsl], xre[k][:],
                  reads=(f"xre{k}",), writes=(f"xo_{oi}_{tt}",))
            outs.append(f"xo_{oi}_{tt}")
    p.final_wait(outs)
    return p


def build_l2attn_launch(p=None):
    p = p or Prog("l2attn")
    c = Consts(p)
    NJ = 16
    SL = 256
    SP_ = 4096
    QSC = 1.0 / np.sqrt(2048.0)
    x_in = p.dram("x_in", [D, TL], F32, kind="ExternalInput").ap()
    wqk = p.dram("wqk", [NJ, D, 128], F32, kind="ExternalInput").ap()
    wv = p.dram("wv", [NJ, D, 64], F32, kind="ExternalInput").ap()
    bqk = p.dram("bqk", [128, NJ], F32, kind="ExternalInput").ap()
    bv = p.dram("bv", [64, NJ], F32, kind="ExternalInput").ap()
    x_out = p.dram("x_out", [D, TL], F32, kind="ExternalOutput").ap()

    onesb = p.sbuf("onesb", [128, 1], BF16)
    onesrb = p.sbuf("onesrb", [1, 128], BF16)
    p.op("DVE", lambda e: e.memset(onesb[:], 1.0), writes=("onesb",))
    p.op("DVE", lambda e: e.memset(onesrb[:], 1.0), writes=("onesrb",))
    bqk_sb = p.sbuf("bqk_sb", [128, NJ], F32)
    p.dma("SP", bqk_sb[:], bqk, writes=("bqk_sb",))
    bv_sb = p.sbuf("bv_sb", [64, NJ], F32)
    p.dma("SP", bv_sb[:], bv, writes=("bv_sb",))
    identb = p.sbuf("identb", [128, 128], BF16)
    p.op("POOL", lambda e: e.memset(identb[:], 1.0), writes=("identb",))
    p.op("POOL", lambda e: e.affine_select(out=identb[:], in_=identb[:], pattern=[[-1, 128]],
                                           compare_op=ALU.is_equal, fill=0.0, base=0,
                                           channel_multiplier=1),
         reads=("identb",), writes=("identb",))

    ln = [p.sbuf(f"ln{i}", [128, TL], BF16) for i in range(DC)]
    lnkeys = [(ln[i], f"ln{i}") for i in range(DC)]
    for tt in range(NTT):
        emit_ln_tile(p, c, ("hbm", x_in), lnkeys, tt, "ln", out_off=tt * TT)

    wqk_sb = [p.sbuf(f"wqk_sb{k}", [128, DC * 128], BF16) for k in range(2)]
    wv_sb = [p.sbuf(f"wv_sb{k}", [128, DC * 64], BF16) for k in range(2)]
    ps_qk = [p.psum(f"ps_qk{k}", [128, TT], F32) for k in range(2)]
    q_sb = p.sbuf("q_sb", [64, SP_], BF16)
    k_sb = p.sbuf("k_sb", [64, SP_], BF16)
    vf_sb = p.sbuf("vf_sb", [64, SP_], BF16)
    vt_sb = p.sbuf("vt_sb", [128, (SP_ // 128) * 64], BF16)
    sqt = p.sbuf("sqt", [64, SP_], BF16)
    n2r = p.sbuf("n2r", [1, SP_], F32)
    nbl = p.sbuf("nbl", [1, SP_], BF16)
    mk_s = p.sbuf("mk_s", [1, 1], F32)
    NPT = 16
    pt_sb = [p.sbuf(f"pt_sb{k}", [128, TT], BF16) for k in range(NPT)]
    rs_row = p.sbuf("rs_row", [1, TT], F32)
    rb_sb = p.sbuf("rb_sb", [64, TT], F32)
    att_h = p.sbuf("att_h", [64, SP_], BF16)
    xre = [p.sbuf(f"xre{k}", [64, SL], F32) for k in range(4)]
    ps_av = c.ps_b
    ps_sum = c.ps_s
    NT128 = SP_ // 128
    outs = []

    for h in range(8):
        ssl = slice(h * SL, (h + 1) * SL)
        for j in range(NJ):
            k = j % 2
            qk_ap = bass.AP(tensor=wqk.tensor, offset=wqk.offset + j * D * 128,
                            ap=[[128, 128], [128 * 128, DC], [1, 128]])
            p.dma("POOL", wqk_sb[k][:], qk_ap, writes=(f"wqk_sb{k}",))
            v_ap = bass.AP(tensor=wv.tensor, offset=wv.offset + j * D * 64,
                           ap=[[64, 128], [128 * 64, DC], [1, 64]])
            p.dma("POOL", wv_sb[k][:], v_ap, writes=(f"wv_sb{k}",))
            for i in range(DC):
                p.op("PE", (lambda e, k=k, i=i, ssl=ssl: e.matmul(
                        ps_qk[k][0:128, 0:SL], wqk_sb[k][:, i * 128:(i + 1) * 128], ln[i][:, ssl],
                        start=(i == 0), stop=(i == DC - 1))),
                     reads=(f"wqk_sb{k}", f"ln{i}"), writes=(f"ps_qk{k}",))
            for i in range(DC):
                p.op("PE", (lambda e, k=k, i=i, ssl=ssl: e.matmul(
                        ps_qk[k][0:64, SL:2 * SL], wv_sb[k][:, i * 64:(i + 1) * 64], ln[i][:, ssl],
                        start=(i == 0), stop=(i == DC - 1))),
                     reads=(f"wv_sb{k}", f"ln{i}"), writes=(f"ps_qk{k}",))
            qap = bass.AP(tensor=q_sb.ap().tensor, offset=j, ap=[[q_sb.ap().ap[0][0], 64], [NJ, SL]])
            kap = bass.AP(tensor=k_sb.ap().tensor, offset=j, ap=[[k_sb.ap().ap[0][0], 64], [NJ, SL]])
            vap = bass.AP(tensor=vf_sb.ap().tensor, offset=j, ap=[[vf_sb.ap().ap[0][0], 64], [NJ, SL]])
            p.op("DVE", (lambda e, k=k, j=j, qap=qap: e.tensor_scalar(
                    qap, ps_qk[k][0:64, 0:SL], bqk_sb[0:64, j:j + 1], QSC, ALU.add, ALU.mult)),
                 reads=(f"ps_qk{k}", "bqk_sb"), writes=("q_sb",))
            p.op("DVE", (lambda e, k=k, j=j, kap=kap: e.tensor_scalar_add(
                    kap, ps_qk[k][64:128, 0:SL], bqk_sb[64:128, j:j + 1])),
                 reads=(f"ps_qk{k}", "bqk_sb"), writes=("k_sb",))
            p.op("DVE", (lambda e, k=k, j=j, vap=vap: e.tensor_scalar_add(
                    vap, ps_qk[k][0:64, SL:2 * SL], bv_sb[0:64, j:j + 1])),
                 reads=(f"ps_qk{k}", "bv_sb"), writes=("vf_sb",))
        psb = [ps_qk[k].bitcast(BF16) for k in range(2)]
        for ts in range(NT128):
            k = ts % 2
            p.op("PE", (lambda e, ts=ts, k=k: e.transpose(psb[k][0:128, 0:64],
                                                          vf_sb[:, ts * 128:(ts + 1) * 128], identb[0:64, 0:64])),
                 reads=("vf_sb", "identb"), writes=(f"ps_qk{k}",))
            p.op("DVE", (lambda e, ts=ts, k=k: e.tensor_copy(vt_sb[:, ts * 64:(ts + 1) * 64],
                                                             psb[k][0:128, 0:64])),
                 reads=(f"ps_qk{k}",), writes=("vt_sb",))
        p.op("ACT", lambda e: e.activation(sqt[:], k_sb[:], AF.Square), reads=("k_sb",), writes=("sqt",))
        for j in range(SP_ // TT):
            p.op("PE", (lambda e, j=j: e.matmul(ps_sum[:], onesb[0:64, :], sqt[:, j * TT:(j + 1) * TT],
                                                start=True, stop=True)),
                 reads=("onesb", "sqt"), writes=("ps_s",))
            p.op("DVE", (lambda e, j=j: e.tensor_copy(n2r[0:1, j * TT:(j + 1) * TT], ps_sum[:])),
                 reads=("ps_s",), writes=("n2r",))
        p.op("DVE", lambda e: e.tensor_reduce(mk_s[:], n2r[:], AX.X, ALU.max), reads=("n2r",), writes=("mk_s",))
        p.op("ACT", lambda e: e.activation(mk_s[:], mk_s[:], AF.Sqrt), reads=("mk_s",), writes=("mk_s",))
        p.op("DVE", lambda e: e.tensor_scalar_mul(mk_s[:], mk_s[:], -1.0), reads=("mk_s",), writes=("mk_s",))
        p.op("ACT", lambda e: e.activation(sqt[:], q_sb[:], AF.Square), reads=("q_sb",), writes=("sqt",))
        for j in range(SP_ // TT):
            p.op("PE", (lambda e, j=j: e.matmul(ps_sum[:], onesb[0:64, :], sqt[:, j * TT:(j + 1) * TT],
                                                start=True, stop=True)),
                 reads=("onesb", "sqt"), writes=("ps_s",))
            p.op("ACT", (lambda e, j=j: e.activation(n2r[0:1, j * TT:(j + 1) * TT], ps_sum[:], AF.Sqrt)),
                 reads=("ps_s",), writes=("n2r",))
        p.op("DVE", lambda e: e.tensor_scalar_mul(nbl[:], n2r[:], mk_s[:]), reads=("n2r", "mk_s"), writes=("nbl",))
        NTQ = SP_ // TT
        for tqt in range(NTQ):
            tql = slice(tqt * TT, (tqt + 1) * TT)
            nblk = (tqt + 1) * (TT // 128)
            scb = [ps_qk[0], ps_qk[1], c.ps_b2]
            sck = ["ps_qk0", "ps_qk1", "ps_b2"]

            def phase1(tk):
                kk = tk % 3
                jj = tk % NPT
                p.op("PE", (lambda e, kk=kk, tk=tk, tql=tql: e.matmul(
                        scb[kk][0:128, :], k_sb[:, tk * 128:(tk + 1) * 128], q_sb[:, tql],
                        start=True, stop=False)),
                     reads=("k_sb", "q_sb"), writes=(sck[kk],))
                p.op("PE", (lambda e, kk=kk, tql=tql: e.matmul(
                        scb[kk][0:128, :], onesrb[0:1, :], nbl[0:1, tql], start=False, stop=True)),
                     reads=("onesrb", "nbl"), writes=(sck[kk],))
                p.op("ACT", (lambda e, kk=kk, jj=jj: e.activation(pt_sb[jj][:], scb[kk][0:128, :], AF.Exp)),
                     reads=(sck[kk],), writes=(f"pt_sb{jj}",))
                if tk * 128 >= tqt * TT:
                    p.op("POOL", (lambda e, jj=jj, tk=tk, tqt=tqt: e.affine_select(
                            out=pt_sb[jj][:], in_=pt_sb[jj][:], pattern=[[1, TT]],
                            compare_op=ALU.is_ge, fill=0.0, base=tqt * TT - tk * 128,
                            channel_multiplier=-1)),
                         reads=(f"pt_sb{jj}",), writes=(f"pt_sb{jj}",))

            def phase2(tk):
                jj = tk % NPT
                first = tk == 0
                last = tk == nblk - 1
                p.op("PE", (lambda e, jj=jj, first=first, last=last:
                            e.matmul(ps_sum[:], onesb[:], pt_sb[jj][:], start=first, stop=last)),
                     reads=("onesb", f"pt_sb{jj}"), writes=("ps_s",))
                p.op("PE", (lambda e, jj=jj, tk=tk, first=first, last=last:
                            e.matmul(ps_av[0:64, :], vt_sb[:, tk * 64:(tk + 1) * 64], pt_sb[jj][:],
                                     start=first, stop=last)),
                     reads=("vt_sb", f"pt_sb{jj}"), writes=("ps_b",))

            G = NPT // 2
            ngrp = (nblk + G - 1) // G
            for gidx in range(ngrp):
                for tk in range(gidx * G, min((gidx + 1) * G, nblk)):
                    phase1(tk)
                if gidx > 0:
                    for tk in range((gidx - 1) * G, gidx * G):
                        phase2(tk)
            for tk in range((ngrp - 1) * G, nblk):
                phase2(tk)
            p.op("DVE", lambda e: e.reciprocal(rs_row[:], ps_sum[:]), reads=("ps_s",), writes=("rs_row",))
            p.op("PE", (lambda e: e.matmul(ps_qk[0][0:64, :], c.ones_row_f[0:1, 0:64], rs_row[:],
                                           start=True, stop=True)),
                 reads=("c_ones_row_f", "rs_row"), writes=("ps_qk0",))
            p.op("DVE", lambda e: e.tensor_copy(rb_sb[:], ps_qk[0][0:64, :]), reads=("ps_qk0",), writes=("rb_sb",))
            p.op("DVE", (lambda e, tql=tql: e.tensor_mul(att_h[:, tql], ps_av[0:64, :], rb_sb[:])),
                 reads=("ps_b", "rb_sb"), writes=("att_h",))
        for ci in range(DC):
            for half in range(2):
                k = (2 * ci + half) % 4
                jj = 2 * ci + half
                r0 = ci * 128 + half * 64
                p.dma("SP", xre[k][:], x_in[r0:r0 + 64, ssl], writes=(f"xre{k}",))
                aap = bass.AP(tensor=att_h.ap().tensor, offset=jj,
                              ap=[[att_h.ap().ap[0][0], 64], [NJ, SL]])
                p.op("DVE", (lambda e, k=k, aap=aap: e.tensor_add(
                        xre[k][:], xre[k][:], aap)),
                     reads=(f"xre{k}", "att_h"), writes=(f"xre{k}",))
                p.dma("SP", x_out[r0:r0 + 64, ssl], xre[k][:],
                      reads=(f"xre{k}",), writes=(f"xo_{h}_{ci}_{half}",))
                outs.append(f"xo_{h}_{ci}_{half}")
    p.final_wait(outs)
    return p


def build_l0attn_launch(p=None):
    p = p or Prog("l0attn")
    c = Consts(p)
    NH = 16
    CH = 128
    NCH = TL // CH
    x_in = p.dram("x_in", [D, TL], F32, kind="ExternalInput").ap()
    wqkv = p.dram("wqkv", [D, 3 * D], F32, kind="ExternalInput").ap()
    bqkv = p.dram("bqkv", [128, 24], F32, kind="ExternalInput").ap()
    wout = p.dram("wout", [D, D], F32, kind="ExternalInput").ap()
    boutd = p.dram("boutd", [128, DC], F32, kind="ExternalInput").ap()
    n1w = p.dram("n1w", [128, DC], F32, kind="ExternalInput").ap()
    n2w = p.dram("n2w", [128, DC], F32, kind="ExternalInput").ap()
    w1 = p.dram("w1", [D, 4 * D], F32, kind="ExternalInput").ap()
    b1d = p.dram("b1d", [128, 32], F32, kind="ExternalInput").ap()
    w2 = p.dram("w2", [4 * D, D], F32, kind="ExternalInput").ap()
    b2d = p.dram("b2d", [128, DC], F32, kind="ExternalInput").ap()
    poscs = p.dram("poscs", [2, TL], F32, kind="ExternalInput").ap()
    moff = p.dram("moff", [64, 1], F32, kind="ExternalInput").ap()
    x_out = p.dram("x_out", [D, TL], F32, kind="ExternalOutput").ap()
    h_scr = p.dram("h_scr", [D, TL], F32)
    st_out = p.dram("st_out", [NH * 64, 130], F32)
    st_g = p.dram("st_g", [2 * NH * 64, 130], F32)

    onesb = p.sbuf("onesb", [128, 1], BF16)
    p.op("DVE", lambda e: e.memset(onesb[:], 1.0), writes=("onesb",))
    identb = p.sbuf("identb", [128, 128], BF16)
    p.op("POOL", lambda e: e.memset(identb[:], 1.0), writes=("identb",))
    p.op("POOL", lambda e: e.affine_select(out=identb[:], in_=identb[:], pattern=[[-1, 128]],
                                           compare_op=ALU.is_equal, fill=0.0, base=0,
                                           channel_multiplier=1),
         reads=("identb",), writes=("identb",))
    moffc = p.sbuf("moffc", [64, 1], F32)
    p.dma("SP", moffc[:], moff, writes=("moffc",))
    bq_sb = p.sbuf("bq_sb", [128, 24], F32)
    p.dma("SP", bq_sb[:], bqkv, writes=("bq_sb",))
    n1_sb = p.sbuf("n1_sb", [128, DC], F32)
    n2_sb = p.sbuf("n2_sb", [128, DC], F32)
    p.dma("SP", n1_sb[:], n1w, writes=("n1_sb",))
    p.dma("SP", n2_sb[:], n2w, writes=("n2_sb",))
    pos_sb = [p.sbuf(f"pos_sb{w}", [1, TL], F32) for w in range(2)]
    for w in range(2):
        p.dma("SP", pos_sb[w][:], poscs[w:w + 1, :], writes=(f"pos_sb{w}",))
    cos_b = p.sbuf("cos_b", [128, TL], BF16)
    sin_b = p.sbuf("sin_b", [128, TL], BF16)
    for w_i, (dst, key) in enumerate(((cos_b, "cos_b"), (sin_b, "sin_b"))):
        for tt in range(NTT):
            p.op("PE", (lambda e, w_i=w_i, tt=tt: e.matmul(c.ps_b[:], c.ones_row_f[:],
                                                           pos_sb[w_i][0:1, tt * TT:(tt + 1) * TT],
                                                           start=True, stop=True)),
                 reads=("c_ones_row_f", f"pos_sb{w_i}"), writes=("ps_b",))
            p.op("DVE", (lambda e, dst=dst, tt=tt, key=key: e.tensor_copy(dst[:, tt * TT:(tt + 1) * TT], c.ps_b[:])),
                 reads=("ps_b",), writes=(key,))

    z_scr = p.dram("z_scr", [D, TL], F32)
    zt = [p.sbuf(f"zt{i}", [128, TT], F32) for i in range(DC)]
    zkeys = [(zt[i], f"zt{i}") for i in range(DC)]
    arm = [p.sbuf(f"arm{i}", [128, TL], BF16) for i in range(DC)]
    for tt in range(NTT):
        emit_ln_tile(p, c, ("hbm", x_in), zkeys, tt, "ln", out_off=0, n_tok_tile=TT)
        for i in range(DC):
            p.dma("SP", z_scr[i * 128:(i + 1) * 128, tt * TT:(tt + 1) * TT], zt[i][:],
                  reads=(f"zt{i}",), writes=("z_scr",))
    for tt in range(NTT):
        emit_ln_tile(p, c, ("hbm", z_scr), [(arm[i], f"arm{i}") for i in range(DC)], tt, "rm",
                     kind="rms", eps=1e-6, gain=(n1_sb, "n1_sb", 0), out_off=tt * TT)

    wq_sb = [p.sbuf(f"wq_sb{k}", [128, DC * 128], BF16) for k in range(2)]
    ps_mm = [p.psum(f"ps_mm{k}", [128, TT], F32) for k in range(2)]
    qkv_scr = p.dram("qkv_scr", [5, D, TL], BF16)
    rl = p.sbuf("rl", [128, TT], BF16)
    stg = [p.sbuf(f"stg{k}", [128, TT], BF16) for k in range(4)]
    for fc in range(24):
        k = fc % 2
        load_wchunks(p, "POOL", wq_sb[k][:], f"wq_sb{k}", wqkv, 3 * D, 0, DC, fc * 128, 128)
        for tt in range(NTT):
            tsl = slice(tt * TT, (tt + 1) * TT)
            for i in range(DC):
                p.op("PE", (lambda e, k=k, i=i, tt=tt: e.matmul(
                        ps_mm[k][:], wq_sb[k][:, i * 128:(i + 1) * 128],
                        arm[i][:, tt * TT:(tt + 1) * TT], start=(i == 0), stop=(i == DC - 1))),
                     reads=(f"wq_sb{k}", f"arm{i}"), writes=(f"ps_mm{k}",))
            if fc < 16:
                p.op("ACT", (lambda e, k=k, fc=fc: e.activation(rl[:], ps_mm[k][:], AF.Relu,
                                                               bias=bq_sb[:, fc:fc + 1])),
                     reads=(f"ps_mm{k}", "bq_sb"), writes=("rl",))
                base = 0 if fc < 8 else 2
                ci = fc % 8
                for si, (wsb, wkey) in enumerate(((cos_b, "cos_b"), (sin_b, "sin_b"))):
                    sg = (2 * tt + si) % 4
                    p.op("DVE", (lambda e, sg=sg, tsl=tsl, wsb=wsb: e.tensor_mul(
                            stg[sg][:], rl[:], wsb[:, tsl])),
                         reads=("rl", wkey), writes=(f"stg{sg}",))
                    p.dma("SP", qkv_scr[base + si, ci * 128:(ci + 1) * 128, tsl], stg[sg][:],
                          reads=(f"stg{sg}",), writes=("qkv_scr",))
            else:
                sg = tt % 4
                p.op("DVE", (lambda e, k=k, fc=fc, sg=sg: e.tensor_scalar_add(
                        stg[sg][:], ps_mm[k][:], bq_sb[:, fc:fc + 1])),
                     reads=(f"ps_mm{k}", "bq_sb"), writes=(f"stg{sg}",))
                p.dma("SP", qkv_scr[4, (fc - 16) * 128:(fc - 15) * 128, tsl], stg[sg][:],
                      reads=(f"stg{sg}",), writes=("qkv_scr",))

    kct = p.sbuf("kct", [128, NCH * 64], BF16)
    kst = p.sbuf("kst", [128, NCH * 64], BF16)
    vt = p.sbuf("vt", [128, NCH * 64], BF16)
    kvt_scr = p.dram("kvt_scr", [NH, 3, 128, NCH * 64], BF16)
    pfx_scr = p.dram("pfx_scr", [NH, NCH, 64, 130], BF16)
    ps_st = p.psum("ps_st", [64, 130], F32)
    st_sb = p.sbuf("st_sb", [64, 130], F32)
    psb_mm = [ps_mm[k].bitcast(BF16) for k in range(2)]
    hq = [p.sbuf(f"hq{j}", [64, TL], BF16) for j in range(4)]
    for h in range(NH):
        ci, po = h // 2, (h % 2) * 64
        for j, scr_i in enumerate((2, 3, 4)):
            p.dma("SP", hq[j][:], qkv_scr[scr_i, h * 64:(h + 1) * 64, :],
                  reads=("qkv_scr",), writes=(f"hq{j}",))
        for src_j, dst, dkey in ((0, kct, "kct"), (1, kst, "kst"), (2, vt, "vt")):
            for ch in range(NCH):
                k = ch % 2
                p.op("PE", (lambda e, src_j=src_j, ch=ch, k=k: e.transpose(
                        psb_mm[k][0:128, 0:64], hq[src_j][:, ch * CH:(ch + 1) * CH],
                        identb[0:64, 0:64])),
                     reads=(f"hq{src_j}", "identb"), writes=(f"ps_mm{k}",))
                p.op("DVE", (lambda e, dst=dst, ch=ch, k=k: e.tensor_copy(
                        dst[:, ch * 64:(ch + 1) * 64], psb_mm[k][0:128, 0:64])),
                     reads=(f"ps_mm{k}",), writes=(dkey,))
        p.dma("SP", kvt_scr[h, 0], kct[:], reads=("kct",), writes=("kvt_scr",))
        p.dma("SP", kvt_scr[h, 1], kst[:], reads=("kst",), writes=("kvt_scr",))
        p.dma("SP", kvt_scr[h, 2], vt[:], reads=("vt",), writes=("kvt_scr",))
        p.op("DVE", lambda e: e.memset(st_sb[:], 0.0), writes=("st_sb",))
        for ch in range(NCH):
            p.dma("POOL", pfx_scr[h, ch], st_sb[:], reads=("st_sb",), writes=("pfx_scr",))
            csl = slice(ch * 64, (ch + 1) * 64)
            p.op("PE", (lambda e, csl=csl: e.matmul(ps_st[0:64, 0:64], kct[:, csl], vt[:, csl],
                                                    start=True, stop=False)),
                 reads=("kct", "vt"), writes=("ps_st",))
            p.op("PE", (lambda e, csl=csl: e.matmul(ps_st[0:64, 64:128], kst[:, csl], vt[:, csl],
                                                    start=False, stop=False)),
                 reads=("kst", "vt"), writes=("ps_st",))
            p.op("PE", (lambda e, csl=csl: e.matmul(ps_st[0:64, 128:129], kct[:, csl], onesb[:],
                                                    start=False, stop=False)),
                 reads=("kct", "onesb"), writes=("ps_st",))
            p.op("PE", (lambda e, csl=csl: e.matmul(ps_st[0:64, 129:130], kst[:, csl], onesb[:],
                                                    start=False, stop=True)),
                 reads=("kst", "onesb"), writes=("ps_st",))
            p.op("DVE", lambda e: e.tensor_add(st_sb[:], st_sb[:], ps_st[:]),
                 reads=("st_sb", "ps_st"), writes=("st_sb",))
        p.dma("SP", st_out.ap()[h * 64:(h + 1) * 64, :], st_sb[:], reads=("st_sb",), writes=("st_out",))

    p.collective("AllGather", ALU.bypass, [[0, 1], [2, 3], [4, 5], [6, 7]],
                 [st_out.ap()], [st_g.ap()], reads=("st_out",), writes=("st_g",))

    att = [p.sbuf(f"att{i}", [128, TL], BF16) for i in range(DC)]
    st0 = p.sbuf("st0", [64, 130], F32)
    stc = p.sbuf("stc", [64, 130], BF16)
    pfx = p.sbuf("pfx", [64, NCH * 130], BF16)
    wt_sb = p.sbuf("wt_sb", [128, CH], BF16)
    nr_row = p.sbuf("nr_row", [1, CH], F32)
    rb2 = p.sbuf("rb2", [64, CH], F32)
    ps_ctx = p.psum("ps_ctx", [128, 512], F32)
    for h in range(NH):
        ci, po = h // 2, (h % 2) * 64
        for j in range(4):
            p.dma("SP", hq[j][:], qkv_scr[j, h * 64:(h + 1) * 64, :],
                  reads=("qkv_scr",), writes=(f"hq{j}",))
        p.dma("SP", kct[:], kvt_scr[h, 0], reads=("kvt_scr",), writes=("kct",))
        p.dma("SP", kst[:], kvt_scr[h, 1], reads=("kvt_scr",), writes=("kst",))
        p.dma("SP", vt[:], kvt_scr[h, 2], reads=("kvt_scr",), writes=("vt",))
        pfx_ap = bass.AP(tensor=pfx_scr.ap().tensor, offset=h * (NCH * 64 * 130),
                         ap=[[130, 64], [64 * 130, NCH], [1, 130]])
        p.dma("SP", pfx[:], pfx_ap, reads=("pfx_scr",), writes=("pfx",))
        p.dma("SP", st0[:], st_g.ap()[h * 64:(h + 1) * 64, :], reads=("st_g",), writes=("st0",))
        p.op("DVE", lambda e: e.tensor_scalar_mul(st0[:], st0[:], moffc[:]),
             reads=("st0", "moffc"), writes=("st0",))
        for ch in range(NCH):
            csl = slice(ch * CH, (ch + 1) * CH)
            k = ch % 2
            p.op("DVE", (lambda e, ch=ch: e.tensor_add(stc[:], pfx[:, ch * 130:(ch + 1) * 130], st0[:])),
                 reads=("pfx", "st0"), writes=("stc",))
            p.op("PE", (lambda e, k=k, ci=ci, po=po, ch=ch, csl=csl: e.matmul(
                    ps_mm[k][0:128, 0:CH], hq[2][:, csl], hq[0][:, csl],
                    start=True, stop=False)),
                 reads=("hq2", "hq0"), writes=(f"ps_mm{k}",))
            p.op("PE", (lambda e, k=k, ci=ci, po=po, csl=csl: e.matmul(
                    ps_mm[k][0:128, 0:CH], hq[3][:, csl], hq[1][:, csl],
                    start=False, stop=True)),
                 reads=("hq3", "hq1"), writes=(f"ps_mm{k}",))
            p.op("DVE", (lambda e, k=k: e.tensor_copy(wt_sb[:], ps_mm[k][0:128, 0:CH])),
                 reads=(f"ps_mm{k}",), writes=("wt_sb",))
            p.op("POOL", lambda e: e.affine_select(out=wt_sb[:], in_=wt_sb[:], pattern=[[1, CH]],
                                                   compare_op=ALU.is_ge, fill=0.0, base=0,
                                                   channel_multiplier=-1),
                 reads=("wt_sb",), writes=("wt_sb",))
            p.op("PE", (lambda e, ch=ch: e.matmul(ps_ctx[0:64, 0:CH], vt[:, ch * 64:(ch + 1) * 64],
                                                  wt_sb[:], start=True, stop=False)),
                 reads=("vt", "wt_sb"), writes=("ps_ctx",))
            p.op("PE", (lambda e, ci=ci, po=po, csl=csl: e.matmul(
                    ps_ctx[0:64, 0:CH], stc[:, 0:64], hq[0][:, csl], start=False, stop=False)),
                 reads=("stc", "hq0"), writes=("ps_ctx",))
            p.op("PE", (lambda e, ci=ci, po=po, csl=csl: e.matmul(
                    ps_ctx[0:64, 0:CH], stc[:, 64:128], hq[1][:, csl], start=False, stop=True)),
                 reads=("stc", "hq1"), writes=("ps_ctx",))
            p.op("PE", (lambda e: e.matmul(ps_ctx[0:1, 128:128 + CH], onesb[:], wt_sb[:],
                                           start=True, stop=False)),
                 reads=("onesb", "wt_sb"), writes=("ps_ctx",))
            p.op("PE", (lambda e, ci=ci, po=po, csl=csl: e.matmul(
                    ps_ctx[0:1, 128:128 + CH], stc[:, 128:129], hq[0][:, csl],
                    start=False, stop=False)),
                 reads=("stc", "hq0"), writes=("ps_ctx",))
            p.op("PE", (lambda e, ci=ci, po=po, csl=csl: e.matmul(
                    ps_ctx[0:1, 128:128 + CH], stc[:, 129:130], hq[1][:, csl],
                    start=False, stop=True)),
                 reads=("stc", "hq1"), writes=("ps_ctx",))
            p.op("DVE", lambda e: e.tensor_scalar_add(nr_row[:], ps_ctx[0:1, 128:128 + CH], 1e-6),
                 reads=("ps_ctx",), writes=("nr_row",))
            p.op("DVE", lambda e: e.reciprocal(nr_row[:], nr_row[:]), reads=("nr_row",), writes=("nr_row",))
            p.op("PE", (lambda e: e.matmul(ps_ctx[0:64, 256:256 + CH], c.ones_row_f[0:1, 0:64], nr_row[:],
                                           start=True, stop=True)),
                 reads=("c_ones_row_f", "nr_row"), writes=("ps_ctx",))
            p.op("DVE", lambda e: e.tensor_copy(rb2[:], ps_ctx[0:64, 256:256 + CH]),
                 reads=("ps_ctx",), writes=("rb2",))
            p.op("DVE", (lambda e, ci=ci, po=po, csl=csl: e.tensor_mul(att[ci][po:po + 64, csl],
                                                                      ps_ctx[0:64, 0:CH], rb2[:])),
                 reads=("ps_ctx", "rb2"), writes=(f"att{ci}",))

    bo_sb = p.sbuf("bo_sb", [128, DC], F32)
    p.dma("SP", bo_sb[:], boutd, writes=("bo_sb",))
    wo_sb = [p.sbuf(f"wo_sb{k}", [128, DC * 128], BF16) for k in range(2)]
    hev = [p.sbuf(f"hev{k}", [128, TT], F32) for k in range(2)]
    for oi in range(DC):
        k = oi % 2
        load_wchunks(p, "POOL", wo_sb[k][:], f"wo_sb{k}", wout, D, 0, DC, oi * 128, 128)
        for tt in range(NTT):
            tsl = slice(tt * TT, (tt + 1) * TT)
            for i in range(DC):
                p.op("PE", (lambda e, k=k, i=i, tt=tt: e.matmul(ps_mm[k][:], wo_sb[k][:, i * 128:(i + 1) * 128],
                                                                att[i][:, tt * TT:(tt + 1) * TT],
                                                                start=(i == 0), stop=(i == DC - 1))),
                     reads=(f"wo_sb{k}", f"att{i}"), writes=(f"ps_mm{k}",))
            p.dma("SP", zt[k][:], z_scr[oi * 128:(oi + 1) * 128, tsl],
                  reads=("z_scr",), writes=(f"zt{k}",))
            p.op("DVE", (lambda e, k=k, oi=oi: e.tensor_scalar_add(hev[k][:], ps_mm[k][:],
                                                                   bo_sb[:, oi:oi + 1])),
                 reads=(f"ps_mm{k}", "bo_sb"), writes=(f"hev{k}",))
            p.op("DVE", (lambda e, k=k: e.tensor_add(hev[k][:], hev[k][:], zt[k][:])),
                 reads=(f"hev{k}", f"zt{k}"), writes=(f"hev{k}",))
            p.dma("SP", h_scr[oi * 128:(oi + 1) * 128, tsl], hev[k][:],
                  reads=(f"hev{k}",), writes=("h_scr",))

    mkeys = [f"m{i}" for i in range(DC)]
    hbkeys = [f"hb{fi}" for fi in range(32)]
    p.barrier([f"arm{i}" for i in range(DC)] + [f"att{i}" for i in range(DC)] + mkeys + hbkeys)
    m = arm
    for tt in range(NTT):
        emit_ln_tile(p, c, ("hbm", h_scr), [(m[i], f"m{i}") for i in range(DC)], tt, "rm2",
                     kind="rms", eps=1e-6, gain=(n2_sb, "n2_sb", 0), out_off=tt * TT)
    b1_sb = p.sbuf("b1_sb", [128, 32], F32)
    b2_sb = p.sbuf("b2_sb", [128, DC], F32)
    p.dma("SP", b1_sb[:], b1d, writes=("b1_sb",))
    p.dma("SP", b2_sb[:], b2d, writes=("b2_sb",))
    w1_sb = [p.sbuf(f"w1_sb{k}", [128, DC * 128], BF16) for k in range(2)]
    w2_sb = [p.sbuf(f"w2_sb{k}", [128, 32 * 128], BF16) for k in range(2)]
    hb = att
    xre = [p.sbuf(f"xre{k}", [128, TT], F32) for k in range(2)]
    hre = [p.sbuf(f"hre{k}", [128, TT], F32) for k in range(2)]
    outs = []
    for tt in range(NTT):
        tsl = slice(tt * TT, (tt + 1) * TT)
        for fi in range(32):
            k = fi % 2
            load_wchunks(p, "POOL", w1_sb[k][:], f"w1_sb{k}", w1, 4 * D, 0, DC, fi * 128, 128)
            for i in range(DC):
                p.op("PE", (lambda e, k=k, i=i, tt=tt: e.matmul(ps_mm[k][:], w1_sb[k][:, i * 128:(i + 1) * 128],
                                                                m[i][:, tt * TT:(tt + 1) * TT],
                                                                start=(i == 0), stop=(i == DC - 1))),
                     reads=(f"w1_sb{k}", f"m{i}"), writes=(f"ps_mm{k}",))
            p.op("ACT", (lambda e, k=k, fi=fi: e.activation(hb[fi // 4][:, (fi % 4) * TT:(fi % 4 + 1) * TT],
                                                            ps_mm[k][:], AF.Gelu_apprx_tanh,
                                                            bias=b1_sb[:, fi:fi + 1])),
                 reads=(f"ps_mm{k}", "b1_sb"), writes=(f"hb{fi}",))
        for oi in range(DC):
            k = oi % 2
            load_wchunks(p, "POOL", w2_sb[k][:], f"w2_sb{k}", w2, D, 0, 32, oi * 128, 128)
            for fi in range(32):
                p.op("PE", (lambda e, k=k, fi=fi: e.matmul(
                        ps_mm[k][:], w2_sb[k][:, fi * 128:(fi + 1) * 128],
                        hb[fi // 4][:, (fi % 4) * TT:(fi % 4 + 1) * TT],
                        start=(fi == 0), stop=(fi == 31))),
                     reads=(f"w2_sb{k}", f"hb{fi}"), writes=(f"ps_mm{k}",))
            p.dma("SP", xre[k][:], x_in[oi * 128:(oi + 1) * 128, tsl], writes=(f"xre{k}",))
            p.dma("SP", hre[k][:], h_scr[oi * 128:(oi + 1) * 128, tsl], reads=("h_scr",), writes=(f"hre{k}",))
            p.op("DVE", (lambda e, k=k, oi=oi: e.tensor_scalar_add(hre[k][:], hre[k][:],
                                                                   b2_sb[:, oi:oi + 1])),
                 reads=(f"hre{k}", "b2_sb"), writes=(f"hre{k}",))
            p.op("DVE", (lambda e, k=k: e.tensor_add(xre[k][:], xre[k][:], hre[k][:])),
                 reads=(f"xre{k}", f"hre{k}"), writes=(f"xre{k}",))
            p.op("DVE", (lambda e, k=k: e.tensor_add(xre[k][:], xre[k][:], ps_mm[k][:])),
                 reads=(f"xre{k}", f"ps_mm{k}"), writes=(f"xre{k}",))
            p.dma("SP", x_out[oi * 128:(oi + 1) * 128, tsl], xre[k][:],
                  reads=(f"xre{k}",), writes=(f"xo_{oi}_{tt}",))
            outs.append(f"xo_{oi}_{tt}")
    p.final_wait(outs)
    return p


def build_l1attn_launch(p=None):
    p = p or Prog("l1attn")
    c = Consts(p)
    x_in = p.dram("x_in", [D, TL], F32, kind="ExternalInput").ap()
    wkt = p.dram("wkt", [D, D], F32, kind="ExternalInput").ap()
    wvt = p.dram("wvt", [D, D], F32, kind="ExternalInput").ap()
    wrt = p.dram("wrt", [D, D], F32, kind="ExternalInput").ap()
    wot = p.dram("wot", [D, D], F32, kind="ExternalInput").ap()
    tdc = p.dram("tdc", [128, DC], F32, kind="ExternalInput").ap()
    tfc = p.dram("tfc", [128, DC], F32, kind="ExternalInput").ap()
    mkc = p.dram("mkc", [128, DC], F32, kind="ExternalInput").ap()
    mvc = p.dram("mvc", [128, DC], F32, kind="ExternalInput").ap()
    mrc = p.dram("mrc", [128, DC], F32, kind="ExternalInput").ap()
    moff = p.dram("moff", [128, 1], F32, kind="ExternalInput").ap()
    x_out = p.dram("x_out", [D, TL], F32, kind="ExternalOutput").ap()
    bnd_out = p.dram("bnd_out", [128, DC], F32)
    bnd_g = p.dram("bnd_g", [256, DC], F32)
    term_out = p.dram("term_out", [128, 2 * DC], F32)
    term_g = p.dram("term_g", [256, 2 * DC], F32)

    moffc = p.sbuf("moffc", [128, 1], F32)
    p.dma("SP", moffc[:], moff, writes=("moffc",))
    cols = {}
    for nm, dr in (("td", tdc), ("tf", tfc), ("mk", mkc), ("mv", mvc), ("mr", mrc)):
        t = p.sbuf(f"col_{nm}", [128, DC], F32)
        p.dma("SP", t[:], dr, writes=(f"col_{nm}",))
        cols[nm] = t
    ew = p.sbuf("col_ew", [128, DC], F32)
    eu = p.sbuf("col_eu", [128, DC], F32)
    p.op("ACT", lambda e: e.activation(ew[:], cols["td"][:], AF.Exp), reads=("col_td",), writes=("col_ew",))
    p.op("ACT", lambda e: e.activation(ew[:], ew[:], AF.Exp, scale=-1.0), reads=("col_ew",), writes=("col_ew",))
    p.op("ACT", lambda e: e.activation(eu[:], cols["tf"][:], AF.Exp), reads=("col_tf",), writes=("col_eu",))

    wcb = [p.sbuf(f"wcb{k}", [128, DC * 128], BF16) for k in range(2)]

    def load_wcb(dr, ei, k, key):
        load_wchunks(p, "POOL", wcb[k][:], key, dr, D, 0, DC, ei * 128, 128)

    z = [p.sbuf(f"z{i}", [128, TL], BF16) for i in range(DC)]
    zkeys = [(z[i], f"z{i}") for i in range(DC)]
    for tt in range(NTT):
        emit_ln_tile(p, c, ("hbm", x_in), zkeys, tt, "ln", out_off=tt * TT)
    zb_sb = p.sbuf("zb_sb", [128, DC], F32)
    for i in range(DC):
        p.op("DVE", (lambda e, i=i: e.tensor_copy(zb_sb[:, i:i + 1], z[i][:, TL - 1:TL])),
             reads=(f"z{i}",), writes=("zb_sb",))
    p.dma("SP", bnd_out.ap(), zb_sb[:], reads=("zb_sb",), writes=("bnd_out",))
    p.collective("AllGather", ALU.bypass, [[0, 1], [2, 3], [4, 5], [6, 7]],
                 [bnd_out.ap()], [bnd_g.ap()], reads=("bnd_out",), writes=("bnd_g",))
    zbi = p.sbuf("zbi", [128, DC], F32)
    p.dma("SP", zbi[:], bnd_g.ap()[0:128, :], reads=("bnd_g",), writes=("zbi",))
    p.op("DVE", lambda e: e.tensor_scalar_mul(zbi[:], zbi[:], moffc[:]),
         reads=("zbi", "moffc"), writes=("zbi",))

    zs = [p.sbuf(f"zs{i}", [128, TL], BF16) for i in range(DC)]
    xr = [p.sbuf(f"xr{i}", [128, TL], BF16) for i in range(DC)]
    dsc = p.sbuf("dsc", [128, TT], F32)
    zsc = p.sbuf("zsc", [128, TT], F32)
    for i in range(DC):
        p.op("DVE", (lambda e, i=i: e.tensor_copy(zs[i][:, 1:TL], z[i][:, 0:TL - 1])),
             reads=(f"z{i}",), writes=(f"zs{i}",))
        p.op("DVE", (lambda e, i=i: e.tensor_copy(zs[i][:, 0:1], zbi[:, i:i + 1])),
             reads=("zbi", f"zs{i}"), writes=(f"zs{i}",))
    for i in range(DC):
        for tt in range(NTT):
            tsl = slice(tt * TT, (tt + 1) * TT)
            p.op("DVE", (lambda e, i=i, tsl=tsl: e.tensor_copy(zsc[:], zs[i][:, tsl])),
                 reads=(f"zs{i}",), writes=("zsc",))
            p.op("DVE", (lambda e, i=i, tsl=tsl: e.tensor_sub(dsc[:], z[i][:, tsl], zsc[:])),
                 reads=(f"z{i}", "zsc"), writes=("dsc",))
            p.op("DVE", (lambda e, i=i, tsl=tsl: e.scalar_tensor_tensor(
                    z[i][:, tsl], dsc[:], cols["mk"][:, i:i + 1], zsc[:], ALU.mult, ALU.add)),
                 reads=("dsc", "col_mk", "zsc"), writes=(f"z{i}",))
            p.op("DVE", (lambda e, i=i, tsl=tsl: e.scalar_tensor_tensor(
                    xr[i][:, tsl], dsc[:], cols["mr"][:, i:i + 1], zsc[:], ALU.mult, ALU.add)),
                 reads=("dsc", "col_mr", "zsc"), writes=(f"xr{i}",))
            p.op("DVE", (lambda e, i=i, tsl=tsl: e.scalar_tensor_tensor(
                    zs[i][:, tsl], dsc[:], cols["mv"][:, i:i + 1], zsc[:], ALU.mult, ALU.add)),
                 reads=("dsc", "col_mv", "zsc"), writes=(f"zs{i}",))
    xk, xv = z, zs

    kv_scr = p.dram("kv_scr", [2, D, TL], BF16)
    ps_mm = [p.psum(f"ps_mm{k}", [128, TT], F32) for k in range(2)]
    ke = p.sbuf("ke", [128, TL], BF16)
    ve = p.sbuf("ve_", [128, TL], BF16)
    kxp = p.sbuf("kxp", [128, TL], BF16)
    zz = p.sbuf("zz", [128, TL], BF16)
    ab = p.sbuf("ab", [128, TL], F32)
    bb = p.sbuf("bb", [128, TL], F32)
    term_sb = p.sbuf("term_sb", [128, DC * 2], F32)
    zero_col = p.sbuf("zero_col", [128, 1], F32)
    p.op("DVE", lambda e: e.memset(zero_col[:], 0.0), writes=("zero_col",))
    for ei in range(DC):
        load_wcb(wkt, ei, 0, "wcb0")
        load_wcb(wvt, ei, 1, "wcb1")
        for tt in range(NTT):
            k = tt % 2
            tsl = slice(tt * TT, (tt + 1) * TT)
            for i in range(DC):
                p.op("PE", (lambda e, k=k, i=i, tsl=tsl: e.matmul(
                        ps_mm[k][:], wcb[0][:, i * 128:(i + 1) * 128],
                        xk[i][:, tsl], start=(i == 0), stop=(i == DC - 1))),
                     reads=("wcb0", f"z{i}"), writes=(f"ps_mm{k}",))
            p.op("DVE", (lambda e, k=k, tsl=tsl: e.tensor_copy(ke[:, tsl], ps_mm[k][:])),
                 reads=(f"ps_mm{k}",), writes=("ke",))
            for i in range(DC):
                p.op("PE", (lambda e, k=k, i=i, tsl=tsl: e.matmul(
                        ps_mm[k][:], wcb[1][:, i * 128:(i + 1) * 128],
                        xv[i][:, tsl], start=(i == 0), stop=(i == DC - 1))),
                     reads=("wcb1", f"zs{i}"), writes=(f"ps_mm{k}",))
            p.op("DVE", (lambda e, k=k, tsl=tsl: e.tensor_copy(ve[:, tsl], ps_mm[k][:])),
                 reads=(f"ps_mm{k}",), writes=("ve_",))
        p.dma("SP", kv_scr[0, ei * 128:(ei + 1) * 128, :], ke[:], reads=("ke",), writes=("kv_scr",))
        p.dma("SP", kv_scr[1, ei * 128:(ei + 1) * 128, :], ve[:], reads=("ve_",), writes=("kv_scr",))
        p.op("ACT", lambda e: e.activation(kxp[:], ke[:], AF.Exp), reads=("ke",), writes=("kxp",))
        p.op("DVE", lambda e: e.tensor_mul(zz[:], ve[:], kxp[:]), reads=("ve_", "kxp"), writes=("zz",))
        ewb = bass.AP(tensor=ew.ap().tensor, offset=ei, ap=[[ew.ap().ap[0][0], 128], [0, TL]])
        p.op("DVE", (lambda e, ewb=ewb: e.tensor_tensor_scan(ab[:], ewb, zz[:], 0.0, ALU.mult, ALU.add)),
             reads=("col_ew", "zz"), writes=("ab",))
        p.op("DVE", (lambda e, ewb=ewb: e.tensor_tensor_scan(bb[:], ewb, kxp[:], 0.0, ALU.mult, ALU.add)),
             reads=("col_ew", "kxp"), writes=("bb",))
        p.op("DVE", (lambda e, ei=ei: e.tensor_copy(term_sb[:, 2 * ei:2 * ei + 1], ab[:, TL - 1:TL])),
             reads=("ab",), writes=("term_sb",))
        p.op("DVE", (lambda e, ei=ei: e.tensor_copy(term_sb[:, 2 * ei + 1:2 * ei + 2], bb[:, TL - 1:TL])),
             reads=("bb",), writes=("term_sb",))
    p.dma("SP", term_out.ap(), term_sb[:], reads=("term_sb",), writes=("term_out",))
    p.collective("AllGather", ALU.bypass, [[0, 1], [2, 3], [4, 5], [6, 7]],
                 [term_out.ap()], [term_g.ap()], reads=("term_out",), writes=("term_g",))
    term_in = p.sbuf("term_in", [128, DC * 2], F32)
    p.dma("SP", term_in[:], term_g.ap()[0:128, :], reads=("term_g",), writes=("term_in",))
    p.op("DVE", lambda e: e.tensor_scalar_mul(term_in[:], term_in[:], moffc[:]),
         reads=("term_in", "moffc"), writes=("term_in",))

    rwkv = [p.sbuf(f"rwkv{i}", [128, TL], BF16) for i in range(DC)]
    num = p.sbuf("num", [128, TL], F32)
    sr = p.sbuf("sr", [128, TL], BF16)
    for ei in range(DC):
        p.dma("SP", ke[:], kv_scr[0, ei * 128:(ei + 1) * 128, :], reads=("kv_scr",), writes=("ke",))
        p.dma("SP", ve[:], kv_scr[1, ei * 128:(ei + 1) * 128, :], reads=("kv_scr",), writes=("ve_",))
        p.op("ACT", lambda e: e.activation(kxp[:], ke[:], AF.Exp), reads=("ke",), writes=("kxp",))
        p.op("DVE", lambda e: e.tensor_mul(zz[:], ve[:], kxp[:]), reads=("ve_", "kxp"), writes=("zz",))
        ewb = bass.AP(tensor=ew.ap().tensor, offset=ei, ap=[[ew.ap().ap[0][0], 128], [0, TL]])
        p.op("DVE", (lambda e, ewb=ewb, ei=ei: e.tensor_tensor_scan(
                ab[:], ewb, zz[:], term_in[:, 2 * ei:2 * ei + 1], ALU.mult, ALU.add)),
             reads=("col_ew", "zz", "term_in"), writes=("ab",))
        p.op("DVE", (lambda e, ewb=ewb, ei=ei: e.tensor_tensor_scan(
                bb[:], ewb, kxp[:], term_in[:, 2 * ei + 1:2 * ei + 2], ALU.mult, ALU.add)),
             reads=("col_ew", "kxp", "term_in"), writes=("bb",))
        p.op("DVE", (lambda e, ei=ei: e.scalar_tensor_tensor(num[:, 1:TL], zz[:, 1:TL],
                                                             eu[:, ei:ei + 1], ab[:, 0:TL - 1],
                                                             ALU.mult, ALU.add)),
             reads=("zz", "col_eu", "ab"), writes=("num",))
        p.op("DVE", (lambda e, ei=ei: e.scalar_tensor_tensor(num[:, 0:1], zz[:, 0:1],
                                                             eu[:, ei:ei + 1], term_in[:, 2 * ei:2 * ei + 1],
                                                             ALU.mult, ALU.add)),
             reads=("zz", "col_eu", "term_in", "num"), writes=("num",))
        p.op("DVE", (lambda e, ei=ei: e.scalar_tensor_tensor(ab[:, 1:TL], kxp[:, 1:TL],
                                                             eu[:, ei:ei + 1], bb[:, 0:TL - 1],
                                                             ALU.mult, ALU.add)),
             reads=("kxp", "col_eu", "bb", "num"), writes=("ab",))
        p.op("DVE", (lambda e, ei=ei: e.scalar_tensor_tensor(ab[:, 0:1], kxp[:, 0:1],
                                                             eu[:, ei:ei + 1], term_in[:, 2 * ei + 1:2 * ei + 2],
                                                             ALU.mult, ALU.add)),
             reads=("kxp", "col_eu", "term_in", "ab"), writes=("ab",))
        p.op("DVE", lambda e: e.reciprocal(ab[:], ab[:]), reads=("ab",), writes=("ab",))
        p.op("DVE", lambda e: e.tensor_mul(num[:], num[:], ab[:]), reads=("num", "ab"), writes=("num",))
        load_wcb(wrt, ei, 0, "wcb0")
        for tt in range(NTT):
            k = tt % 2
            tsl = slice(tt * TT, (tt + 1) * TT)
            for i in range(DC):
                p.op("PE", (lambda e, k=k, i=i, tsl=tsl: e.matmul(
                        ps_mm[k][:], wcb[0][:, i * 128:(i + 1) * 128],
                        xr[i][:, tsl], start=(i == 0), stop=(i == DC - 1))),
                     reads=("wcb0", f"xr{i}"), writes=(f"ps_mm{k}",))
            p.op("ACT", (lambda e, k=k, tsl=tsl: e.activation(sr[:, tsl], ps_mm[k][:], AF.Sigmoid)),
                 reads=(f"ps_mm{k}",), writes=("sr",))
        p.op("DVE", (lambda e, ei=ei: e.tensor_mul(rwkv[ei][:], num[:], sr[:])),
             reads=("num", "sr"), writes=(f"rwkv{ei}",))

    xre = [p.sbuf(f"xre{k}", [128, TT], F32) for k in range(2)]
    outs = []
    for oi in range(DC):
        load_wcb(wot, oi, oi % 2, f"wcb{oi % 2}")
        for tt in range(NTT):
            k = (oi * NTT + tt) % 2
            tsl = slice(tt * TT, (tt + 1) * TT)
            for i in range(DC):
                p.op("PE", (lambda e, k=k, i=i, oi=oi, tsl=tsl: e.matmul(
                        ps_mm[k][:], wcb[oi % 2][:, i * 128:(i + 1) * 128],
                        rwkv[i][:, tsl], start=(i == 0), stop=(i == DC - 1))),
                     reads=(f"wcb{oi % 2}", f"rwkv{i}"), writes=(f"ps_mm{k}",))
            p.dma("SP", xre[k][:], x_in[oi * 128:(oi + 1) * 128, tsl], writes=(f"xre{k}",))
            p.op("DVE", (lambda e, k=k: e.tensor_add(xre[k][:], xre[k][:], ps_mm[k][:])),
                 reads=(f"xre{k}", f"ps_mm{k}"), writes=(f"xre{k}",))
            p.dma("SP", x_out[oi * 128:(oi + 1) * 128, tsl], xre[k][:],
                  reads=(f"xre{k}",), writes=(f"xo_{oi}_{tt}",))
            outs.append(f"xo_{oi}_{tt}")
    p.final_wait(outs)
    return p


def build_l2ffn_inner(first, p=None):
    p = p or Prog("l2f" + ("a" if first else "b"))
    c = Consts(p)
    NHH = 8
    W_ = 512
    NW = TL // W_
    HID = 2730
    NFC = 22
    x_in = p.dram("x_in", [D, TL], F32, kind="ExternalInput").ap()
    if not first:
        y_in = p.dram("y_in", [D, TL], F32, kind="ExternalInput").ap()
    wqkv = p.dram("wqkv", [5, D, 512], F32, kind="ExternalInput").ap()
    tabs = p.dram("tabs", [4, 64, TL], F32, kind="ExternalInput").ap()
    qsc = p.dram("qsc", [64, 2], F32, kind="ExternalInput").ap()
    ksc = p.dram("ksc", [64, 2], F32, kind="ExternalInput").ap()
    wo = p.dram("wo", [512, D], F32, kind="ExternalInput").ap()
    lng = p.dram("lng", [128, DC], F32, kind="ExternalInput").ap()
    lnb = p.dram("lnb", [128, DC], F32, kind="ExternalInput").ap()
    w1 = p.dram("w1", [D, 5460], F32, kind="ExternalInput").ap()
    b1a = p.dram("b1a", [128, NFC], F32, kind="ExternalInput").ap()
    b1g = p.dram("b1g", [128, NFC], F32, kind="ExternalInput").ap()
    w2 = p.dram("w2", [HID, D], F32, kind="ExternalInput").ap()
    b2d = p.dram("b2d", [128, DC], F32, kind="ExternalInput").ap()
    moff = p.dram("moff", [128, 1], F32, kind="ExternalInput").ap()
    y_out = p.dram("y_out", [D, TL], F32, kind="ExternalOutput").ap()
    if not first:
        x_out = p.dram("x_out", [D, TL], F32, kind="ExternalOutput").ap()
    y_scr = p.dram("y_scr", [D, TL], F32)
    y1_scr = p.dram("y1_scr", [D, TL], F32)
    qkv_scr = p.dram("qkv_scr", [3, 512, TL], BF16)
    bnd1_o = p.dram("bnd1_o", [128, 4], F32)
    bnd1_g = p.dram("bnd1_g", [256, 4], F32)
    kvw_o = p.dram("kvw_o", [1024, W_], BF16)
    kvw_g = p.dram("kvw_g", [2048, W_], BF16)
    bnd2_o = p.dram("bnd2_o", [128, 4], F32)
    bnd2_g = p.dram("bnd2_g", [256, 4], F32)

    onesb = p.sbuf("onesb", [128, 1], BF16)
    p.op("DVE", lambda e: e.memset(onesb[:], 1.0), writes=("onesb",))
    onesrb = p.sbuf("onesrb", [1, 128], BF16)
    p.op("DVE", lambda e: e.memset(onesrb[:], 1.0), writes=("onesrb",))
    identb = p.sbuf("identb", [128, 128], BF16)
    p.op("POOL", lambda e: e.memset(identb[:], 1.0), writes=("identb",))
    p.op("POOL", lambda e: e.affine_select(out=identb[:], in_=identb[:], pattern=[[-1, 128]],
                                           compare_op=ALU.is_equal, fill=0.0, base=0,
                                           channel_multiplier=1),
         reads=("identb",), writes=("identb",))
    moffc = p.sbuf("moffc", [128, 1], F32)
    p.dma("SP", moffc[:], moff, writes=("moffc",))
    qs_sb = p.sbuf("qs_sb", [64, 2], F32)
    ks_sb = p.sbuf("ks_sb", [64, 2], F32)
    p.dma("SP", qs_sb[:], qsc, writes=("qs_sb",))
    p.dma("SP", ks_sb[:], ksc, writes=("ks_sb",))

    if first:
        yt = [p.sbuf(f"yt{i}", [128, TT], F32) for i in range(DC)]
        for tt in range(NTT):
            emit_ln_tile(p, c, ("hbm", x_in), [(yt[i], f"yt{i}") for i in range(DC)], tt, "ln",
                         out_off=0, n_tok_tile=TT)
            for i in range(DC):
                p.dma("SP", y_scr[i * 128:(i + 1) * 128, tt * TT:(tt + 1) * TT], yt[i][:],
                      reads=(f"yt{i}",), writes=("y_scr",))
        ysrc = y_scr
    else:
        ysrc = y_in

    bsc = p.sbuf("bsc", [128, 4], F32)
    for i in range(4):
        p.dma("SP", bsc[:, i:i + 1], ysrc[(4 + i) * 128:(5 + i) * 128, TL - 1:TL],
              reads=("y_scr",) if first else (), writes=("bsc",), allow_slow_non_contiguous=True)
    p.dma("SP", bnd1_o.ap(), bsc[:], reads=("bsc",), writes=("bnd1_o",))
    p.collective("AllGather", ALU.bypass, [[0, 1], [2, 3], [4, 5], [6, 7]],
                 [bnd1_o.ap()], [bnd1_g.ap()], reads=("bnd1_o",), writes=("bnd1_g",))
    bin1 = p.sbuf("bin1", [128, 4], F32)
    p.dma("SP", bin1[:], bnd1_g.ap()[0:128, :], reads=("bnd1_g",), writes=("bin1",))
    p.op("DVE", lambda e: e.tensor_scalar_mul(bin1[:], bin1[:], moffc[:]),
         reads=("bin1", "moffc"), writes=("bin1",))

    ysh = [p.sbuf(f"ysh{i}", [128, TL], BF16) for i in range(DC)]
    ytmp = p.sbuf("ytmp", [128, TT], F32)

    def build_shifted(src_dr, src_key, bcol, bkey, dst, dkey, i):
        sh = 0 if i < 4 else 1
        for tt in range(NTT):
            if sh == 0:
                p.dma("SP", ytmp[:], src_dr[i * 128:(i + 1) * 128, tt * TT:(tt + 1) * TT],
                      reads=src_key, writes=("ytmp",))
                p.op("DVE", (lambda e, dst=dst, tt=tt: e.tensor_copy(dst[:, tt * TT:(tt + 1) * TT], ytmp[:])),
                     reads=("ytmp",), writes=(dkey,))
            elif tt == 0:
                p.dma("SP", ytmp[:, 0:TT - 1], src_dr[i * 128:(i + 1) * 128, 0:TT - 1],
                      reads=src_key, writes=("ytmp",))
                p.op("DVE", (lambda e, dst=dst: e.tensor_copy(dst[:, 1:TT], ytmp[:, 0:TT - 1])),
                     reads=("ytmp",), writes=(dkey,))
                p.op("DVE", (lambda e, dst=dst, bcol=bcol, i=i: e.tensor_copy(dst[:, 0:1],
                                                                              bcol[:, i - 4:i - 3])),
                     reads=(bkey, dkey), writes=(dkey,))
            else:
                p.dma("SP", ytmp[:], src_dr[i * 128:(i + 1) * 128, tt * TT - 1:(tt + 1) * TT - 1],
                      reads=src_key, writes=("ytmp",))
                p.op("DVE", (lambda e, dst=dst, tt=tt: e.tensor_copy(dst[:, tt * TT:(tt + 1) * TT], ytmp[:])),
                     reads=("ytmp",), writes=(dkey,))

    for i in range(DC):
        build_shifted(ysrc, ("y_scr",) if first else (), bin1, "bin1", ysh[i], f"ysh{i}", i)

    ps_mm = [p.psum(f"ps_mm{k}", [128, TT], F32) for k in range(2)]
    psb_mm = [ps_mm[k].bitcast(BF16) for k in range(2)]
    yev = [p.sbuf(f"yev{k}", [128, TT], F32) for k in range(2)]
    p.push_scope()
    tab_sb = [p.sbuf(f"tab{j}", [64, TL], BF16) for j in range(4)]
    for j in range(4):
        p.dma("POOL", tab_sb[j][:], tabs[j], writes=(f"tab{j}",))
    wq_sb = [p.sbuf(f"wq_sb{k}", [128, DC * 64], BF16) for k in range(5)]
    prj = [p.sbuf(f"prj{j}", [64, TL], BF16) for j in range(3)]
    n2r = p.sbuf("n2r", [1, TL], F32)
    rn_row = p.sbuf("rn_row", [1, TL], F32)
    qh = p.sbuf("qh", [64, TL], BF16)
    qph = p.sbuf("qph", [64, TL], BF16)
    qrot = p.sbuf("qrot", [64, TL], BF16)
    krot = p.sbuf("krot", [64, TL], BF16)
    ts1 = p.sbuf("ts1", [64, TL], BF16)
    sqt = p.sbuf("sqt", [64, TL], BF16)
    kmax = p.sbuf("kmax", [1, 1], F32)
    ps_r = c.ps_b
    def project_dma(h, j):
        pr_ap = bass.AP(tensor=wqkv.tensor, offset=wqkv.offset + j * D * 512 + h * 64,
                        ap=[[512, 128], [128 * 512, DC], [1, 64]])
        p.dma("POOL", wq_sb[j][:], pr_ap, writes=(f"wq_sb{j}",))

    def project(h, j, dst, dkey):
        k = j
        kp = j % 2
        for tt in range(NTT):
            tsl = slice(tt * TT, (tt + 1) * TT)
            for i in range(DC):
                p.op("PE", (lambda e, k=k, kp=kp, i=i, tsl=tsl: e.matmul(
                        ps_mm[kp][0:64, :], wq_sb[k][:, i * 64:(i + 1) * 64], ysh[i][:, tsl],
                        start=(i == 0), stop=(i == DC - 1))),
                     reads=(f"wq_sb{k}", f"ysh{i}"), writes=(f"ps_mm{kp}",))
            p.op("DVE", (lambda e, dst=dst, kp=kp, tsl=tsl: e.tensor_copy(dst[:, tsl], ps_mm[kp][0:64, :])),
                 reads=(f"ps_mm{kp}",), writes=(dkey,))

    def norms_of(srck, dst_row):
        p.op("ACT", (lambda e: e.activation(sqt[:], prj[0][:] if srck == "prj0" else prj[1][:], AF.Square)),
             reads=(srck,), writes=("sqt",))
        for tt in range(NTT):
            p.op("PE", (lambda e, tt=tt: e.matmul(c.ps_s[:], onesb[0:64, :],
                                                  sqt[:, tt * TT:(tt + 1) * TT], start=True, stop=True)),
                 reads=("onesb", "sqt"), writes=("ps_s",))
            p.op("ACT", (lambda e, tt=tt, dst_row=dst_row: e.activation(
                    (rn_row if dst_row == "rn_row" else n2r)[0:1, tt * TT:(tt + 1) * TT],
                    c.ps_s[:], AF.Sqrt)),
                 reads=("ps_s",), writes=(dst_row,))
        p.op("DVE", (lambda e, dst_row=dst_row: e.tensor_scalar_max(
                (rn_row if dst_row == "rn_row" else n2r)[:],
                (rn_row if dst_row == "rn_row" else n2r)[:], 1e-12)),
             reads=(dst_row,), writes=(dst_row,))
        p.op("DVE", (lambda e, dst_row=dst_row: e.reciprocal(
                (rn_row if dst_row == "rn_row" else n2r)[:],
                (rn_row if dst_row == "rn_row" else n2r)[:])),
             reads=(dst_row,), writes=(dst_row,))

    def scaled_hat(srct, srck, row, rowkey, scol, sckey, dst, dkey):
        for tt in range(NTT):
            tsl = slice(tt * TT, (tt + 1) * TT)
            p.op("PE", (lambda e, row=row, tsl=tsl: e.matmul(ps_r[0:64, :], c.ones_row_f[0:1, 0:64],
                                                             row[0:1, tsl], start=True, stop=True)),
                 reads=("c_ones_row_f", rowkey), writes=("ps_b",))
            p.op("DVE", (lambda e, srct=srct, scol=scol, dst=dst, tsl=tsl: e.scalar_tensor_tensor(
                    dst[:, tsl], srct[:, tsl], scol, ps_r[0:64, :], ALU.mult, ALU.mult)),
                 reads=(srck, sckey, "ps_b"), writes=(dkey,))

    for h in range(NHH):
        for j in range(5):
            project_dma(h, j)
        project(h, 0, prj[0], "prj0")
        norms_of("prj0", "rn_row")
        project(h, 1, prj[1], "prj1")
        scaled_hat(prj[0], "prj0", rn_row, "rn_row", qs_sb[:, 0:1], "qs_sb", qh, "qh")
        scaled_hat(prj[1], "prj1", rn_row, "rn_row", qs_sb[:, 1:2], "qs_sb", qph, "qph")
        project(h, 2, prj[0], "prj0")
        norms_of("prj0", "n2r")
        project(h, 3, prj[1], "prj1")
        scaled_hat(prj[0], "prj0", n2r, "n2r", ks_sb[:, 0:1], "ks_sb", ts1, "ts1")
        scaled_hat(prj[1], "prj1", n2r, "n2r", ks_sb[:, 1:2], "ks_sb", sqt, "sqt")
        project(h, 4, prj[2], "prj2")
        p.op("DVE", lambda e: e.tensor_mul(qrot[:], qh[:], tab_sb[0][:]),
             reads=("qh", "tab0"), writes=("qrot",))
        p.op("DVE", lambda e: e.tensor_mul(qph[:], qph[:], tab_sb[1][:]),
             reads=("qph", "tab1"), writes=("qph",))
        p.op("DVE", lambda e: e.tensor_add(qrot[:], qrot[:], qph[:]),
             reads=("qrot", "qph"), writes=("qrot",))
        p.op("DVE", lambda e: e.tensor_mul(krot[:], ts1[:], tab_sb[2][:]),
             reads=("ts1", "tab2"), writes=("krot",))
        p.op("DVE", lambda e: e.tensor_mul(sqt[:], sqt[:], tab_sb[3][:]),
             reads=("sqt", "tab3"), writes=("sqt",))
        p.op("DVE", lambda e: e.tensor_add(krot[:], krot[:], sqt[:]),
             reads=("krot", "sqt"), writes=("krot",))
        p.op("ACT", lambda e: e.activation(sqt[:], qh[:], AF.Square), reads=("qh",), writes=("sqt",))
        for tt in range(NTT):
            p.op("PE", (lambda e, tt=tt: e.matmul(c.ps_s[:], onesb[0:64, :],
                                                  sqt[:, tt * TT:(tt + 1) * TT], start=True, stop=True)),
                 reads=("onesb", "sqt"), writes=("ps_s",))
            p.op("ACT", (lambda e, tt=tt: e.activation(rn_row[0:1, tt * TT:(tt + 1) * TT],
                                                       c.ps_s[:], AF.Sqrt)),
                 reads=("ps_s",), writes=("rn_row",))
        p.op("ACT", lambda e: e.activation(sqt[:], ts1[:], AF.Square), reads=("ts1",), writes=("sqt",))
        for tt in range(NTT):
            p.op("PE", (lambda e, tt=tt: e.matmul(c.ps_s[:], onesb[0:64, :],
                                                  sqt[:, tt * TT:(tt + 1) * TT], start=True, stop=True)),
                 reads=("onesb", "sqt"), writes=("ps_s",))
            p.op("DVE", (lambda e, tt=tt: e.tensor_copy(n2r[0:1, tt * TT:(tt + 1) * TT], c.ps_s[:])),
                 reads=("ps_s",), writes=("n2r",))
        p.op("DVE", lambda e: e.tensor_reduce(kmax[:], n2r[:], AX.X, ALU.max), reads=("n2r",), writes=("kmax",))
        p.op("ACT", lambda e: e.activation(kmax[:], kmax[:], AF.Sqrt), reads=("kmax",), writes=("kmax",))
        p.op("DVE", lambda e: e.tensor_scalar_mul(kmax[:], kmax[:], -8.0), reads=("kmax",), writes=("kmax",))
        p.op("DVE", lambda e: e.tensor_scalar_mul(rn_row[:], rn_row[:], kmax[:]),
             reads=("rn_row", "kmax"), writes=("rn_row",))
        p.dma("SP", qkv_scr[0, h * 64:(h + 1) * 64, :], qrot[:], reads=("qrot",), writes=("qkv_scr",))
        p.dma("SP", qkv_scr[1, h * 64:(h + 1) * 64, :], krot[:], reads=("krot",), writes=("qkv_scr",))
        p.dma("SP", qkv_scr[2, h * 64:(h + 1) * 64, :], prj[2][:], reads=("prj2",), writes=("qkv_scr",))
        if not hasattr(p, "_nb_scr"):
            p._nb_scr = p.dram("nb_scr", [8, TL], F32)
        p.dma("SP", p._nb_scr.ap()[h:h + 1, :], rn_row[:], reads=("rn_row",), writes=("nb_scr",))
        p.dma("POOL", kvw_o.ap()[h * 64:(h + 1) * 64, :], krot[:, TL - W_:TL],
              reads=("krot",), writes=("kvw_o",))
        p.dma("POOL", kvw_o.ap()[512 + h * 64:512 + (h + 1) * 64, :], prj[2][:, TL - W_:TL],
              reads=("prj2",), writes=("kvw_o",))
    p.collective("AllGather", ALU.bypass, [[0, 1], [2, 3], [4, 5], [6, 7]],
                 [kvw_o.ap()], [kvw_g.ap()], reads=("kvw_o",), writes=("kvw_g",))

    att_scr = p.dram("att_scr", [512, TL], BF16)
    atw = p.sbuf("atw", [64, W_], BF16)
    kin = p.sbuf("kin", [64, W_], BF16)
    vin = p.sbuf("vin", [64, W_], BF16)
    vt_sb = p.sbuf("vt_sb", [128, (TL // 128) * 64], BF16)
    vti = p.sbuf("vti", [128, (W_ // 128) * 64], BF16)
    nb_row = p.sbuf("nb_row", [1, TL], BF16)
    nbm_row = p.sbuf("nbm_row", [1, TL], BF16)
    pt_sb = [p.sbuf(f"pt_sb{k}", [128, W_], BF16) for k in range(2)]
    rs_row = p.sbuf("rs_row", [1, W_], F32)
    rb_sb = p.sbuf("rb_sb", [64, W_], F32)
    moff2 = p.sbuf("moff2", [1, 1], F32)
    p.op("DVE", lambda e: e.tensor_scalar(moff2[:], moffc[0:1, :], 1e30, -1e30,
                                          ALU.mult, ALU.add),
         reads=("moffc",), writes=("moff2",))
    ps_sum = c.ps_s
    ps_av = c.ps_b2
    for h in range(NHH):
        p.dma("SP", qrot[:], qkv_scr[0, h * 64:(h + 1) * 64, :], reads=("qkv_scr",), writes=("qrot",))
        p.dma("SP", krot[:], qkv_scr[1, h * 64:(h + 1) * 64, :], reads=("qkv_scr",), writes=("krot",))
        p.dma("SP", ts1[:], qkv_scr[2, h * 64:(h + 1) * 64, :], reads=("qkv_scr",), writes=("ts1",))
        p.dma("SP", kin[:], kvw_g.ap()[h * 64:(h + 1) * 64, :], reads=("kvw_g",), writes=("kin",))
        p.dma("SP", vin[:], kvw_g.ap()[512 + h * 64:512 + (h + 1) * 64, :], reads=("kvw_g",), writes=("vin",))
        p.dma("POOL", nb_row[:], p._nb_scr.ap()[h:h + 1, :], reads=("nb_scr",), writes=("nb_row",))
        p.op("DVE", (lambda e: e.tensor_scalar_add(nbm_row[:], nb_row[:], moff2[:])),
             reads=("nb_row", "moff2"), writes=("nbm_row",))
        for ts in range(TL // 128):
            k = ts % 2
            p.op("PE", (lambda e, ts=ts, k=k: e.transpose(psb_mm[k][0:128, 0:64],
                                                          ts1[:, ts * 128:(ts + 1) * 128], identb[0:64, 0:64])),
                 reads=("ts1", "identb"), writes=(f"ps_mm{k}",))
            p.op("DVE", (lambda e, ts=ts, k=k: e.tensor_copy(vt_sb[:, ts * 64:(ts + 1) * 64],
                                                             psb_mm[k][0:128, 0:64])),
                 reads=(f"ps_mm{k}",), writes=("vt_sb",))
        for ts in range(W_ // 128):
            k = ts % 2
            p.op("PE", (lambda e, ts=ts, k=k: e.transpose(psb_mm[k][0:128, 0:64],
                                                          vin[:, ts * 128:(ts + 1) * 128], identb[0:64, 0:64])),
                 reads=("vin", "identb"), writes=(f"ps_mm{k}",))
            p.op("DVE", (lambda e, ts=ts, k=k: e.tensor_copy(vti[:, ts * 64:(ts + 1) * 64],
                                                             psb_mm[k][0:128, 0:64])),
                 reads=(f"ps_mm{k}",), writes=("vti",))
        for w in range(NW):
            tql = slice(w * W_, (w + 1) * W_)
            blocks = []
            for tk in range(4):
                blocks.append(("gin", tk) if w == 0 else ("prev", (w - 1) * 4 + tk))
            for tk in range(4):
                blocks.append(("own", w * 4 + tk))
            first_b = True
            for bi, (src, tk) in enumerate(blocks):
                kk = bi % 2
                if src == "gin":
                    kap, kkey = kin[:, tk * 128:(tk + 1) * 128], "kin"
                    vap, vkey = vti[:, tk * 64:(tk + 1) * 64], "vti"
                    nrow, nkey = nbm_row, "nbm_row"
                else:
                    kap, kkey = krot[:, tk * 128:(tk + 1) * 128], "krot"
                    vap, vkey = vt_sb[:, tk * 64:(tk + 1) * 64], "vt_sb"
                    nrow, nkey = nb_row, "nb_row"
                p.op("PE", (lambda e, kk=kk, kap=kap, tql=tql: e.matmul(
                        ps_mm[kk][:], kap, qrot[:, tql], start=True, stop=False)),
                     reads=(kkey, "qrot"), writes=(f"ps_mm{kk}",))
                p.op("PE", (lambda e, kk=kk, nrow=nrow, tql=tql: e.matmul(
                        ps_mm[kk][:], onesrb[0:1, :], nrow[0:1, tql], start=False, stop=True)),
                     reads=("onesrb", nkey), writes=(f"ps_mm{kk}",))
                p.op("ACT", (lambda e, kk=kk: e.activation(pt_sb[kk][:], ps_mm[kk][:], AF.Exp)),
                     reads=(f"ps_mm{kk}",), writes=(f"pt_sb{kk}",))
                if src == "gin":
                    p.op("POOL", (lambda e, kk=kk, tk=tk: e.affine_select(
                            out=pt_sb[kk][:], in_=pt_sb[kk][:], pattern=[[-1, W_]],
                            compare_op=ALU.is_ge, fill=0.0, base=tk * 128,
                            channel_multiplier=1)),
                         reads=(f"pt_sb{kk}",), writes=(f"pt_sb{kk}",))
                elif src == "own":
                    p.op("POOL", (lambda e, kk=kk, tk=tk: e.affine_select(
                            out=pt_sb[kk][:], in_=pt_sb[kk][:], pattern=[[1, W_]],
                            compare_op=ALU.is_ge, fill=0.0, base=-(tk % 4) * 128,
                            channel_multiplier=-1)),
                         reads=(f"pt_sb{kk}",), writes=(f"pt_sb{kk}",))
                else:
                    p.op("POOL", (lambda e, kk=kk, tk=tk: e.affine_select(
                            out=pt_sb[kk][:], in_=pt_sb[kk][:], pattern=[[-1, W_]],
                            compare_op=ALU.is_ge, fill=0.0, base=(tk % 4) * 128,
                            channel_multiplier=1)),
                         reads=(f"pt_sb{kk}",), writes=(f"pt_sb{kk}",))
                p.op("PE", (lambda e, kk=kk, first_b=first_b, last=(bi == len(blocks) - 1):
                            e.matmul(ps_sum[:], onesb[:], pt_sb[kk][:], start=first_b, stop=last)),
                     reads=("onesb", f"pt_sb{kk}"), writes=("ps_s",))
                p.op("PE", (lambda e, kk=kk, vap=vap, first_b=first_b, last=(bi == len(blocks) - 1):
                            e.matmul(ps_av[0:64, 0:W_], vap, pt_sb[kk][:], start=first_b, stop=last)),
                     reads=(vkey, f"pt_sb{kk}"), writes=("ps_b2",))
                first_b = False
            p.op("DVE", lambda e: e.reciprocal(rs_row[:], ps_sum[0:1, 0:W_]), reads=("ps_s",), writes=("rs_row",))
            p.op("PE", (lambda e: e.matmul(ps_mm[0][0:64, :], c.ones_row_f[0:1, 0:64], rs_row[:],
                                           start=True, stop=True)),
                 reads=("c_ones_row_f", "rs_row"), writes=("ps_mm0",))
            p.op("DVE", lambda e: e.tensor_copy(rb_sb[:], ps_mm[0][0:64, 0:W_]),
                 reads=("ps_mm0",), writes=("rb_sb",))
            p.op("DVE", (lambda e, tql=tql: e.tensor_mul(atw[:], ps_av[0:64, 0:W_], rb_sb[:])),
                 reads=("ps_b2", "rb_sb"), writes=("atw",))
            p.dma("SP", att_scr.ap()[h * 64:(h + 1) * 64, tql], atw[:],
                  reads=("atw",), writes=("att_scr",))

    att = [p.sbuf(f"att{i}", [128, TT], BF16) for i in range(4)]
    wo_sb = p.sbuf("wo_sb", [128, 4 * 128 * DC], BF16)
    for oi in range(DC):
        load_wchunks(p, "POOL", wo_sb[:, oi * 512:(oi + 1) * 512], "wo_sb", wo, D, 0, 4, oi * 128, 128)
    for tt in range(NTT):
        tsl = slice(tt * TT, (tt + 1) * TT)
        for i in range(4):
            p.dma("SP", att[i][:], att_scr.ap()[i * 128:(i + 1) * 128, tsl],
                  reads=("att_scr",), writes=(f"att{i}",))
        for oi in range(DC):
            k = oi % 2
            for i in range(4):
                p.op("PE", (lambda e, k=k, i=i, oi=oi: e.matmul(
                        ps_mm[k][:], wo_sb[:, oi * 512 + i * 128:oi * 512 + (i + 1) * 128], att[i][:],
                        start=(i == 0), stop=(i == 3))),
                     reads=("wo_sb", f"att{i}"), writes=(f"ps_mm{k}",))
            p.dma("SP", yev[k][:], ysrc[oi * 128:(oi + 1) * 128, tsl],
                  reads=("y_scr",) if first else (), writes=(f"yev{k}",))
            p.op("DVE", (lambda e, k=k: e.tensor_add(yev[k][:], yev[k][:], ps_mm[k][:])),
                 reads=(f"yev{k}", f"ps_mm{k}"), writes=(f"yev{k}",))
            p.dma("SP", y1_scr[oi * 128:(oi + 1) * 128, tsl], yev[k][:],
                  reads=(f"yev{k}",), writes=("y1_scr",))

    p.section_barrier()
    p.pop_scope()
    p.push_scope()

    for i in range(4):
        p.dma("SP", bsc[:, i:i + 1], y1_scr[(4 + i) * 128:(5 + i) * 128, TL - 1:TL],
              reads=("y1_scr",), writes=("bsc",), allow_slow_non_contiguous=True)
    p.dma("SP", bnd2_o.ap(), bsc[:], reads=("bsc",), writes=("bnd2_o",))
    p.collective("AllGather", ALU.bypass, [[0, 1], [2, 3], [4, 5], [6, 7]],
                 [bnd2_o.ap()], [bnd2_g.ap()], reads=("bnd2_o",), writes=("bnd2_g",))
    bin2 = p.sbuf("bin2", [128, 4], F32)
    p.dma("SP", bin2[:], bnd2_g.ap()[0:128, :], reads=("bnd2_g",), writes=("bin2",))
    p.op("DVE", lambda e: e.tensor_scalar_mul(bin2[:], bin2[:], moffc[:]),
         reads=("bin2", "moffc"), writes=("bin2",))
    ysh2t = p.sbuf("ysh2t", [128, TT], F32)
    for i in range(DC):
        for tt in range(NTT):
            if i < 4:
                p.dma("SP", ytmp[:], y1_scr[i * 128:(i + 1) * 128, tt * TT:(tt + 1) * TT],
                      reads=("y1_scr",), writes=("ytmp",))
                p.op("DVE", lambda e: e.tensor_copy(ysh2t[:], ytmp[:]), reads=("ytmp",), writes=("ysh2t",))
            elif tt == 0:
                p.dma("SP", ytmp[:, 0:TT - 1], y1_scr[i * 128:(i + 1) * 128, 0:TT - 1],
                      reads=("y1_scr",), writes=("ytmp",))
                p.op("DVE", lambda e: e.tensor_copy(ysh2t[:, 1:TT], ytmp[:, 0:TT - 1]),
                     reads=("ytmp",), writes=("ysh2t",))
                p.op("DVE", (lambda e, i=i: e.tensor_copy(ysh2t[:, 0:1], bin2[:, i - 4:i - 3])),
                     reads=("bin2", "ysh2t"), writes=("ysh2t",))
            else:
                p.dma("SP", ytmp[:], y1_scr[i * 128:(i + 1) * 128, tt * TT - 1:(tt + 1) * TT - 1],
                      reads=("y1_scr",), writes=("ytmp",))
                p.op("DVE", lambda e: e.tensor_copy(ysh2t[:], ytmp[:]), reads=("ytmp",), writes=("ysh2t",))
            p.dma("SP", y_scr[i * 128:(i + 1) * 128, tt * TT:(tt + 1) * TT], ysh2t[:],
                  reads=("ysh2t",), writes=("y_scr2",))
    lng_sb = p.sbuf("lng_sb", [128, DC], F32)
    lnb_sb = p.sbuf("lnb_sb", [128, DC], F32)
    p.dma("SP", lng_sb[:], lng, writes=("lng_sb",))
    p.dma("SP", lnb_sb[:], lnb, writes=("lnb_sb",))
    mkeys = [f"m{i}" for i in range(DC)]
    p.barrier([f"ysh{i}" for i in range(DC)] + mkeys + ["y_scr2", "lnxb0", "lnxb1"])
    m = ysh
    for tt in range(NTT):
        emit_ln_tile(p, c, ("hbm", y_scr), [(m[i], f"m{i}") for i in range(DC)], tt, "lnm",
                     gain=(lng_sb, "lng_sb", 0), beta=(lnb_sb, "lnb_sb", 0), out_off=tt * TT)

    b1a_sb = p.sbuf("b1a_sb", [128, NFC], F32)
    b1g_sb = p.sbuf("b1g_sb", [128, NFC], F32)
    b2_sb = p.sbuf("b2_sb", [128, DC], F32)
    p.dma("SP", b1a_sb[:], b1a, writes=("b1a_sb",))
    p.dma("SP", b1g_sb[:], b1g, writes=("b1g_sb",))
    p.dma("SP", b2_sb[:], b2d, writes=("b2_sb",))
    w1_sb = [p.sbuf(f"w1_sb{k}", [128, DC * 128], BF16) for k in range(4)]
    w2_sb = [p.sbuf(f"w2_sb{k}", [128, NFC * 128], BF16) for k in range(2)]
    gl = p.sbuf("gl", [128, TT], F32)
    gl2 = p.sbuf("gl2", [128, TT], F32)
    hbf = [p.sbuf(f"hbf{j}", [128, 4 * TT], BF16) for j in range(6)]
    xre = None if first else [p.sbuf(f"xre{k}", [128, TT], F32) for k in range(2)]
    outs = []
    for tt in range(NTT):
        tsl = slice(tt * TT, (tt + 1) * TT)
        for fi in range(NFC):
            k = fi % 2
            fw = 128 if fi < 21 else 42
            pa, pak = (ps_mm[0], "ps_mm0") if k == 0 else (c.ps_b, "ps_b")
            pg, pgk = (ps_mm[1], "ps_mm1") if k == 0 else (c.ps_b2, "ps_b2")
            glb, glk = (gl, "gl") if k == 0 else (gl2, "gl2")
            for half, base in ((0, 0), (1, 2730)):
                kw = k if half == 0 else k + 2
                w1dst = bass.AP(tensor=w1_sb[kw].ap().tensor, offset=0,
                                ap=[[w1_sb[kw].ap().ap[0][0], 128], [128, DC], [1, fw]])
                load_wchunks(p, "POOL", w1dst, f"w1_sb{kw}", w1, 5460, 0, DC, base + fi * 128, fw)
                pp, ppk = (pa, pak) if half == 0 else (pg, pgk)
                for i in range(DC):
                    p.op("PE", (lambda e, kw=kw, i=i, tsl=tsl, fw=fw, pp=pp: e.matmul(
                            pp[0:fw, 0:TT],
                            w1_sb[kw][:, i * 128:i * 128 + fw], m[i][:, tsl],
                            start=(i == 0), stop=(i == DC - 1))),
                         reads=(f"w1_sb{kw}", f"m{i}"), writes=(ppk,))
            p.op("ACT", (lambda e, fi=fi, fw=fw, pg=pg, glb=glb: e.activation(
                    glb[0:fw, :], pg[0:fw, 0:TT], AF.Gelu, bias=b1g_sb[0:fw, fi:fi + 1])),
                 reads=(pgk, "b1g_sb"), writes=(glk,))
            p.op("DVE", (lambda e, fi=fi, fw=fw, pa=pa: e.tensor_scalar(
                    pa[0:fw, 0:TT], pa[0:fw, 0:TT], b1a_sb[0:fw, fi:fi + 1], None,
                    ALU.add)),
                 reads=(pak, "b1a_sb"), writes=(pak,))
            p.op("DVE", (lambda e, fi=fi, fw=fw, pa=pa, glb=glb: e.tensor_mul(
                    hbf[fi % 6][0:fw, (fi // 6) * TT:(fi // 6 + 1) * TT],
                    pa[0:fw, 0:TT], glb[0:fw, :])),
                 reads=(pak, glk), writes=(f"hbf{fi}",))
        for oi in range(DC):
            k = oi % 2
            kw = oi % 2
            load_wchunks(p, "POOL", w2_sb[kw][:, 0:21 * 128], f"w2_sb{kw}", w2, D, 0, 21, oi * 128, 128)
            p.dma("POOL", w2_sb[kw][0:42, 21 * 128:22 * 128],
                  w2[21 * 128:21 * 128 + 42, oi * 128:(oi + 1) * 128], writes=(f"w2_sb{kw}",))
            for fi in range(NFC):
                fw = 128 if fi < 21 else 42
                p.op("PE", (lambda e, k=k, kw=kw, fi=fi, fw=fw: e.matmul(
                        ps_mm[k][:], w2_sb[kw][0:fw, fi * 128:fi * 128 + fw] if False else
                        w2_sb[kw][0:fw, fi * 128:fi * 128 + 128][0:fw, 0:128],
                        hbf[fi % 6][0:fw, (fi // 6) * TT:(fi // 6 + 1) * TT],
                        start=(fi == 0), stop=(fi == NFC - 1))),
                     reads=(f"w2_sb{kw}", f"hbf{fi}"), writes=(f"ps_mm{k}",))
            p.dma("SP", yev[k][:], y1_scr[oi * 128:(oi + 1) * 128, tsl], reads=("y1_scr",), writes=(f"yev{k}",))
            p.op("DVE", (lambda e, k=k, oi=oi: e.tensor_scalar_add(yev[k][:], yev[k][:],
                                                                   b2_sb[:, oi:oi + 1])),
                 reads=(f"yev{k}", "b2_sb"), writes=(f"yev{k}",))
            p.op("DVE", (lambda e, k=k: e.tensor_add(yev[k][:], yev[k][:], ps_mm[k][:])),
                 reads=(f"yev{k}", f"ps_mm{k}"), writes=(f"yev{k}",))
            p.dma("SP", y_out[oi * 128:(oi + 1) * 128, tsl], yev[k][:],
                  reads=(f"yev{k}",), writes=(f"yo_{oi}_{tt}",))
            outs.append(f"yo_{oi}_{tt}")
            if not first:
                p.dma("SP", xre[k][:], x_in[oi * 128:(oi + 1) * 128, tsl], writes=(f"xre{k}",))
                p.op("DVE", (lambda e, k=k: e.scalar_tensor_tensor(xre[k][:], yev[k][:], 0.5,
                                                                   xre[k][:], ALU.mult, ALU.add)),
                     reads=(f"yev{k}", f"xre{k}"), writes=(f"xre{k}",))
                p.dma("SP", x_out[oi * 128:(oi + 1) * 128, tsl], xre[k][:],
                      reads=(f"xre{k}",), writes=(f"xf_{oi}_{tt}",))
                outs.append(f"xf_{oi}_{tt}")
    p.pop_scope()
    p.final_wait(outs)
    return p



_PERM = np.concatenate([np.arange(32, 64), np.arange(0, 32)])


def _cshape(a):
    return np.ascontiguousarray(np.asarray(a, np.float32).reshape(-1, 128).T)


def _l2ffn_static(il, hh, inputs):
    wqkv = inputs["L2_ffn_wqkv"][il]
    q_scale = inputs["L2_ffn_q_scale"][il]
    k_scale = inputs["L2_ffn_k_scale"][il]
    Wq, Wk, Wv = wqkv[:, :512], wqkv[:, 512:1024], wqkv[:, 1024:]
    permute = lambda A: np.ascontiguousarray(
        A.reshape(1024, 8, 64)[:, :, _PERM].reshape(1024, 512))
    wqkv5 = np.ascontiguousarray(
        np.stack([Wq, permute(Wq), Wk, permute(Wk), Wv]).astype(np.float32))
    tg = (np.arange(2048) + hh * 2048).astype(np.float64)
    hi = np.arange(0, 64, 2, dtype=np.float64)
    invf = 1.0 / 10000.0 ** (hi / 64)
    fr = tg[None, :] * np.concatenate([invf, invf])[:, None]
    base = (hi + 0.4 * 64) / (1.4 * 64)
    power = (tg - 2048.0) / 256.0
    sc = np.concatenate([base, base])[:, None] ** power[None, :]
    sig = np.concatenate([-np.ones(32), np.ones(32)])[:, None]
    tabs = np.ascontiguousarray(np.stack(
        [8.0 * np.cos(fr) * sc, 8.0 * sig * np.sin(fr) * sc,
         np.cos(fr) / sc, sig * np.sin(fr) / sc]).astype(np.float32))
    pad = lambda v, n: np.concatenate([v, np.zeros(n - len(v), v.dtype)])
    b1 = inputs["L2_ffn_b1"][il]
    return {
        "wqkv": wqkv5, "tabs": tabs,
        "qsc": np.ascontiguousarray(np.stack([q_scale, q_scale[_PERM]], 1).astype(np.float32)),
        "ksc": np.ascontiguousarray(np.stack([k_scale, k_scale[_PERM]], 1).astype(np.float32)),
        "wo": np.ascontiguousarray(inputs["L2_ffn_wo"][il]),
        "lng": _cshape(inputs["L2_ffn_ln_g"][il]), "lnb": _cshape(inputs["L2_ffn_ln_b"][il]),
        "w1": np.ascontiguousarray(inputs["L2_ffn_w1"][il]),
        "b1a": np.ascontiguousarray(pad(b1[:2730], 22 * 128).reshape(22, 128).T),
        "b1g": np.ascontiguousarray(pad(b1[2730:], 22 * 128).reshape(22, 128).T),
        "w2": np.ascontiguousarray(inputs["L2_ffn_w2"][il]),
        "b2d": _cshape(inputs["L2_ffn_b2"][il]),
        "moff": np.full((128, 1), 1.0 if hh == 1 else 0.0, np.float32),
    }


def _section_inputs(inputs):
    sel8 = np.zeros((8, 1024), np.float32)
    for e in range(8):
        sel8[e, e * 128:(e + 1) * 128] = 1.0
    l0f = {"wr": inputs["L0_ffn_Wr"], "wi": inputs["L0_ffn_Wi"], "wo": inputs["L0_ffn_Wo"],
           "sel8": sel8}
    l0a = {"wqkv": inputs["L0_attn_qkv_w"], "bqkv": _cshape(inputs["L0_attn_qkv_b"]),
           "wout": inputs["L0_attn_out_w"], "boutd": _cshape(inputs["L0_attn_out_b"]),
           "n1w": _cshape(inputs["L0_attn_norm1_w"]), "n2w": _cshape(inputs["L0_attn_norm2_w"]),
           "w1": inputs["L0_attn_mlp_w1"], "b1d": _cshape(inputs["L0_attn_mlp_b1"]),
           "w2": inputs["L0_attn_mlp_w2"], "b2d": _cshape(inputs["L0_attn_mlp_b2"])}
    l1f = {"w1t": np.ascontiguousarray(inputs["L1_ffn_W1"].T),
           "w2t": np.ascontiguousarray(inputs["L1_ffn_W2"].T),
           "b1d": _cshape(inputs["L1_ffn_b1"]), "b2d": _cshape(inputs["L1_ffn_b2"])}
    l1a = {"wkt": np.ascontiguousarray(inputs["L1_attn_Wk"].T),
           "wvt": np.ascontiguousarray(inputs["L1_attn_Wv"].T),
           "wrt": np.ascontiguousarray(inputs["L1_attn_Wr"].T),
           "wot": np.ascontiguousarray(inputs["L1_attn_Wo"].T),
           "tdc": _cshape(inputs["L1_attn_time_decay"]), "tfc": _cshape(inputs["L1_attn_time_first"]),
           "mkc": _cshape(inputs["L1_attn_time_mix_k"]), "mvc": _cshape(inputs["L1_attn_time_mix_v"]),
           "mrc": _cshape(inputs["L1_attn_time_mix_r"])}
    l2a = {"wqk": np.ascontiguousarray(np.stack(
               [inputs["L2_attn_qkv_kernel"][:, j * 192:j * 192 + 128] for j in range(16)])),
           "wv": np.ascontiguousarray(np.stack(
               [inputs["L2_attn_qkv_kernel"][:, j * 192 + 128:j * 192 + 192] for j in range(16)])),
           "bqk": np.ascontiguousarray(np.stack(
               [inputs["L2_attn_qkv_bias"][0, j * 192:j * 192 + 128] for j in range(16)], 1)),
           "bv": np.ascontiguousarray(np.stack(
               [inputs["L2_attn_qkv_bias"][0, j * 192 + 128:j * 192 + 192] for j in range(16)], 1))}
    l3a = {"wqk": np.ascontiguousarray(inputs["L3_attn_Wqkv"][:, :, 0:128]),
           "wv": np.ascontiguousarray(
               inputs["L3_attn_Wqkv"][:, :, 128:192].transpose(1, 0, 2).reshape(1024, 1024)),
           "bqk": np.ascontiguousarray(inputs["L3_attn_bqkv"][:, 0:128].T),
           "bv": np.ascontiguousarray(inputs["L3_attn_bqkv"][:, 128:192].reshape(1, 1024)),
           "wout": np.ascontiguousarray(inputs["L3_attn_Wout"]),
           "boutd": _cshape(inputs["L3_attn_bout"])}
    l3f = {"wgd": inputs["L3_ffn_Wg"], "bgd": inputs["L3_ffn_bg"].reshape(1, 8),
           "wet": np.ascontiguousarray(np.transpose(inputs["L3_ffn_We"], (0, 2, 1))),
           "bed": inputs["L3_ffn_be"], "sel8": sel8}

    def pc_moff128(cidx):
        return {"moff": np.full((128, 1), 1.0 if (cidx % 2) == 1 else 0.0, np.float32)}

    def pc_l0a(cidx):
        hh = cidx % 2
        t_loc = (np.arange(2048) + hh * 2048) * (np.pi / 2 / 4096)
        return {"poscs": np.stack([np.cos(t_loc), np.sin(t_loc)]).astype(np.float32),
                "moff": np.full((64, 1), 1.0 if hh == 1 else 0.0, np.float32)}

    def pc_l3a(cidx):
        return {"moff": np.array([[0.0 if (cidx % 2) == 1 else -30000.0]], np.float32)}

    l2A = [[_l2ffn_static(0, cidx % 2, inputs) for cidx in range(NCORES)],
           [_l2ffn_static(1, cidx % 2, inputs) for cidx in range(NCORES)]]
    secs = [
        ("s00_", l0f, None), ("s01_", l0a, pc_l0a), ("s02_", l0f, None),
        ("s03_", l1f, None), ("s04_", l1a, pc_moff128), ("s05_", l1f, None),
        ("s06_", l2A[0], None), ("s07_", l2A[1], None),
        ("s08_", l2a, None),
        ("s09_", l2A[0], None), ("s10_", l2A[1], None),
        ("s11_", l3f, None), ("s12_", l3a, pc_l3a), ("s13_", l3f, None),
    ]
    return secs


_SEC_BUILDERS = None


def _sec_builders():
    global _SEC_BUILDERS
    if _SEC_BUILDERS is None:
        _SEC_BUILDERS = [
            build_l0ffn_launch, build_l0attn_launch, build_l0ffn_launch,
            build_l1ffn_launch, build_l1attn_launch, build_l1ffn_launch,
            lambda p: build_l2ffn_inner(True, p), lambda p: build_l2ffn_inner(False, p),
            build_l2attn_launch,
            lambda p: build_l2ffn_inner(True, p), lambda p: build_l2ffn_inner(False, p),
            build_l3ffn_launch, build_l3attn_launch, build_l3ffn_launch,
        ]
    return _SEC_BUILDERS


_MEGA = None


def build_megakernel():
    global _MEGA
    if _MEGA is not None:
        return _MEGA
    p = Prog("mega")
    nc = p.nc
    x0 = nc.dram_tensor("x0", [D, TL], F32, kind="ExternalInput")
    xcur = nc.dram_tensor("xcur", [D, TL], F32)
    xout = nc.dram_tensor("xout", [D, TL], F32, kind="ExternalOutput")
    yps = [nc.dram_tensor(f"yps{j}", [D, TL], F32) for j in range(2)]
    ydump = [nc.dram_tensor(f"ydump{j}", [D, TL], F32) for j in range(2)]
    p.dma("SP", xcur.ap(), x0.ap(), writes=("xcur0",))
    builders = _sec_builders()
    for i, b in enumerate(builders):
        p.begin_section(f"s{i:02d}_")
        dm = {"x_in": xcur, "x_out": xcur}
        if i == 13:
            dm["x_out"] = xout
        if i in (6, 9):
            dm["y_out"] = yps[0 if i == 6 else 1]
        if i in (7, 10):
            dm["y_in"] = yps[0 if i == 7 else 1]
            dm["y_out"] = ydump[0 if i == 7 else 1]
        p.dram_map = dm
        b(p)
        p.end_section()
    p.finalize()
    _MEGA = p
    return p


def kernel(**inputs):
    inputs = {k: np.asarray(v) for k, v in inputs.items()}
    x = inputs["L0_attn_x"]
    secs = _section_inputs(inputs)
    in_maps = []
    for cidx in range(NCORES):
        b_, hh = cidx // 2, cidx % 2
        m = {"x0": np.ascontiguousarray(x[b_, hh * 2048:(hh + 1) * 2048, :].T)}
        for pref, static, pcfn in secs:
            st = static[cidx] if isinstance(static, list) else static
            for kk, vv in st.items():
                m[pref + kk] = vv
            if pcfn is not None:
                for kk, vv in pcfn(cidx).items():
                    m[pref + kk] = vv
        in_maps.append(m)
    p = build_megakernel()
    res = run_bass_kernel_spmd(p.nc, in_maps, core_ids=list(range(NCORES)))
    out = np.empty((4, 4096, 1024), np.float32)
    for cidx in range(NCORES):
        b_, hh = cidx // 2, cidx % 2
        out[b_, hh * 2048:(hh + 1) * 2048, :] = np.asarray(res.results[cidx]["xout"]).T
    return out
